# version 08
# baseline (speedup 1.0000x reference)
import numpy as np
import concourse.bass as bass
import concourse.mybir as mybir
from concourse.bass_utils import run_bass_kernel_spmd
from contextlib import ExitStack

F32 = mybir.dt.float32
BF16 = mybir.dt.bfloat16
I32 = mybir.dt.int32
AF = mybir.ActivationFunctionType
ALU = mybir.AluOpType
AX = mybir.AxisListType

NCORE = 8
T = 2048
D = 1024
C = 128
NCH = 16
ND = 8
FF = 3072
NOT = 24
QMAX = 32767.0

DBG = True
STAGE = 9


class Ctr:
    def __init__(self, sem):
        self.sem = sem
        self.n = 0

    def tick(self, inst, k=1):
        inst.then_inc(self.sem, k)
        self.n += k
        return self.n


def build_nc():
    nc = bass.Bass("TRN2", target_bir_lowering=False, debug=False)
    di = {}

    def inp(name, shape, dt=F32):
        di[name] = nc.dram_tensor(name, list(shape), dt, kind="ExternalInput").ap()
        return di[name]

    xT = inp("xT", [D, T])
    wqT = inp("wqT", [D, C]); wkT = inp("wkT", [D, C])
    wvT = inp("wvT", [D, C]); wgT = inp("wgT", [D, C])
    wabT = inp("wabT", [D, 2])
    woT = inp("woT", [C, D])
    bqc = inp("bqc", [C, 1]); bkc = inp("bkc", [C, 1])
    bvc = inp("bvc", [C, 1]); bgc = inp("bgc", [C, 1])
    babb = inp("babb", [C, 2])
    boc = inp("boc", [128, 8])
    up_o = inp("up_o", [384, D]); up_t = inp("up_t", [D, 384])
    gt_o = inp("gt_o", [384, D]); gt_t = inp("gt_t", [D, 384])
    dn_o = inp("dn_o", [C, FF]); dn_t = inp("dn_t", [FF, C])
    IDENT = inp("IDENT", [128, 128])
    CUMT = inp("CUMT", [128, 128])
    MASKT = inp("MASKT", [128, 128])
    ONES1 = inp("ONES1", [128, 128])
    IDX32 = inp("IDX32", [128, 1], I32)
    IDXDQ = inp("IDXDQ", [128, 64], I32)
    PDIDX = inp("PDIDX", [128, 6], I32)
    MSEL = inp("MSEL", [128, 4])

    o_out = nc.dram_tensor("o_out", [D, T], F32, kind="ExternalOutput").ap()
    dbg = {}
    if DBG:
        for nm, sh, dt in [("dq", [C, T], BF16), ("dkh", [C, T], BF16),
                           ("dv", [C, T], BF16), ("dg", [C, T], F32),
                           ("dla", [128, NCH], F32), ("dbc", [128, NCH], F32),
                           ("dO", [C, T], BF16), ("dxffn", [D, T], F32),
                           ("dwup", [D, 384], BF16), ("duvw", [128, 256], BF16),
                           ("dpct", [128, T], BF16), ("ddcf", [128, T], BF16),
                           ("duv2", [128, T], BF16), ("dwfu", [128, T], BF16),
                           ("dkrf", [128, T], BF16), ("dntl", [128, T], BF16),
                           ("dxtl", [128, T], BF16),
                           ("dmst", [128, 128], BF16)]:
            dbg[nm] = nc.dram_tensor(nm, sh, dt, kind="ExternalOutput").ap()

    b_su = nc.dram_tensor("b_su", [64, 384], F32)
    b_sg = nc.dram_tensor("b_sg", [64, 384], F32)
    b_sd = nc.dram_tensor("b_sd", [192, C], F32)
    b_dqu = nc.dram_tensor("b_dqu", [D, 384], BF16)
    b_dqg = nc.dram_tensor("b_dqg", [D, 384], BF16)
    b_dqd = nc.dram_tensor("b_dqd", [FF, C], BF16)
    b_wu = nc.dram_tensor("b_wu", [NCORE * D, 384], BF16)
    b_wg = nc.dram_tensor("b_wg", [NCORE * D, 384], BF16)
    b_wd = nc.dram_tensor("b_wd", [NCORE * FF, C], BF16)
    b_pd_in = nc.dram_tensor("b_pd_in", [256, 128], BF16)
    b_pd_out = nc.dram_tensor("b_pd_out", [NCORE * 256, 128], BF16)
    b_mx_in = nc.dram_tensor("b_mx_in", [1, 2], F32)
    b_mx_out = nc.dram_tensor("b_mx_out", [1, 2], F32)
    b_up = nc.dram_tensor("b_up", [FF, T], BF16)
    b_gate = nc.dram_tensor("b_gate", [FF, T], BF16)
    b_g2 = nc.dram_tensor("b_g2", [FF, T], BF16)
    b_opre = nc.dram_tensor("b_opre", [FF, T], BF16)

    with ExitStack() as stk:
        e = stk.enter_context
        GEN = e(nc.sbuf_tensor("GEN", [128, 40960], BF16))
        GENF = e(nc.sbuf_tensor("GENF", [128, 6144], F32))
        XFFN = e(nc.sbuf_tensor("XFFN", [128, 8 * T], F32))
        ROWS = e(nc.sbuf_tensor("ROWS", [128, 2048], F32))
        COLS = e(nc.sbuf_tensor("COLS", [128, 256], F32))
        CONST = e(nc.sbuf_tensor("CONST", [128, 512], F32))
        CONB = e(nc.sbuf_tensor("CONB", [128, 260], BF16))
        SM = e(nc.sbuf_tensor("SM", [128, 64], F32))
        SMI = e(nc.sbuf_tensor("SMI", [128, 8], I32))
        SMIQ = e(nc.sbuf_tensor("SMIQ", [128, 64], I32))
        STF = e(nc.sbuf_tensor("STF", [128, 2592], F32))
        STB = e(nc.sbuf_tensor("STB", [128, 4096], BF16))
        MTF = e(nc.sbuf_tensor("MTF", [128, 128], F32))
        MTB = e(nc.sbuf_tensor("MTB", [128, 1664], BF16))
        AWB = e(nc.sbuf_tensor("AWB", [128, 1040], BF16))
        PB = [e(nc.psum_tensor("pb%d" % k, [128, 512], F32)) for k in range(8)]

        STB1 = STB[:, 0:2048]
        STB2 = STB[:, 2048:4096]

        def gb(o, n):
            return GEN[:, o:o + n]
        XBF = [gb(2048 * k, 2048) for k in range(8)]
        QB = gb(16384, 2048); QA = gb(18432, 2048)
        KH = gb(20480, 2048); K2 = gb(22528, 2048)
        VB = gb(24576, 2048)
        UVB = gb(26624, 2048)
        WTN = gb(28672, 2048)
        KTT = gb(30720, 2048)
        OBF = gb(0, 2048)
        SCR = [STB[:, 1024 + 256 * k:1280 + 256 * k] for k in range(10)]
        OJ = [gb(512 * j, 512) for j in range(NOT)]

        GF = GENF[:, 0:2048]
        EIF = GENF[:, 2048:4096]
        FSC = GENF[:, 4096:6144]
        SIGF = FSC

        INT1 = GEN[:, 20480:24576].bitcast(I32)
        INT2 = GEN[:, 16384:20480].bitcast(I32)
        INT1o = GEN[:, 24576:28672].bitcast(I32)
        INT2o = GEN[:, 28672:32768].bitcast(I32)
        SIGFo = GENF[:, 0:2048]

        LA_ROW = ROWS[0:1, :]
        RTMP = GENF[0:1, 2048:4096]
        RTMP2 = ROWS[64:65, :]
        MROW = ROWS[0:1, :]
        SQR = GENF[0:1, 0:2048]
        VARR = GENF[0:1, 2048:4096]
        RSTD = GENF[0:1, 0:2048]
        MRSTD = GENF[0:1, 2048:4096]

        LAC = COLS[:, 0:16]; BCC = COLS[:, 16:32]; ACOL = COLS[:, 32:48]
        BAC = COLS[:, 48:64]; RC_ = COLS[:, 64:80]; ALC = COLS[:, 80:96]
        ZAC = COLS[:, 96:112]; ZBC = COLS[:, 112:128]
        CT1 = COLS[:, 128:144]; CT2 = COLS[:, 144:160]
        BIAS4 = COLS[:, 160:164]
        BAB = COLS[:, 164:166]
        BOC = COLS[:, 166:174]
        SCL = COLS[:, 174:206]
        NEGC = COLS[:, 206:207]
        MSL = COLS[:, 208:212]

        IDE = CONST[:, 0:128]; CUM = CONST[:, 128:256]
        MSK = CONST[:, 256:384]
        ONE = CONST[0:1, 384:512]
        ONE32 = CONST[32:33, 384:512]
        ONE64 = CONST[64:65, 384:512]
        ONE96 = CONST[96:97, 384:512]
        IDEB = CONB[:, 0:128]; ONEB = CONB[0:1, 128:256]
        ONECB = CONB[:, 256:257]

        def wq_t(d): return GEN[:, 26624 + d * 128:26624 + (d + 1) * 128]
        def wk_t(d): return GEN[:, 27648 + d * 128:27648 + (d + 1) * 128]
        def wv_t(d): return GEN[:, 28672 + d * 128:28672 + (d + 1) * 128]
        def wg_t(d): return GEN[:, 29696 + d * 128:29696 + (d + 1) * 128]
        def wo_t(h): return AWB[:, h * 128:(h + 1) * 128]
        def wab_t(d): return AWB[:, 1024 + 2 * d:1026 + 2 * d]

        MT_F = MTF[:, 0:128]
        MT_B = MTB[:, 0:128]
        M_B = MTB[:, 128:256]
        PCM = MTB[:, 256:384]
        PCMT = MTB[:, 384:512]
        DCM = MTB[:, 512:640]
        DCB = MTB[:, 640:768]
        GPD = [MTB[:, 768 + k * 128:896 + k * 128] for k in range(6)]
        MACC = MTB[:, 1536:1664]

        def pbb(k, n=128, off=0):
            return PB[k].ap().bitcast(BF16)[:, off:off + n]

        ld = Ctr(e(nc.semaphore("ld")))
        sw = Ctr(e(nc.semaphore("sw")))
        gq = Ctr(e(nc.semaphore("gq")))
        pe = Ctr(e(nc.semaphore("pe")))
        ve = Ctr(e(nc.semaphore("ve")))
        se = Ctr(e(nc.semaphore("se")))
        po = Ctr(e(nc.semaphore("po")))
        st = Ctr(e(nc.semaphore("st")))
        ldb = Ctr(e(nc.semaphore("ldb")))
        gqb = Ctr(e(nc.semaphore("gqb")))
        swb = Ctr(e(nc.semaphore("swb")))
        lp5 = Ctr(e(nc.semaphore("lp5")))
        lp5b = Ctr(e(nc.semaphore("lp5b")))
        ldw2 = Ctr(e(nc.semaphore("ldw2")))
        ldw2b = Ctr(e(nc.semaphore("ldw2b")))
        sp3 = Ctr(e(nc.semaphore("sp3")))
        sp3b = Ctr(e(nc.semaphore("sp3b")))
        cc = [Ctr(e(nc.semaphore("cc%d" % k))) for k in range(7)]
        blk = e(nc.Block())

        @blk.sync
        def _(sy):
            ld.tick(sy.dma_start(out=IDE, in_=IDENT[:, :]), 16)
            ld.tick(sy.dma_start(out=CUM, in_=CUMT[:, :]), 16)
            ld.tick(sy.dma_start(out=MSK, in_=MASKT[:, :]), 16)
            ld.tick(sy.dma_start(out=CONST[:, 384:512], in_=ONES1[:, :]), 16)
            ld.tick(sy.dma_start(out=SMI[:, 0:1], in_=IDX32[:, :]), 16)
            ld.tick(sy.dma_start(out=SMI[:, 1:7], in_=PDIDX[:, :]), 16)
            ld.tick(sy.dma_start(out=SMIQ[:, :], in_=IDXDQ[:, :]), 16)
            ld.tick(sy.dma_start(out=MSL, in_=MSEL[:, :]), 16)
            ld.tick(sy.dma_start(out=BIAS4[:, 0:1], in_=bqc[:, :]), 16)
            ld.tick(sy.dma_start(out=BIAS4[:, 1:2], in_=bkc[:, :]), 16)
            ld.tick(sy.dma_start(out=BIAS4[:, 2:3], in_=bvc[:, :]), 16)
            ld.tick(sy.dma_start(out=BIAS4[:, 3:4], in_=bgc[:, :]), 16)
            ld.tick(sy.dma_start(out=BAB, in_=babb[:, :]), 16)
            ld.tick(sy.dma_start(out=BOC, in_=boc[:, :]), 16)
            for d in range(8):
                ld.tick(sy.dma_start(out=XFFN[:, d * T:(d + 1) * T],
                                     in_=xT[d * 128:(d + 1) * 128, :]), 16)
            for d in range(8):
                ld.tick(sy.dma_start(out=GENF[:, 512 * d:512 * d + 128],
                                     in_=wqT[d * 128:(d + 1) * 128, :]), 16)
                ld.tick(sy.dma_start(out=GENF[:, 512 * d + 128:512 * d + 256],
                                     in_=wkT[d * 128:(d + 1) * 128, :]), 16)
                ld.tick(sy.dma_start(out=GENF[:, 512 * d + 256:512 * d + 384],
                                     in_=wvT[d * 128:(d + 1) * 128, :]), 16)
                ld.tick(sy.dma_start(out=GENF[:, 512 * d + 384:512 * d + 512],
                                     in_=wgT[d * 128:(d + 1) * 128, :]), 16)
            for d in range(8):
                ld.tick(sy.dma_start(out=FSC[:, 2 * d:2 * d + 2],
                                     in_=wabT[d * 128:(d + 1) * 128, :]), 16)
            for h in range(8):
                ld.tick(sy.dma_start(out=FSC[:, 16 + 128 * h:16 + 128 * (h + 1)],
                                     in_=woT[:, h * 128:(h + 1) * 128]), 16)

        ph0_ld = ld.n

        @blk.vector
        def _(v):
            v.wait_ge(ld.sem, ph0_ld)
            for d in range(8):
                v.tensor_copy(out=XBF[d], in_=XFFN[:, d * T:(d + 1) * T])
                v.tensor_copy(out=wq_t(d), in_=GENF[:, 512 * d:512 * d + 128])
                v.tensor_copy(out=wk_t(d), in_=GENF[:, 512 * d + 128:512 * d + 256])
                v.tensor_copy(out=wv_t(d), in_=GENF[:, 512 * d + 256:512 * d + 384])
                v.tensor_copy(out=wg_t(d), in_=GENF[:, 512 * d + 384:512 * d + 512])
                v.tensor_copy(out=wab_t(d), in_=FSC[:, 2 * d:2 * d + 2])
            for h in range(8):
                v.tensor_copy(out=wo_t(h), in_=FSC[:, 16 + 128 * h:16 + 128 * (h + 1)])
            v.tensor_copy(out=IDEB, in_=IDE)
            v.tensor_copy(out=ONEB, in_=ONE)
            v.memset(ONECB, 1.0)
            v.memset(NEGC, -1.0)
            last = v.memset(SM[:, 0:2], 0.0)
            ve.tick(last, 1)

        ph0_ve = ve.n

        dq_cfg = [(up_o, up_t, b_su, b_dqu, 3, 8, 384, 1024, 32, 0, 8),
                  (gt_o, gt_t, b_sg, b_dqg, 3, 8, 384, 1024, 32, 0, 8),
                  (dn_o, dn_t, b_sd, b_dqd, 1, 24, 128, 3072, 96, 16, 40)]

        for (sh_o, sh_t, bscl, bdq, n_ot, n_it, NO, NI, NB, ixs, ixi) in dq_cfg:
            segw = NI if NI <= 1536 else 1536
            nseg = NI // segw
            nbs = NB // nseg
            for ot3 in range(n_ot):
                for seg in range(nseg):
                    @blk.sync
                    def _(sy, sh_o=sh_o, ot3=ot3, seg=seg, segw=segw):
                        sy.wait_ge(ve.sem, ve.n)
                        ld.tick(sy.dma_start(
                            out=STF[:, 0:segw],
                            in_=sh_o[ot3 * 128:(ot3 + 1) * 128,
                                     seg * segw:(seg + 1) * segw]), 16)
                    need_ld = ld.n

                    @blk.vector
                    def _(v, seg=seg, segw=segw, nbs=nbs, need_ld=need_ld):
                        v.wait_ge(ld.sem, need_ld)
                        v.wait_ge(se.sem, se.n)
                        ve.tick(v.tensor_reduce(
                            out=STF[:, 1600 + seg * nbs:1600 + (seg + 1) * nbs]
                                .rearrange("p (n w) -> p n w", w=1),
                            in_=STF[:, 0:segw].rearrange("p (n w) -> p n w", w=32),
                            axis=AX.X, op=ALU.max, apply_absolute_value=True), 1)

                @blk.vector
                def _(v, NB=NB):
                    v.wait_ge(ve.sem, ve.n)
                    v.tensor_scalar(out=STF[:, 1696:1696 + NB], in0=STF[:, 1600:1600 + NB],
                                    scalar1=1.0 / 7.0, scalar2=1e-12,
                                    op0=ALU.mult, op1=ALU.max)
                    ve.tick(v.engine_nop(), 1)
                    v.wait_ge(ve.sem, ve.n)
                    ve.tick(v.reciprocal(out=STF[:, 1792:1792 + NB],
                                         in_=STF[:, 1696:1696 + NB]), 1)
                need_ve = ve.n

                @blk.tensor
                def _(t, NB=NB, need_ve=need_ve):
                    t.wait_ge(ve.sem, need_ve)
                    t.wait_ge(se.sem, se.n)
                    t.transpose(PB[0][0:NB, 0:128], STF[:, 1696:1696 + NB], IDE)
                    pe.tick(t.transpose(PB[1][0:NB, 0:128], STF[:, 1792:1792 + NB], IDE), 1)
                need_pe = pe.n

                @blk.scalar
                def _(s, NB=NB, need_pe=need_pe):
                    s.wait_ge(pe.sem, need_pe)
                    s.wait_ge(sw.sem, sw.n)
                    s.copy(out=STF[0:NB, 1888:2016], in_=PB[0][0:NB, 0:128])
                    se.tick(s.copy(out=STF[0:NB, 2016:2144], in_=PB[1][0:NB, 0:128]), 1)
                need_se = se.n

                @blk.scalar
                def _(s, bscl=bscl, ot3=ot3, NB=NB, need_se=need_se):
                    s.wait_ge(se.sem, need_se)
                    sw.tick(s.dma_start(out=bscl[0:NB, ot3 * 128:(ot3 + 1) * 128],
                                        in_=STF[0:NB, 1888:2016]), 16)
                    sw.tick(s.dma_start(out=bscl[NB:2 * NB, ot3 * 128:(ot3 + 1) * 128],
                                        in_=STF[0:NB, 2016:2144]), 16)

            dqv = {}
            dqw = {}
            base_sw = sw.n
            for it in range(n_it):
                par = it % 2
                sl0, sl1, sl2 = (0, 512, 1024) if par == 0 else (1408, 1792, 2176)
                stb_off = 0 if par == 0 else 512
                ints = INT1 if par == 0 else INT2
                LDP = ld if par == 0 else ldb
                GQP = gq if par == 0 else gqb
                SWP = sw if par == 0 else swb

                @blk.sync
                def _(sy, sh_t=sh_t, it=it, NO=NO, sl0=sl0, LDP=LDP):
                    if it < 2:
                        sy.wait_ge(ve.sem, ve.n)
                        sy.wait_ge(sw.sem, sw.n)
                        sy.wait_ge(swb.sem, swb.n)
                    else:
                        sy.wait_ge(ve.sem, dqv[it - 2])
                    LDP.tick(sy.dma_start(out=STF[:, sl0:sl0 + NO],
                                          in_=sh_t[it * 128:(it + 1) * 128, :]), 16)
                need_ld = LDP.n

                @blk.gpsimd
                def _(g, bscl=bscl, it=it, NO=NO, NB=NB, ixs=ixs, ixi=ixi,
                      sl1=sl1, sl2=sl2, base_sw=base_sw, GQP=GQP):
                    if it < 2:
                        g.wait_ge(sw.sem, base_sw)
                    else:
                        g.wait_ge(ve.sem, dqv[it - 2])
                    GQP.tick(g.indirect_dma_start(
                        out=STF[:, sl1:sl1 + NO], out_offset=None,
                        in_=bscl.ap(),
                        in_offset=bass.IndirectOffsetOnAxis(
                            ap=SMIQ[:, ixs + it:ixs + it + 1], axis=0)), 16)
                    GQP.tick(g.indirect_dma_start(
                        out=STF[:, sl2:sl2 + NO], out_offset=None,
                        in_=bscl.ap(),
                        in_offset=bass.IndirectOffsetOnAxis(
                            ap=SMIQ[:, ixi + it:ixi + it + 1], axis=0)), 16)
                need_gq = GQP.n

                @blk.vector
                def _(v, it=it, NO=NO, sl0=sl0, sl1=sl1, sl2=sl2, stb_off=stb_off,
                      ints=ints, need_ld=need_ld, need_gq=need_gq, LDP=LDP, GQP=GQP,
                      SWP2=(sw if it % 2 == 0 else swb)):
                    v.wait_ge(LDP.sem, need_ld)
                    v.wait_ge(GQP.sem, need_gq)
                    if it >= 2:
                        v.wait_ge(SWP2.sem, dqw[it - 2])
                    v.tensor_tensor(out=ints[:, 0:NO], in0=STF[:, sl0:sl0 + NO],
                                    in1=STF[:, sl2:sl2 + NO], op=ALU.mult)
                    ve.tick(v.engine_nop(), 1)
                    v.wait_ge(ve.sem, ve.n)
                    ve.tick(v.tensor_tensor(out=STB1[:, stb_off:stb_off + NO],
                                            in0=ints[:, 0:NO],
                                            in1=STF[:, sl1:sl1 + NO],
                                            op=ALU.mult), 1)
                dqv[it] = ve.n
                need_ve = ve.n

                @blk.scalar
                def _(s, bdq=bdq, it=it, NO=NO, stb_off=stb_off, need_ve=need_ve,
                      SWP=SWP):
                    s.wait_ge(ve.sem, need_ve)
                    SWP.tick(s.dma_start(out=bdq[it * 128:(it + 1) * 128, :],
                                         in_=STB1[:, stb_off:stb_off + NO]), 16)
                dqw[it] = SWP.n

        @blk.gpsimd
        def _(g):
            g.wait_ge(sw.sem, sw.n)
            g.wait_ge(swb.sem, swb.n)
            cc[0].tick(g.collective_compute(
                "AllGather", ALU.bypass, replica_groups=[list(range(NCORE))],
                ins=[b_dqu.ap()], outs=[b_wu.ap()]), 1)
            cc[0].tick(g.collective_compute(
                "AllGather", ALU.bypass, replica_groups=[list(range(NCORE))],
                ins=[b_dqg.ap()], outs=[b_wg.ap()]), 1)
            cc[0].tick(g.collective_compute(
                "AllGather", ALU.bypass, replica_groups=[list(range(NCORE))],
                ins=[b_dqd.ap()], outs=[b_wd.ap()]), 1)

        if DBG:
            @blk.sync
            def _(sy):
                sy.wait_ge(sw.sem, sw.n)
                sy.wait_ge(swb.sem, swb.n)
                st.tick(sy.dma_start(out=dbg["dwup"][:, :], in_=b_dqu.ap()[:, :]), 16)

        if STAGE >= 1:
            @blk.tensor
            def _(t):
                t.wait_ge(ve.sem, ph0_ve)
                t.wait_ge(se.sem, se.n)
                last = None
                for c_ in range(16):
                    for d in range(8):
                        last = t.matmul(PB[4][:, 2 * c_:2 * c_ + 2],
                                        XBF[d][:, c_ * 128:(c_ + 1) * 128], wab_t(d),
                                        start=(d == 0), stop=(d == 7))
                pe.tick(last, 1)
            pe_zab = pe.n

            @blk.vector
            def _(v):
                v.wait_ge(pe.sem, pe_zab)
                v.tensor_copy(out=ZAC,
                              in_=PB[4].ap()[:, 0:32].rearrange("p (c t) -> p t c", t=2)[:, 0, :])
                ve.tick(v.tensor_copy(
                    out=ZBC,
                    in_=PB[4].ap()[:, 0:32].rearrange("p (c t) -> p t c", t=2)[:, 1, :]), 1)
            ve_zab = ve.n

            for tc in range(4):
                @blk.tensor
                def _(t, tc=tc):
                    t.wait_ge(ve.sem, ph0_ve)
                    t.wait_ge(se.sem, se.n)
                    last = None
                    for d in range(8):
                        last = t.matmul(PB[5][0:1, 0:512], wab_t(d)[:, 1:2],
                                        XBF[d][:, tc * 512:(tc + 1) * 512],
                                        start=(d == 0), stop=(d == 7))
                    pe.tick(last, 1)
                need_pe = pe.n

                @blk.scalar
                def _(s, tc=tc, need_pe=need_pe):
                    s.wait_ge(pe.sem, need_pe)
                    se.tick(s.activation(out=RTMP[:, tc * 512:(tc + 1) * 512],
                                         in_=PB[5][0:1, 0:512], func=AF.Sigmoid,
                                         bias=BAB[0:1, 1:2]), 1)

            @blk.scalar
            def _(s):
                s.wait_ge(ve.sem, ve_zab)
                se.tick(s.activation(out=CT1, in_=ZAC, func=AF.Abs, bias=BAB[:, 0:1]), 1)
                s.wait_ge(se.sem, se.n)
                se.tick(s.activation(out=CT2, in_=CT1, func=AF.Exp, scale=NEGC), 1)
            se_c1 = se.n

            @blk.vector
            def _(v):
                v.wait_ge(se.sem, se_c1)
                ve.tick(v.tensor_scalar_add(out=CT2, in0=CT2, scalar1=1.0), 1)
            ve_c1 = ve.n

            @blk.scalar
            def _(s):
                s.wait_ge(ve.sem, ve_c1)
                se.tick(s.activation(out=CT2, in_=CT2, func=AF.Ln), 1)
            se_c2 = se.n

            @blk.vector
            def _(v):
                v.wait_ge(se.sem, se_c2)
                v.tensor_scalar(out=CT1, in0=ZAC, scalar1=BAB[:, 0:1], scalar2=0.0,
                                op0=ALU.add, op1=ALU.min)
                ve.tick(v.engine_nop(), 1)
                v.wait_ge(ve.sem, ve.n)
                ve.tick(v.tensor_sub(out=CT1, in0=CT1, in1=CT2), 1)
            ve_la = ve.n

            @blk.tensor
            def _(t):
                t.wait_ge(ve.sem, ve_la)
                pe.tick(t.matmul(PB[6][:, 0:16], CUM, CT1, start=True, stop=True), 1)
            pe_cum = pe.n

            @blk.vector
            def _(v):
                v.wait_ge(pe.sem, pe_cum)
                ve.tick(v.tensor_copy(out=LAC, in_=PB[6][:, 0:16]), 1)
            ve_lac = ve.n

            @blk.scalar
            def _(s):
                s.wait_ge(ve.sem, ve_zab)
                se.tick(s.activation(out=BCC, in_=ZBC, func=AF.Sigmoid, bias=BAB[:, 1:2]), 1)
                s.wait_ge(ve.sem, ve_lac)
                se.tick(s.activation(out=ACOL, in_=LAC, func=AF.Exp), 1)
            se_bc = se.n

            @blk.vector
            def _(v):
                v.wait_ge(se.sem, se_bc)
                ve.tick(v.tensor_mul(out=BAC, in0=BCC, in1=ACOL), 1)

            @blk.gpsimd
            def _(g):
                g.wait_ge(ve.sem, ve_lac)
                gq.tick(g.dma_start(out=SM[0:1, 16:32], in_=LAC[127:128, :]), 16)
            gq_lal = gq.n

            @blk.tensor
            def _(t):
                t.wait_ge(gq.sem, gq_lal)
                t.wait_ge(ve.sem, ve.n)
                pe.tick(t.matmul(PB[6][:, 16:32], ONE, SM[0:1, 16:32],
                                 start=True, stop=True), 1)
            pe_lal = pe.n

            @blk.vector
            def _(v):
                v.wait_ge(pe.sem, pe_lal)
                ve.tick(v.tensor_sub(out=CT1, in0=PB[6][:, 16:32], in1=LAC), 1)
            ve_rc = ve.n

            @blk.scalar
            def _(s):
                s.wait_ge(ve.sem, ve_rc)
                se.tick(s.activation(out=RC_, in_=CT1, func=AF.Exp), 1)
                se.tick(s.activation(out=ALC, in_=PB[6][:, 16:32], func=AF.Exp), 1)
            se_rc = se.n

            for c_ in range(16):
                @blk.tensor
                def _(t, c_=c_):
                    t.wait_ge(ve.sem, ve.n)
                    if c_ == 0:
                        t.wait_ge(ve.sem, ve_lac)
                    pe.tick(t.matmul(PB[7][0:1, 0:128], LAC[:, c_:c_ + 1], IDE,
                                     start=True, stop=True), 1)
                need_pe = pe.n

                @blk.vector
                def _(v, c_=c_, need_pe=need_pe):
                    v.wait_ge(pe.sem, need_pe)
                    ve.tick(v.tensor_copy(out=LA_ROW[:, c_ * 128:(c_ + 1) * 128],
                                          in_=PB[7][0:1, 0:128]), 1)
            ve_larow = ve.n

            for tc in range(4):
                sl = slice(tc * 512, (tc + 1) * 512)

                @blk.tensor
                def _(t, tc=tc, sl=sl):
                    t.wait_ge(ve.sem, ve.n)
                    t.wait_ge(se.sem, se.n)
                    last = None
                    for d in range(8):
                        st_ = (d == 0); sp = (d == 7)
                        t.matmul(PB[0][:, 0:512], wq_t(d), XBF[d][:, sl], start=st_, stop=sp)
                        t.matmul(PB[1][:, 0:512], wk_t(d), XBF[d][:, sl], start=st_, stop=sp)
                        t.matmul(PB[2][:, 0:512], wv_t(d), XBF[d][:, sl], start=st_, stop=sp)
                        last = t.matmul(PB[3][:, 0:512], wg_t(d), XBF[d][:, sl],
                                        start=st_, stop=sp)
                    pe.tick(last, 1)
                pe_proj = pe.n

                @blk.vector
                def _(v, sl=sl, pe_proj=pe_proj):
                    v.wait_ge(pe.sem, pe_proj)
                    v.tensor_scalar(out=QB[:, sl], in0=PB[0][:, 0:512],
                                    scalar1=BIAS4[:, 0:1], scalar2=None, op0=ALU.add)
                    v.tensor_scalar(out=VB[:, sl], in0=PB[2][:, 0:512],
                                    scalar1=BIAS4[:, 2:3], scalar2=None, op0=ALU.add)
                    ve.tick(v.tensor_scalar(out=STF[:, 0:512], in0=PB[1][:, 0:512],
                                            scalar1=BIAS4[:, 1:2], scalar2=None,
                                            op0=ALU.add), 1)
                ve_kf = ve.n

                @blk.scalar
                def _(s, sl=sl, pe_proj=pe_proj, ve_kf=ve_kf):
                    s.wait_ge(pe.sem, pe_proj)
                    se.tick(s.activation(out=GF[:, sl], in_=PB[3][:, 0:512],
                                         func=AF.Sigmoid, bias=BIAS4[:, 3:4]), 1)
                    s.wait_ge(ve.sem, ve_kf)
                    se.tick(s.activation(out=STB1[:, 0:512], in_=STF[:, 0:512],
                                         func=AF.Square), 1)
                se_sq = se.n

                @blk.tensor
                def _(t, se_sq=se_sq):
                    t.wait_ge(se.sem, se_sq)
                    pe.tick(t.matmul(PB[5][0:1, 0:512], ONECB, STB1[:, 0:512],
                                     start=True, stop=True), 1)
                pe_nrm = pe.n

                @blk.scalar
                def _(s, sl=sl, pe_nrm=pe_nrm):
                    s.wait_ge(pe.sem, pe_nrm)
                    se.tick(s.activation(out=RTMP2[:, sl], in_=PB[5][0:1, 0:512],
                                         func=AF.Sqrt), 1)
                se_nrm = se.n

                @blk.vector
                def _(v, sl=sl, se_nrm=se_nrm):
                    v.wait_ge(se.sem, se_nrm)
                    ve.tick(v.tensor_scalar_max(out=RTMP2[:, sl], in0=RTMP2[:, sl],
                                                scalar1=1e-12), 1)
                    v.wait_ge(ve.sem, ve.n)
                    ve.tick(v.reciprocal(out=RTMP2[:, sl], in_=RTMP2[:, sl]), 1)
                ve_rn = ve.n

                @blk.tensor
                def _(t, sl=sl, ve_rn=ve_rn):
                    t.wait_ge(ve.sem, ve_rn)
                    t.matmul(PB[6][:, 0:512], ONE64, RTMP2[:, sl], start=True, stop=True)
                    pe.tick(t.matmul(PB[7][:, 0:512], ONE, RTMP[:, sl],
                                     start=True, stop=True), 1)
                pe_bc = pe.n

                @blk.vector
                def _(v, sl=sl, pe_bc=pe_bc):
                    v.wait_ge(pe.sem, pe_bc)
                    ve.tick(v.tensor_mul(out=KH[:, sl], in0=STF[:, 0:512],
                                         in1=PB[6][:, 0:512]), 1)
                    v.wait_ge(ve.sem, ve.n)
                    ve.tick(v.tensor_mul(out=K2[:, sl], in0=KH[:, sl],
                                         in1=PB[7][:, 0:512]), 1)

            @blk.scalar
            def _(s):
                s.wait_ge(ve.sem, ve.n)
                se.tick(s.activation(out=RTMP, in_=LA_ROW, func=AF.Exp), 1)
            se_ar = se.n

            for tc in range(4):
                sl = slice(tc * 512, (tc + 1) * 512)

                @blk.tensor
                def _(t, sl=sl, se_ar=se_ar):
                    t.wait_ge(se.sem, se_ar)
                    t.wait_ge(ve.sem, ve.n)
                    pe.tick(t.matmul(PB[6][:, 0:512], ONE, RTMP[:, sl],
                                     start=True, stop=True), 1)
                pe_ar = pe.n

                @blk.vector
                def _(v, sl=sl, pe_ar=pe_ar):
                    v.wait_ge(pe.sem, pe_ar)
                    ve.tick(v.tensor_mul(out=QA[:, sl], in0=QB[:, sl],
                                         in1=PB[6][:, 0:512]), 1)

            if DBG:
                @blk.sync
                def _(sy):
                    sy.wait_ge(ve.sem, ve.n)
                    sy.wait_ge(se.sem, se.n)
                    st.tick(sy.dma_start(out=dbg["dq"][:, :], in_=QB), 16)
                    st.tick(sy.dma_start(out=dbg["dkh"][:, :], in_=KH), 16)
                    st.tick(sy.dma_start(out=dbg["dv"][:, :], in_=VB), 16)
                    st.tick(sy.dma_start(out=dbg["dg"][:, :], in_=GF), 16)
                    st.tick(sy.dma_start(out=dbg["dla"][:, :], in_=LAC), 16)
                    st.tick(sy.dma_start(out=dbg["dbc"][:, :], in_=BCC), 16)

        NTIL = GEN[:, 2048:4096]
        XTIL = GEN[:, 4096:6144]
        PPA = GEN[:, 6144:8192]
        TTIL = GEN[:, 8192:10240]
        PPB = GEN[:, 10240:12288]
        RHSF = GEN[:, 12288:16384]
        KRF = STB[:, 1024:3072]
        WFU = GEN[:, 0:2048]
        PCTF = GEN[:, 2048:4096]
        DCF = GEN[:, 4096:6144]
        MBC = GEN[:, 6144:8192]
        MTC = GEN[:, 8192:10240]
        UFU = GEN[:, 10240:12288]
        URF = GEN[:, 12288:14336]
        UTF = GEN[:, 14336:16384]
        PTF = GEN[:, 2048:4096]

        def c128(buf, c_):
            return buf[:, c_ * 128:(c_ + 1) * 128]

        if STAGE >= 2:
            @blk.tensor
            def _(t):
                t.wait_ge(ve.sem, ve.n)
                t.wait_ge(se.sem, se.n)
                last = None
                for c_ in range(NCH):
                    bk, off = c_ % 4, (c_ // 4) * 128
                    t.matmul(PB[bk][:, off:off + 128], KH[:, c_ * 128:(c_ + 1) * 128],
                             K2[:, c_ * 128:(c_ + 1) * 128], start=True, stop=True)
                    last = t.matmul(PB[4 + bk][:, off:off + 128], ONE,
                                    LA_ROW[:, c_ * 128:(c_ + 1) * 128],
                                    start=True, stop=True)
                pe.tick(last, 1)
            p2a = pe.n

            @blk.vector
            def _(v):
                v.wait_ge(pe.sem, p2a)
                last = None
                for c_ in range(NCH):
                    bk, off = c_ % 4, (c_ // 4) * 128
                    last = v.scalar_tensor_tensor(out=c128(FSC, c_),
                                                  in0=PB[4 + bk][:, off:off + 128],
                                                  scalar=LAC[:, c_:c_ + 1], in1=MSK,
                                                  op0=ALU.subtract, op1=ALU.add)
                ve.tick(last, 1)
            v2a = ve.n

            @blk.scalar
            def _(s):
                s.wait_ge(ve.sem, v2a)
                se.tick(s.activation(out=FSC, in_=FSC, func=AF.Exp), 1)
            s2a = se.n

            @blk.vector
            def _(v):
                v.wait_ge(se.sem, s2a)
                last = None
                for c_ in range(NCH):
                    last = v.tensor_add(out=c128(EIF, c_), in0=c128(FSC, c_), in1=IDE)
                for c_ in range(NCH):
                    bk, off = c_ % 4, (c_ // 4) * 128
                    last = v.tensor_mul(out=c128(NTIL, c_), in0=c128(FSC, c_),
                                        in1=PB[bk][:, off:off + 128])
                ve.tick(last, 1)
                v.wait_ge(ve.sem, ve.n)
                last = None
                for c_ in range(NCH):
                    last = v.tensor_sub(out=c128(XTIL, c_), in0=IDEB, in1=c128(NTIL, c_))
                ve.tick(last, 1)
            v2b = ve.n

            @blk.tensor
            def _(t):
                t.wait_ge(ve.sem, v2b)
                last = None
                for c_ in range(NCH):
                    bk, off = c_ % 2, (c_ // 2) * 128
                    last = t.transpose(pbb(bk, 128, off), c128(NTIL, c_), IDEB)
                pe.tick(last, 1)
            p2b = pe.n

            @blk.scalar
            def _(s):
                s.wait_ge(pe.sem, p2b)
                last = None
                for c_ in range(NCH):
                    bk, off = c_ % 2, (c_ // 2) * 128
                    last = s.copy(out=c128(TTIL, c_), in_=pbb(bk, 128, off))
                se.tick(last, 1)
            s2b = se.n

            pcur, pnxt = NTIL, PPA
            prev_apply_v = v2b
            prev_t_evac = s2b
            for j in range(1, 7):
                @blk.tensor
                def _(t, pcur=pcur, prev_t_evac=prev_t_evac, prev_apply_v=prev_apply_v):
                    t.wait_ge(se.sem, prev_t_evac)
                    t.wait_ge(ve.sem, prev_apply_v)
                    last = None
                    for c_ in range(NCH):
                        bk, off = 2 + c_ % 4, (c_ // 4) * 128
                        last = t.matmul(PB[bk][:, off:off + 128], c128(TTIL, c_),
                                        c128(pcur, c_), start=True, stop=True)
                    pe.tick(last, 1)
                ppw = pe.n

                @blk.scalar
                def _(s, pnxt=pnxt, ppw=ppw):
                    s.wait_ge(pe.sem, ppw)
                    last = None
                    for c_ in range(NCH):
                        bk, off = 2 + c_ % 4, (c_ // 4) * 128
                        last = s.copy(out=c128(pnxt, c_), in_=PB[bk][:, off:off + 128])
                    se.tick(last, 1)
                spw = se.n

                @blk.tensor
                def _(t, pnxt=pnxt, spw=spw):
                    t.wait_ge(se.sem, spw)
                    last = None
                    for c_ in range(NCH):
                        bk, off = c_ % 2, (c_ // 2) * 128
                        last = t.transpose(pbb(bk, 128, off), c128(pnxt, c_), IDEB)
                    pe.tick(last, 1)
                ptr = pe.n

                @blk.scalar
                def _(s, ptr=ptr):
                    s.wait_ge(pe.sem, ptr)
                    last = None
                    for c_ in range(NCH):
                        bk, off = c_ % 2, (c_ // 2) * 128
                        last = s.copy(out=c128(TTIL, c_), in_=pbb(bk, 128, off))
                    se.tick(last, 1)
                str_ = se.n

                @blk.tensor
                def _(t, str_=str_):
                    t.wait_ge(se.sem, str_)
                    last = None
                    for c_ in range(NCH):
                        bk, off = 2 + c_ % 4, (c_ // 4) * 128
                        t.matmul(PB[bk][:, off:off + 128], IDEB, c128(XTIL, c_),
                                 start=True, stop=False)
                        last = t.matmul(PB[bk][:, off:off + 128], c128(TTIL, c_),
                                        c128(XTIL, c_), start=False, stop=True)
                    pe.tick(last, 1)
                pap = pe.n

                @blk.vector
                def _(v, pap=pap):
                    v.wait_ge(pe.sem, pap)
                    last = None
                    for c_ in range(NCH):
                        bk, off = 2 + c_ % 4, (c_ // 4) * 128
                        last = v.tensor_copy(out=c128(XTIL, c_), in_=PB[bk][:, off:off + 128])
                    ve.tick(last, 1)
                prev_apply_v = ve.n
                prev_t_evac = str_
                pcur, pnxt = pnxt, (PPB if pnxt is PPA else PPA)

            @blk.tensor
            def _(t):
                t.wait_ge(ve.sem, prev_apply_v)
                last = None
                for c_ in range(NCH):
                    bk, off = c_ % 2, (c_ // 2) * 128
                    t.transpose(pbb(bk, 128, off), VB[:, c_ * 128:(c_ + 1) * 128], IDEB)
                    last = t.transpose(pbb(2 + bk, 128, off), KH[:, c_ * 128:(c_ + 1) * 128],
                                       IDEB)
                pe.tick(last, 1)
            p2c = pe.n

            @blk.scalar
            def _(s):
                s.wait_ge(pe.sem, p2c)
                last = None
                for c_ in range(NCH):
                    bk, off = c_ % 2, (c_ // 2) * 128
                    last = s.copy(out=c128(KTT, c_), in_=pbb(2 + bk, 128, off))
                se.tick(last, 1)
            s2c = se.n

            @blk.vector
            def _(v):
                v.wait_ge(pe.sem, p2c)
                v.wait_ge(se.sem, s2c)
                last = None
                for c_ in range(NCH):
                    bk, off = c_ % 2, (c_ // 2) * 128
                    v.tensor_scalar(out=RHSF[:, c_ * 256:c_ * 256 + 128],
                                    in0=pbb(bk, 128, off), scalar1=BCC[:, c_:c_ + 1],
                                    scalar2=None, op0=ALU.mult)
                    v.tensor_scalar(out=RHSF[:, c_ * 256 + 128:c_ * 256 + 256],
                                    in0=pbb(2 + bk, 128, off), scalar1=BAC[:, c_:c_ + 1],
                                    scalar2=None, op0=ALU.mult)
                    last = v.tensor_scalar(out=c128(KRF, c_), in0=pbb(2 + bk, 128, off),
                                           scalar1=RC_[:, c_:c_ + 1], scalar2=None,
                                           op0=ALU.mult)
                ve.tick(last, 1)
            v2c = ve.n

            @blk.tensor
            def _(t):
                t.wait_ge(ve.sem, v2c)
                t.wait_ge(se.sem, s2c)
                last = None
                for c_ in range(NCH):
                    bk, off = c_ % 8, (c_ // 8) * 256
                    last = t.matmul(PB[bk][:, off:off + 256], c128(XTIL, c_),
                                    RHSF[:, c_ * 256:(c_ + 1) * 256],
                                    start=True, stop=True)
                pe.tick(last, 1)
            p2d = pe.n

            @blk.scalar
            def _(s):
                s.wait_ge(pe.sem, p2d)
                last = None
                for c_ in range(NCH):
                    bk, off = c_ % 8, (c_ // 8) * 256
                    last = s.copy(out=c128(UVB, c_), in_=PB[bk][:, off:off + 128])
                se.tick(last, 1)
            s2d = se.n

            @blk.vector
            def _(v):
                v.wait_ge(pe.sem, p2d)
                v.wait_ge(se.sem, s2d)
                last = None
                for c_ in range(NCH):
                    bk, off = c_ % 8, (c_ // 8) * 256
                    last = v.tensor_scalar(out=c128(WFU, c_),
                                           in0=PB[bk][:, off + 128:off + 256],
                                           scalar1=-1.0, scalar2=None, op0=ALU.mult)
                ve.tick(last, 1)
            v2d = ve.n

            @blk.tensor
            def _(t):
                t.wait_ge(ve.sem, v2d)
                t.wait_ge(se.sem, s2d)
                last = None
                for c_ in range(NCH):
                    bk, off = c_ % 2, (c_ // 2) * 128
                    last = t.transpose(pbb(bk, 128, off), c128(WFU, c_), IDEB)
                pe.tick(last, 1)
            p2e = pe.n

            @blk.scalar
            def _(s):
                s.wait_ge(pe.sem, p2e)
                last = None
                for c_ in range(NCH):
                    bk, off = c_ % 2, (c_ // 2) * 128
                    last = s.copy(out=c128(WTN, c_), in_=pbb(bk, 128, off))
                se.tick(last, 1)
            s2e = se.n

            @blk.tensor
            def _(t):
                t.wait_ge(se.sem, s2e)
                last = None
                for c_ in range(NCH):
                    bk, off = 2 + c_ % 4, (c_ // 4) * 128
                    last = t.matmul(PB[bk][:, off:off + 128], c128(WFU, c_),
                                    c128(KRF, c_), start=True, stop=True)
                pe.tick(last, 1)
            p2f = pe.n

            @blk.vector
            def _(v):
                v.wait_ge(pe.sem, p2f)
                last = None
                for c_ in range(NCH):
                    bk, off = 2 + c_ % 4, (c_ // 4) * 128
                    last = v.scalar_tensor_tensor(out=c128(PCTF, c_), in0=IDE,
                                                  scalar=ALC[:, c_:c_ + 1],
                                                  in1=PB[bk][:, off:off + 128],
                                                  op0=ALU.mult, op1=ALU.add)
                ve.tick(last, 1)
            v2f = ve.n

            @blk.tensor
            def _(t):
                t.wait_ge(ve.sem, v2f)
                last = None
                for c_ in range(NCH):
                    bk, off = 2 + c_ % 4, (c_ // 4) * 128
                    last = t.matmul(PB[bk][:, off:off + 128], c128(KRF, c_),
                                    c128(UVB, c_), start=True, stop=True)
                pe.tick(last, 1)
            p2g = pe.n

            @blk.scalar
            def _(s):
                s.wait_ge(pe.sem, p2g)
                last = None
                for c_ in range(NCH):
                    bk, off = 2 + c_ % 4, (c_ // 4) * 128
                    last = s.copy(out=c128(DCF, c_), in_=PB[bk][:, off:off + 128])
                se.tick(last, 1)
            s2g = se.n

            vcomp = max(v2f, 0)
            scomp = s2g
            for c_ in range(NCH):
                if c_ == 0:
                    @blk.tensor
                    def _(t, scomp=scomp):
                        t.wait_ge(ve.sem, v2f)
                        t.wait_ge(se.sem, scomp)
                        pe.tick(t.transpose(pbb(6, 128, 768), c128(PCTF, 0), IDEB), 1)
                    pc0 = pe.n

                    @blk.vector
                    def _(v, pc0=pc0):
                        v.wait_ge(pe.sem, pc0)
                        v.tensor_copy(out=PCM, in_=pbb(6, 128, 768))
                        v.tensor_copy(out=PCMT, in_=c128(PCTF, 0))
                        ve.tick(v.tensor_copy(out=DCM, in_=c128(DCF, 0)), 1)
                    vcomp = ve.n
                else:
                    @blk.tensor
                    def _(t, c_=c_, vcomp=vcomp):
                        t.wait_ge(ve.sem, vcomp)
                        t.matmul(PB[6][:, 384:512], c128(PCTF, c_), PCM,
                                 start=True, stop=True)
                        t.matmul(PB[7][:, 256:384], PCM, c128(PCTF, c_),
                                 start=True, stop=True)
                        t.matmul(PB[7][:, 384:512], c128(PCTF, c_), DCM,
                                 start=True, stop=False)
                        pe.tick(t.matmul(PB[7][:, 384:512], IDEB, c128(DCF, c_),
                                         start=False, stop=True), 1)
                    pcmp = pe.n

                    @blk.vector
                    def _(v, pcmp=pcmp):
                        v.wait_ge(pe.sem, pcmp)
                        v.tensor_copy(out=PCM, in_=PB[6][:, 384:512])
                        v.tensor_copy(out=PCMT, in_=PB[7][:, 256:384])
                        ve.tick(v.tensor_copy(out=DCM, in_=PB[7][:, 384:512]), 1)
                    vcomp = ve.n

        if STAGE >= 3:
            @blk.scalar
            def _(s):
                s.wait_ge(ve.sem, ve.n)
                sw.tick(s.dma_start(out=b_pd_in.ap()[0:128, :], in_=PCMT), 16)
                sw.tick(s.dma_start(out=b_pd_in.ap()[128:256, :], in_=DCM), 16)
            sw_pd = sw.n

            @blk.gpsimd
            def _(g):
                g.wait_ge(sw.sem, sw_pd)
                cc[1].tick(g.collective_compute(
                    "AllGather", ALU.bypass, replica_groups=[list(range(NCORE))],
                    ins=[b_pd_in.ap()], outs=[b_pd_out.ap()]), 1)
                g.wait_ge(cc[1].sem, 1)
                for k in range(6):
                    gq.tick(g.indirect_dma_start(
                        out=GPD[k], out_offset=None,
                        in_=b_pd_out.ap()[:, :],
                        in_offset=bass.IndirectOffsetOnAxis(ap=SMI[:, 1 + k:2 + k], axis=0)), 16)
            gq_pd = gq.n

            @blk.vector
            def _(v):
                v.wait_ge(gq.sem, gq_pd)
                v.memset(MACC, 0.0)
                ve.tick(v.memset(FSC[:, 0:128], 0.0), 1)
            v6 = ve.n

            for jstep in range(3):
                @blk.tensor
                def _(t, jstep=jstep, v6=v6):
                    t.wait_ge(ve.sem, v6)
                    t.matmul(PB[0][:, 0:128], GPD[2 * jstep], MACC, start=True, stop=False)
                    pe.tick(t.matmul(PB[0][:, 0:128], IDEB, GPD[2 * jstep + 1],
                                     start=False, stop=True), 1)
                p11 = pe.n

                @blk.vector
                def _(v, jstep=jstep, p11=p11):
                    v.wait_ge(pe.sem, p11)
                    ve.tick(v.tensor_copy(out=MACC, in_=PB[0][:, 0:128]), 1)
                    v.wait_ge(ve.sem, ve.n)
                    ve.tick(v.scalar_tensor_tensor(out=FSC[:, 0:128], in0=PB[0][:, 0:128],
                                                   scalar=MSL[:, jstep + 1:jstep + 2],
                                                   in1=FSC[:, 0:128],
                                                   op0=ALU.mult, op1=ALU.add), 1)
                v6 = ve.n

            @blk.vector
            def _(v):
                v.wait_ge(ve.sem, v6)
                ve.tick(v.tensor_copy(out=c128(MBC, 0), in_=FSC[:, 0:128]), 1)
            v7 = ve.n

            if DBG:
                @blk.sync
                def _(sy):
                    sy.wait_ge(ve.sem, v7)
                    st.tick(sy.dma_start(out=dbg["dmst"][:, :], in_=c128(MBC, 0)), 16)

        if STAGE >= 4:
            vch = v7
            sch = se.n
            for c_ in range(NCH - 1):
                @blk.tensor
                def _(t, c_=c_, vch=vch, sch=sch):
                    t.wait_ge(ve.sem, vch)
                    if c_ > 0:
                        t.wait_ge(se.sem, sch)
                    bk = c_ % 2
                    t.matmul(PB[bk][:, 0:128], c128(PCTF, c_), c128(MBC, c_),
                             start=True, stop=False)
                    pe.tick(t.matmul(PB[bk][:, 0:128], IDEB, c128(DCF, c_),
                                     start=False, stop=True), 1)
                pch = pe.n

                @blk.scalar
                def _(s, c_=c_, pch=pch):
                    s.wait_ge(pe.sem, pch)
                    se.tick(s.copy(out=c128(MBC, c_ + 1), in_=PB[c_ % 2][:, 0:128]), 1)
                sch = se.n

            @blk.tensor
            def _(t):
                t.wait_ge(se.sem, sch)
                last = None
                for c_ in range(NCH):
                    bk, off = 2 + c_ % 2, (c_ // 2) * 128
                    last = t.transpose(pbb(bk, 128, off), c128(MBC, c_), IDEB)
                pe.tick(last, 1)
            p4a = pe.n

            @blk.scalar
            def _(s):
                s.wait_ge(pe.sem, p4a)
                last = None
                for c_ in range(NCH):
                    bk, off = 2 + c_ % 2, (c_ // 2) * 128
                    last = s.copy(out=c128(MTC, c_), in_=pbb(bk, 128, off))
                se.tick(last, 1)
            s4a = se.n

            @blk.tensor
            def _(t):
                t.wait_ge(se.sem, s4a)
                last = None
                for c_ in range(NCH):
                    bk, off = 4 + c_ % 4, (c_ // 4) * 128
                    t.matmul(PB[bk][:, off:off + 128], IDEB, c128(UVB, c_),
                             start=True, stop=False)
                    last = t.matmul(PB[bk][:, off:off + 128], c128(WTN, c_), c128(MBC, c_),
                                    start=False, stop=True)
                pe.tick(last, 1)
            p4b = pe.n

            @blk.scalar
            def _(s):
                s.wait_ge(pe.sem, p4b)
                last = None
                for c_ in range(NCH):
                    bk, off = 4 + c_ % 4, (c_ // 4) * 128
                    last = s.copy(out=c128(UFU, c_), in_=PB[bk][:, off:off + 128])
                se.tick(last, 1)
            s4b = se.n

            @blk.vector
            def _(v):
                v.wait_ge(pe.sem, p4b)
                v.wait_ge(se.sem, s4b)
                last = None
                for c_ in range(NCH):
                    bk, off = 4 + c_ % 4, (c_ // 4) * 128
                    last = v.tensor_scalar(out=c128(URF, c_), in0=PB[bk][:, off:off + 128],
                                           scalar1=RC_[:, c_:c_ + 1], scalar2=None,
                                           op0=ALU.mult)
                ve.tick(last, 1)
            v4b = ve.n

            @blk.tensor
            def _(t):
                t.wait_ge(se.sem, s4b)
                last = None
                for c_ in range(NCH):
                    bk, off = c_ % 2, (c_ // 2) * 128
                    last = t.transpose(pbb(bk, 128, off), c128(UFU, c_), IDEB)
                pe.tick(last, 1)
            p4c = pe.n

            @blk.scalar
            def _(s):
                s.wait_ge(pe.sem, p4c)
                last = None
                for c_ in range(NCH):
                    bk, off = c_ % 2, (c_ // 2) * 128
                    last = s.copy(out=c128(UTF, c_), in_=pbb(bk, 128, off))
                se.tick(last, 1)
            s4c = se.n

            @blk.tensor
            def _(t):
                t.wait_ge(se.sem, s4c)
                t.wait_ge(ve.sem, v4b)
                last = None
                for c_ in range(NCH):
                    bk, off = 4 + c_ % 4, (c_ // 4) * 128
                    last = t.matmul(PB[bk][:, off:off + 128], c128(UTF, c_),
                                    QB[:, c_ * 128:(c_ + 1) * 128], start=True, stop=True)
                pe.tick(last, 1)
            p4d = pe.n

            @blk.vector
            def _(v):
                v.wait_ge(pe.sem, p4d)
                last = None
                for c_ in range(NCH):
                    bk, off = 4 + c_ % 4, (c_ // 4) * 128
                    last = v.tensor_mul(out=c128(PTF, c_), in0=c128(EIF, c_),
                                        in1=PB[bk][:, off:off + 128])
                ve.tick(last, 1)
            v4d = ve.n

            @blk.tensor
            def _(t):
                t.wait_ge(ve.sem, v4d)
                last = None
                for c_ in range(NCH):
                    bk, off = 4 + c_ % 4, (c_ // 4) * 128
                    t.matmul(PB[bk][:, off:off + 128], c128(KTT, c_), c128(PTF, c_),
                             start=True, stop=False)
                    last = t.matmul(PB[bk][:, off:off + 128], c128(MTC, c_),
                                    QA[:, c_ * 128:(c_ + 1) * 128], start=False, stop=True)
                pe.tick(last, 1)
            p4e = pe.n

            @blk.scalar
            def _(s):
                s.wait_ge(pe.sem, p4e)
                last = None
                for c_ in range(NCH):
                    bk, off = 4 + c_ % 4, (c_ // 4) * 128
                    last = s.copy(out=OBF[:, c_ * 128:(c_ + 1) * 128],
                                  in_=PB[bk][:, off:off + 128])
                se.tick(last, 1)

            if DBG:
                @blk.sync
                def _(sy):
                    sy.wait_ge(se.sem, se.n)
                    st.tick(sy.dma_start(out=dbg["dO"][:, :], in_=OBF), 16)

        if STAGE >= 5:
            @blk.vector
            def _(v, st_now=st.n):
                v.wait_ge(se.sem, se.n)
                if DBG:
                    v.wait_ge(st.sem, st_now)
                ve.tick(v.tensor_mul(out=OBF, in0=OBF, in1=GF), 1)
            v11 = ve.n

            vwo = {}
            for h in range(8):
                bk0 = 0 if h % 2 == 0 else 4

                @blk.tensor
                def _(t, h=h, bk0=bk0, v11=v11):
                    if h == 0:
                        t.wait_ge(ve.sem, v11)
                    if h >= 2:
                        t.wait_ge(ve.sem, vwo[h - 2])
                    pe_i = None
                    for tc in range(4):
                        pe_i = t.matmul(PB[bk0 + tc][:, 0:512], wo_t(h),
                                        OBF[:, tc * 512:(tc + 1) * 512],
                                        start=True, stop=True)
                    pe.tick(pe_i, 1)
                p19 = pe.n

                @blk.vector
                def _(v, h=h, bk0=bk0, p19=p19):
                    v.wait_ge(pe.sem, p19)
                    last = None
                    for tc in range(4):
                        sl2 = slice(h * T + tc * 512, h * T + (tc + 1) * 512)
                        last = v.scalar_tensor_tensor(out=XFFN[:, sl2],
                                                      in0=PB[bk0 + tc][:, 0:512],
                                                      scalar=BOC[:, h:h + 1],
                                                      in1=XFFN[:, sl2],
                                                      op0=ALU.add, op1=ALU.add)
                    ve.tick(last, 1)
                vwo[h] = ve.n
            lastv = ve.n

            for h in range(8):
                @blk.scalar
                def _(s, h=h, lastv=lastv, se_now=se.n):
                    s.wait_ge(ve.sem, lastv)
                    if h > 0:
                        s.wait_ge(pe.sem, pe.n)
                        s.wait_ge(se.sem, se_now)
                    se.tick(s.activation(out=STB1, in_=XFFN[:, h * T:(h + 1) * T],
                                           func=AF.Identity), 1)
                    s.wait_ge(se.sem, se.n)
                    se.tick(s.activation(out=STB2, in_=STB1, func=AF.Square), 1)
                s10 = se.n

                @blk.tensor
                def _(t, h=h, s10=s10):
                    t.wait_ge(se.sem, s10)
                    if h == 0:
                        t.wait_ge(ve.sem, ve.n)
                    last = None
                    for tc in range(4):
                        last = t.matmul(PB[4 + tc][0:1, 0:512], ONECB,
                                        STB1[:, tc * 512:(tc + 1) * 512],
                                        start=(h == 0), stop=(h == 7))
                        last = t.matmul(PB[tc][0:1, 0:512], ONECB,
                                        STB2[:, tc * 512:(tc + 1) * 512],
                                        start=(h == 0), stop=(h == 7))
                    pe.tick(last, 1)
            p20 = pe.n

            @blk.vector
            def _(v):
                v.wait_ge(pe.sem, p20)
                for tc in range(4):
                    v.tensor_scalar(out=MROW[:, tc * 512:(tc + 1) * 512],
                                    in0=PB[4 + tc][0:1, 0:512], scalar1=1.0 / D,
                                    scalar2=None, op0=ALU.mult)
                    v.tensor_scalar(out=SQR[:, tc * 512:(tc + 1) * 512],
                                    in0=PB[tc][0:1, 0:512], scalar1=1.0 / D,
                                    scalar2=None, op0=ALU.mult)
                ve.tick(v.engine_nop(), 1)
                v.wait_ge(ve.sem, ve.n)
                ve.tick(v.tensor_mul(out=VARR, in0=MROW, in1=MROW), 1)
                v.wait_ge(ve.sem, ve.n)
                v.tensor_sub(out=VARR, in0=SQR, in1=VARR)
                ve.tick(v.engine_nop(), 1)
                v.wait_ge(ve.sem, ve.n)
                ve.tick(v.tensor_scalar_add(out=VARR, in0=VARR, scalar1=1e-5), 1)
            v12 = ve.n

            @blk.scalar
            def _(s):
                s.wait_ge(ve.sem, v12)
                se.tick(s.activation(out=VARR, in_=VARR, func=AF.Sqrt), 1)
            s11 = se.n

            @blk.vector
            def _(v):
                v.wait_ge(se.sem, s11)
                ve.tick(v.reciprocal(out=RSTD, in_=VARR), 1)
                v.wait_ge(ve.sem, ve.n)
                ve.tick(v.tensor_mul(out=MRSTD, in0=MROW, in1=RSTD), 1)
            v13 = ve.n

            @blk.tensor
            def _(t):
                t.wait_ge(ve.sem, v13)
                last = None
                for tc in range(4):
                    last = t.matmul(PB[4 + tc][:, 0:512], ONE,
                                    RSTD[:, tc * 512:(tc + 1) * 512],
                                    start=True, stop=True)
                    last = t.matmul(PB[tc][:, 0:512], ONE,
                                    MRSTD[:, tc * 512:(tc + 1) * 512],
                                    start=True, stop=True)
                pe.tick(last, 1)
            p21 = pe.n

            @blk.scalar
            def _(s):
                s.wait_ge(pe.sem, p21)
                last = None
                for tc in range(4):
                    last = s.copy(out=STF[:, tc * 512:(tc + 1) * 512], in_=PB[4 + tc][:, 0:512])
                se.tick(last, 1)
                s.wait_ge(se.sem, se.n)
                last = None
                for tc in range(4):
                    last = s.copy(out=FSC[:, tc * 512:(tc + 1) * 512], in_=PB[tc][:, 0:512])
                se.tick(last, 1)
            s12 = se.n

            for h in range(8):
                @blk.vector
                def _(v, h=h, s12=s12):
                    v.wait_ge(se.sem, s12)
                    sl = slice(h * T, (h + 1) * T)
                    v.tensor_mul(out=XFFN[:, sl], in0=XFFN[:, sl], in1=STF[:, 0:2048])
                    ve.tick(v.engine_nop(), 1)
                    v.wait_ge(ve.sem, ve.n)
                    ve.tick(v.tensor_sub(out=XFFN[:, sl], in0=XFFN[:, sl],
                                         in1=FSC[:, 0:2048]), 1)
                    v.wait_ge(ve.sem, ve.n)
                    ve.tick(v.tensor_reduce(out=SM[:, 40 + h:41 + h], in_=XFFN[:, sl],
                                            axis=AX.X, op=ALU.max,
                                            apply_absolute_value=True), 1)
                lastv = ve.n

            @blk.vector
            def _(v):
                v.wait_ge(ve.sem, lastv)
                ve.tick(v.tensor_reduce(out=SM[:, 48:49], in_=SM[:, 40:48], axis=AX.X,
                                        op=ALU.max, apply_absolute_value=False), 1)
            v14 = ve.n

            @blk.tensor
            def _(t):
                t.wait_ge(ve.sem, v14)
                t.wait_ge(se.sem, se.n)
                pe.tick(t.transpose(PB[5][0:1, 0:128], SM[:, 48:49], IDE), 1)
            pe_sx = pe.n

            @blk.vector
            def _(v):
                v.wait_ge(pe.sem, pe_sx)
                ve.tick(v.tensor_reduce(out=SM[0:1, 49:50], in_=PB[5][0:1, 0:128],
                                        axis=AX.X, op=ALU.max), 1)
            ve_sx = ve.n

            @blk.gpsimd
            def _(g):
                g.wait_ge(ve.sem, ve_sx)
                gq.tick(g.dma_start(out=b_mx_in.ap()[0:1, 0:1], in_=SM[0:1, 49:50]), 16)
                g.wait_ge(gq.sem, gq.n)
                cc[2].tick(g.collective_compute(
                    "AllReduce", ALU.max, replica_groups=[list(range(NCORE))],
                    ins=[b_mx_in.ap()[0:1, 0:1]], outs=[b_mx_out.ap()[0:1, 0:1]]), 1)
                g.wait_ge(cc[2].sem, cc[2].n)
                gq.tick(g.dma_start(out=SM[0:1, 50:51], in_=b_mx_out.ap()[0:1, 0:1]), 16)
            gq_sx = gq.n

            @blk.vector
            def _(v):
                v.wait_ge(gq.sem, gq_sx)
                v.tensor_scalar(out=SM[0:1, 51:52], in0=SM[0:1, 50:51],
                                scalar1=1.0 / QMAX, scalar2=1e-12, op0=ALU.mult, op1=ALU.max)
                ve.tick(v.engine_nop(), 1)
                v.wait_ge(ve.sem, ve.n)
                ve.tick(v.reciprocal(out=SM[0:1, 52:53], in_=SM[0:1, 51:52]), 1)
            v15 = ve.n

            @blk.tensor
            def _(t):
                t.wait_ge(ve.sem, v15)
                t.wait_ge(se.sem, se.n)
                pe.tick(t.matmul(PB[5][:, 0:2], ONE, SM[0:1, 51:53], start=True, stop=True), 1)
            p22 = pe.n

            @blk.vector
            def _(v):
                v.wait_ge(pe.sem, p22)
                ve.tick(v.tensor_copy(out=SCL[:, 0:2], in_=PB[5][:, 0:2]), 1)
            v16 = ve.n

            xqv = {}
            for h in range(8):
                ints = INT1 if h % 2 == 0 else INT1o

                @blk.vector
                def _(v, h=h, ints=ints, v16=v16):
                    if h < 2:
                        v.wait_ge(ve.sem, v16)
                    else:
                        v.wait_ge(ve.sem, xqv[h - 2])
                    sl = slice(h * T, (h + 1) * T)
                    v.tensor_scalar(out=ints, in0=XFFN[:, sl], scalar1=SCL[:, 1:2],
                                    scalar2=None, op0=ALU.mult)
                    ve.tick(v.engine_nop(), 1)
                    v.wait_ge(ve.sem, ve.n)
                    ve.tick(v.tensor_scalar(out=XBF[h], in0=ints, scalar1=SCL[:, 0:1],
                                            scalar2=None, op0=ALU.mult), 1)
                xqv[h] = ve.n

            if DBG:
                @blk.sync
                def _(sy):
                    sy.wait_ge(ve.sem, ve.n)
                    st.tick(sy.dma_start(
                        out=dbg["dxffn"].rearrange("(h p) t -> p h t", p=128),
                        in_=XFFN.ap().rearrange("p (h t) -> p h t", h=8)), 16)

        def emit_mm_ot(ot, bsrc, bdst, acc_col, pe_used, ve_red, se_evac, sw_spill,
                       first_extra_wait=None):
            par = ot % 2
            wsl = GEN[:, 20480 + par * 1024:20480 + par * 1024 + 1024]
            rank, oc = ot // 3, ot % 3
            LDP = ld if par == 0 else ldb
            SWP = sw if par == 0 else swb

            @blk.sync
            def _(sy, ot=ot, rank=rank, oc=oc, wsl=wsl, LDP=LDP):
                sy.wait_ge(pe.sem, pe.n if ot < 2 else pe_used[ot - 2])
                for i in range(8):
                    LDP.tick(sy.dma_start(
                        out=wsl[:, i * 128:(i + 1) * 128],
                        in_=bsrc[rank * D + i * 128:rank * D + (i + 1) * 128,
                                 oc * 128:(oc + 1) * 128]), 16)
            need_ld = LDP.n

            @blk.tensor
            def _(t, ot=ot, par=par, wsl=wsl, need_ld=need_ld, LDP=LDP):
                t.wait_ge(LDP.sem, need_ld)
                if first_extra_wait is not None:
                    first_extra_wait(t)
                if ot == 0:
                    t.wait_ge(cc[0].sem, 3)
                    t.wait_ge(ve.sem, ve.n)
                    t.wait_ge(se.sem, se.n)
                if ot >= 2:
                    t.wait_ge(se.sem, se_evac[ot - 2])
                bk = [0, 1, 2, 3] if par == 0 else [4, 5, 6, 7]
                last = None
                for i in range(8):
                    for tc in range(4):
                        last = t.matmul(PB[bk[tc]][:, 0:512], wsl[:, i * 128:(i + 1) * 128],
                                        XBF[i][:, tc * 512:(tc + 1) * 512],
                                        start=(i == 0), stop=(i == 7))
                pe.tick(last, 1)
            pe_used[ot] = pe.n
            need_pe = pe.n

            @blk.scalar
            def _(s, ot=ot, par=par, need_pe=need_pe, SWP=SWP, st_now=st.n,
                  sw_now=sw.n, swb_now=swb.n):
                s.wait_ge(pe.sem, need_pe)
                if DBG and ot < 2:
                    s.wait_ge(st.sem, st_now)
                if ot < 2:
                    s.wait_ge(sw.sem, sw_now)
                    s.wait_ge(swb.sem, swb_now)
                if ot >= 2:
                    s.wait_ge(SWP.sem, sw_spill[ot - 2])
                    s.wait_ge(ve.sem, ve_red[ot - 2])
                stg = GEN[:, 16384:18432] if par == 0 else GEN[:, 18432:20480]
                bk = [0, 1, 2, 3] if par == 0 else [4, 5, 6, 7]
                last = None
                for tc in range(4):
                    last = s.copy(out=stg[:, tc * 512:(tc + 1) * 512],
                                  in_=PB[bk[tc]][:, 0:512])
                se.tick(last, 1)
            se_evac[ot] = se.n
            need_se = se.n

            @blk.vector
            def _(v, ot=ot, par=par, acc_col=acc_col, need_se=need_se, ve_now=ve.n):
                v.wait_ge(se.sem, need_se)
                if ot >= 1:
                    v.wait_ge(ve.sem, ve_red[ot - 1])
                else:
                    v.wait_ge(ve.sem, ve_now)
                stg = GEN[:, 16384:18432] if par == 0 else GEN[:, 18432:20480]
                v.tensor_reduce(out=SM[:, 54:55], in_=stg, axis=AX.X,
                                op=ALU.max, apply_absolute_value=True)
                ve.tick(v.engine_nop(), 1)
                v.wait_ge(ve.sem, ve.n)
                if ot == 0:
                    ve.tick(v.tensor_copy(out=SM[:, acc_col:acc_col + 1],
                                          in_=SM[:, 54:55]), 1)
                else:
                    ve.tick(v.tensor_tensor(out=SM[:, acc_col:acc_col + 1],
                                            in0=SM[:, acc_col:acc_col + 1],
                                            in1=SM[:, 54:55], op=ALU.max), 1)
            ve_red[ot] = ve.n

            @blk.scalar
            def _(s, ot=ot, par=par, need_se=need_se, SWP=SWP):
                s.wait_ge(se.sem, need_se)
                stg = GEN[:, 16384:18432] if par == 0 else GEN[:, 18432:20480]
                SWP.tick(s.dma_start(out=bdst[ot * 128:(ot + 1) * 128, :], in_=stg), 16)
            sw_spill[ot] = SWP.n

        def bc_cols(rsrc, k, dstc, waitv):
            @blk.tensor
            def _(t, rsrc=rsrc, k=k, waitv=waitv):
                t.wait_ge(ve.sem, waitv)
                t.wait_ge(se.sem, se.n)
                pe.tick(t.matmul(PB[5][:, 0:k], ONE, SM[0:1, rsrc:rsrc + k],
                                 start=True, stop=True), 1)
            pw = pe.n

            @blk.vector
            def _(v, k=k, dstc=dstc, pw=pw):
                v.wait_ge(pe.sem, pw)
                ve.tick(v.tensor_copy(out=SCL[:, dstc:dstc + k], in_=PB[5][:, 0:k]), 1)
            return ve.n

        def allred_max(accc, rowdst, ccx):
            @blk.tensor
            def _(t, accc=accc):
                t.wait_ge(ve.sem, ve.n)
                t.wait_ge(se.sem, se.n)
                pe.tick(t.transpose(PB[5][0:1, 0:128], SM[:, accc:accc + 1], IDE), 1)
            pw = pe.n

            @blk.vector
            def _(v, pw=pw, rowdst=rowdst):
                v.wait_ge(pe.sem, pw)
                ve.tick(v.tensor_reduce(out=SM[0:1, rowdst:rowdst + 1],
                                        in_=PB[5][0:1, 0:128],
                                        axis=AX.X, op=ALU.max), 1)
            vw = ve.n

            @blk.gpsimd
            def _(g, vw=vw, rowdst=rowdst, ccx=ccx):
                g.wait_ge(ve.sem, vw)
                gq.tick(g.dma_start(out=b_mx_in.ap()[0:1, 0:1],
                                    in_=SM[0:1, rowdst:rowdst + 1]), 16)
                g.wait_ge(gq.sem, gq.n)
                cc[ccx].tick(g.collective_compute(
                    "AllReduce", ALU.max, replica_groups=[list(range(NCORE))],
                    ins=[b_mx_in.ap()[0:1, 0:1]], outs=[b_mx_out.ap()[0:1, 0:1]]), 1)
                g.wait_ge(cc[ccx].sem, cc[ccx].n)
                gq.tick(g.dma_start(out=SM[0:1, rowdst:rowdst + 1],
                                    in_=b_mx_out.ap()[0:1, 0:1]), 16)
            return gq.n

        def emit_p3_ot(ot, ld_in, po_g2, ve_g2, sw_g2):
            par = ot % 2
            stg_in = STB1 if par == 0 else STB2
            stfb = STF.ap().bitcast(BF16)
            stg_out = stfb[:, 0:2048] if par == 0 else stfb[:, 2048:4096]
            i1 = (GEN[:, 24576:28672].bitcast(I32) if par == 0
                  else GEN[:, 28672:32768].bitcast(I32))
            i2 = GENF[:, 2048:4096].bitcast(I32)
            sgf = SIGF if par == 0 else SIGFo
            LDP = lp5 if par == 0 else lp5b
            SWP = sp3 if par == 0 else sp3b

            @blk.sync
            def _(sy, ot=ot, stg_in=stg_in, LDP=LDP, sw_now=sw.n, swb_now=swb.n):
                if ot < 2:
                    sy.wait_ge(ve.sem, ve.n)
                    sy.wait_ge(sw.sem, sw_now)
                    sy.wait_ge(swb.sem, swb_now)
                else:
                    sy.wait_ge(ve.sem, ve_g2[ot - 2])
                LDP.tick(sy.dma_start(out=stg_in,
                                      in_=b_gate.ap()[ot * 128:(ot + 1) * 128, :]), 16)
            ld_in[ot] = LDP.n

            @blk.vector
            def _(v, ot=ot, stg_in=stg_in, i1=i1, LDP=LDP):
                v.wait_ge(LDP.sem, ld_in[ot])
                if ot >= 2:
                    v.wait_ge(ve.sem, po_g2[ot - 2])
                ve.tick(v.tensor_scalar(out=i1, in0=stg_in, scalar1=SCL[:, 4:5],
                                        scalar2=None, op0=ALU.mult), 1)
            vv = ve.n

            @blk.scalar
            def _(s, vv=vv, i1=i1, sgf=sgf):
                s.wait_ge(ve.sem, vv)
                se.tick(s.activation(out=sgf, in_=i1, func=AF.Sigmoid,
                                     scale=SCL[:, 2:3]), 1)
            ss = se.n

            @blk.vector
            def _(v, ot=ot, ss=ss, i2=i2, sgf=sgf):
                v.wait_ge(se.sem, ss)
                if ot >= 1:
                    v.wait_ge(ve.sem, po_g2[ot - 1])
                ve.tick(v.tensor_scalar(out=i2, in0=sgf, scalar1=65536.0,
                                        scalar2=65535.0, op0=ALU.mult, op1=ALU.min), 1)
            vv2 = ve.n

            @blk.vector
            def _(v, ot=ot, stg_out=stg_out, vv2=vv2, i1=i1, i2=i2, SWP=SWP):
                v.wait_ge(ve.sem, vv2)
                if ot >= 2:
                    v.wait_ge(SWP.sem, sw_g2[ot - 2])
                ve.tick(v.tensor_tensor(out=stg_out, in0=i1, in1=i2, op=ALU.mult), 1)
            po_g2[ot] = ve.n
            pp = ve.n

            @blk.vector
            def _(v, ot=ot, stg_out=stg_out, pp=pp):
                v.wait_ge(ve.sem, pp)
                if ot >= 1:
                    v.wait_ge(ve.sem, ve_g2[ot - 1])
                v.tensor_reduce(out=SM[:, 45:46], in_=stg_out, axis=AX.X,
                                op=ALU.max, apply_absolute_value=True)
                ve.tick(v.engine_nop(), 1)
                v.wait_ge(ve.sem, ve.n)
                if ot == 0:
                    ve.tick(v.tensor_copy(out=SM[:, 55:56], in_=SM[:, 45:46]), 1)
                else:
                    ve.tick(v.tensor_tensor(out=SM[:, 55:56], in0=SM[:, 55:56],
                                            in1=SM[:, 45:46], op=ALU.max), 1)
            ve_g2[ot] = ve.n

            @blk.scalar
            def _(s, ot=ot, stg_out=stg_out, pp=pp, SWP=SWP):
                s.wait_ge(ve.sem, pp)
                SWP.tick(s.dma_start(out=b_g2.ap()[ot * 128:(ot + 1) * 128, :],
                                     in_=stg_out), 16)
            sw_g2[ot] = SWP.n

        if STAGE >= 6:
            pe_used = {}
            ve_red = {}
            se_evac = {}
            sw_spill = {}
            mm_pass_env = dict(bsrc=b_wg.ap(), bdst=b_gate.ap())
            for ot in range(NOT):
                emit_mm_ot(ot, b_wg.ap(), b_gate.ap(), 56, pe_used, ve_red,
                           se_evac, sw_spill)

            gq_sg = allred_max(56, 58, 3)

            pe_used2 = {}
            ve_red2 = {}
            se_evac2 = {}
            sw_spill2 = {}
            ld_in = {}
            po_g2 = {}
            ve_g2 = {}
            sw_g2 = {}

            for ot in range(6):
                emit_mm_ot(ot, b_wu.ap(), b_up.ap(), 57, pe_used2, ve_red2,
                           se_evac2, sw_spill2)

            @blk.vector
            def _(v, gq_sg=gq_sg):
                v.wait_ge(gq.sem, gq_sg)
                v.tensor_scalar(out=SM[0:1, 60:61], in0=SM[0:1, 58:59],
                                scalar1=1.0 / QMAX, scalar2=1e-12, op0=ALU.mult, op1=ALU.max)
                ve.tick(v.engine_nop(), 1)
                v.wait_ge(ve.sem, ve.n)
                ve.tick(v.reciprocal(out=SM[0:1, 61:62], in_=SM[0:1, 60:61]), 1)
            v_sg = ve.n
            bc_cols(60, 1, 2, v_sg)
            ve_sg2 = bc_cols(61, 1, 4, ve.n)


            def up7_wait(t, vebc=ve.n):
                t.wait_ge(ve.sem, vebc)

            for ot in range(6, NOT):
                emit_mm_ot(ot, b_wu.ap(), b_up.ap(), 57, pe_used2, ve_red2,
                           se_evac2, sw_spill2,
                           first_extra_wait=(up7_wait if ot == 6 else None))
                emit_p3_ot(ot - 6, ld_in, po_g2, ve_g2, sw_g2)
            for ot2 in range(NOT - 6, NOT):
                emit_p3_ot(ot2, ld_in, po_g2, ve_g2, sw_g2)

            gq_su = allred_max(57, 59, 6)

            @blk.vector
            def _(v, gq_su=gq_su):
                v.wait_ge(gq.sem, gq_su)
                v.tensor_scalar(out=SM[0:1, 62:63], in0=SM[0:1, 59:60],
                                scalar1=1.0 / QMAX, scalar2=1e-12, op0=ALU.mult, op1=ALU.max)
                ve.tick(v.engine_nop(), 1)
                v.wait_ge(ve.sem, ve.n)
                ve.tick(v.reciprocal(out=SM[0:1, 63:64], in_=SM[0:1, 62:63]), 1)
            v_su = ve.n
            bc_cols(62, 1, 3, v_su)
            ve_scl6 = bc_cols(63, 1, 5, ve.n)

            gq_g2 = allred_max(55, 54, 4)

            @blk.vector
            def _(v, gq_g2=gq_g2):
                v.wait_ge(gq.sem, gq_g2)
                v.tensor_scalar(out=SM[0:1, 61:62], in0=SM[0:1, 60:61],
                                scalar1=1.0 / 65536.0, scalar2=None, op0=ALU.mult)
                ve.tick(v.engine_nop(), 1)
                v.wait_ge(ve.sem, ve.n)
                v.tensor_mul(out=SM[0:1, 54:55], in0=SM[0:1, 54:55], in1=SM[0:1, 61:62])
                ve.tick(v.engine_nop(), 1)
                v.wait_ge(ve.sem, ve.n)
                v.tensor_scalar(out=SM[0:1, 54:55], in0=SM[0:1, 54:55],
                                scalar1=1.0 / QMAX, scalar2=1e-12, op0=ALU.mult, op1=ALU.max)
                ve.tick(v.engine_nop(), 1)
                v.wait_ge(ve.sem, ve.n)
                ve.tick(v.reciprocal(out=SM[0:1, 53:54], in_=SM[0:1, 54:55]), 1)
                v.wait_ge(ve.sem, ve.n)
                ve.tick(v.tensor_mul(out=SM[0:1, 53:54], in0=SM[0:1, 53:54],
                                     in1=SM[0:1, 61:62]), 1)
            v18 = ve.n
            ve_scl8 = bc_cols(53, 2, 6, v18)

        if STAGE >= 8:
            ld_g2 = {}
            ld_up = {}
            po_op = {}
            ve_op = {}
            sw_op = {}
            for ot in range(NOT):
                par = ot % 2
                stg_g2 = STB1 if par == 0 else STB2
                stg_up = XBF[2] if par == 0 else XBF[3]
                stg_out = XBF[4] if par == 0 else XBF[5]

                LDP = ld if par == 0 else ldb
                SWP = sw if par == 0 else swb

                @blk.sync
                def _(sy, ot=ot, stg_g2=stg_g2, stg_up=stg_up, LDP=LDP):
                    if ot < 2:
                        sy.wait_ge(ve.sem, ve_scl8)
                        sy.wait_ge(sw.sem, sw.n)
                        sy.wait_ge(swb.sem, swb.n)
                        sy.wait_ge(sp3.sem, sp3.n)
                        sy.wait_ge(sp3b.sem, sp3b.n)
                    else:
                        sy.wait_ge(ve.sem, ve_op[ot - 2])
                        sy.wait_ge(ve.sem, po_op[ot - 2])
                    LDP.tick(sy.dma_start(out=stg_g2, in_=b_g2.ap()[ot * 128:(ot + 1) * 128, :]), 16)
                    LDP.tick(sy.dma_start(out=stg_up, in_=b_up.ap()[ot * 128:(ot + 1) * 128, :]), 16)
                need_ld = LDP.n

                i1 = INT1 if par == 0 else INT1o
                i2 = INT2 if par == 0 else INT2o

                @blk.vector
                def _(v, ot=ot, stg_g2=stg_g2, stg_up=stg_up, need_ld=need_ld,
                      i1=i1, i2=i2, LDP=LDP):
                    v.wait_ge(LDP.sem, need_ld)
                    if ot >= 2:
                        v.wait_ge(ve.sem, po_op[ot - 2])
                    v.tensor_scalar(out=i1, in0=stg_g2, scalar1=SCL[:, 6:7],
                                    scalar2=None, op0=ALU.mult)
                    ve.tick(v.tensor_scalar(out=i2, in0=stg_up, scalar1=SCL[:, 5:6],
                                            scalar2=None, op0=ALU.mult), 1)
                vv = ve.n

                @blk.vector
                def _(v, ot=ot, stg_out=stg_out, vv=vv, i1=i1, i2=i2, SWP=SWP):
                    v.wait_ge(ve.sem, vv)
                    if ot >= 2:
                        v.wait_ge(SWP.sem, sw_op[ot - 2])
                    ve.tick(v.tensor_tensor(out=stg_out, in0=i1, in1=i2,
                                            op=ALU.mult), 1)
                po_op[ot] = ve.n
                pp = ve.n

                @blk.vector
                def _(v, ot=ot, stg_out=stg_out, pp=pp):
                    v.wait_ge(ve.sem, pp)
                    if ot >= 1:
                        v.wait_ge(ve.sem, ve_op[ot - 1])
                    v.tensor_reduce(out=SM[:, 54:55], in_=stg_out, axis=AX.X,
                                    op=ALU.max, apply_absolute_value=True)
                    ve.tick(v.engine_nop(), 1)
                    v.wait_ge(ve.sem, ve.n)
                    if ot == 0:
                        ve.tick(v.tensor_copy(out=SM[:, 55:56], in_=SM[:, 54:55]), 1)
                    else:
                        ve.tick(v.tensor_tensor(out=SM[:, 55:56], in0=SM[:, 55:56],
                                                in1=SM[:, 54:55], op=ALU.max), 1)
                ve_op[ot] = ve.n

                @blk.scalar
                def _(s, ot=ot, stg_out=stg_out, pp=pp, SWP=SWP):
                    s.wait_ge(ve.sem, pp)
                    SWP.tick(s.dma_start(out=b_opre.ap()[ot * 128:(ot + 1) * 128, :],
                                         in_=stg_out), 16)
                sw_op[ot] = SWP.n

            @blk.tensor
            def _(t):
                t.wait_ge(ve.sem, ve.n)
                t.wait_ge(se.sem, se.n)
                pe.tick(t.transpose(PB[5][0:1, 0:128], SM[:, 55:56], IDE), 1)
            pe_ox = pe.n

            @blk.vector
            def _(v):
                v.wait_ge(pe.sem, pe_ox)
                ve.tick(v.tensor_reduce(out=SM[0:1, 54:55], in_=PB[5][0:1, 0:128],
                                        axis=AX.X, op=ALU.max), 1)
            ve_ox = ve.n

            @blk.gpsimd
            def _(g):
                g.wait_ge(ve.sem, ve_ox)
                gq.tick(g.dma_start(out=b_mx_in.ap()[0:1, 0:1], in_=SM[0:1, 54:55]), 16)
                g.wait_ge(gq.sem, gq.n)
                cc[5].tick(g.collective_compute(
                    "AllReduce", ALU.max, replica_groups=[list(range(NCORE))],
                    ins=[b_mx_in.ap()[0:1, 0:1]], outs=[b_mx_out.ap()[0:1, 0:1]]), 1)
                g.wait_ge(cc[5].sem, cc[5].n)
                gq.tick(g.dma_start(out=SM[0:1, 54:55], in_=b_mx_out.ap()[0:1, 0:1]), 16)
            gq_o = gq.n

            @blk.vector
            def _(v):
                v.wait_ge(gq.sem, gq_o)
                v.tensor_mul(out=SM[:, 60:61], in0=SCL[:, 3:4], in1=SCL[:, 7:8])
                ve.tick(v.engine_nop(), 1)
                v.wait_ge(ve.sem, ve.n)
                v.tensor_mul(out=SM[0:1, 54:55], in0=SM[0:1, 54:55], in1=SM[0:1, 60:61])
                ve.tick(v.engine_nop(), 1)
                v.wait_ge(ve.sem, ve.n)
                v.tensor_scalar(out=SM[0:1, 54:55], in0=SM[0:1, 54:55],
                                scalar1=1.0 / QMAX, scalar2=1e-12, op0=ALU.mult, op1=ALU.max)
                ve.tick(v.engine_nop(), 1)
                v.wait_ge(ve.sem, ve.n)
                ve.tick(v.reciprocal(out=SM[0:1, 53:54], in_=SM[0:1, 54:55]), 1)
                v.wait_ge(ve.sem, ve.n)
                ve.tick(v.tensor_mul(out=SM[0:1, 53:54], in0=SM[0:1, 53:54],
                                     in1=SM[0:1, 60:61]), 1)
            v19 = ve.n

            @blk.tensor
            def _(t):
                t.wait_ge(ve.sem, v19)
                t.wait_ge(se.sem, se.n)
                pe.tick(t.matmul(PB[5][:, 0:2], ONE, SM[0:1, 53:55], start=True, stop=True), 1)
            p25 = pe.n

            @blk.vector
            def _(v):
                v.wait_ge(pe.sem, p25)
                ve.tick(v.tensor_copy(out=SCL[:, 8:10], in_=PB[5][:, 0:2]), 1)
            ve_scl10 = ve.n

        if STAGE >= 9:
            OJH = [gb(1024 * j, 1024) for j in range(NOT)]
            for half in range(2):
                hof = half * 1024
                p5v = {}
                for j in range(NOT):
                    par = j % 2
                    stg = STB1[:, 0:1024] if par == 0 else STB1[:, 1024:2048]
                    ints = (GEN[:, 26624:28672].bitcast(I32) if par == 0
                            else GEN[:, 28672:30720].bitcast(I32))
                    LDP = lp5 if par == 0 else lp5b

                    @blk.sync
                    def _(sy, j=j, hof=hof, stg=stg, half=half, LDP=LDP):
                        if half == 0 and j < 2:
                            sy.wait_ge(ve.sem, ve_scl10)
                            sy.wait_ge(sw.sem, sw.n)
                            sy.wait_ge(swb.sem, swb.n)
                        elif j >= 2:
                            sy.wait_ge(ve.sem, p5v[j - 2])
                        else:
                            sy.wait_ge(ve.sem, ve.n)
                        LDP.tick(sy.dma_start(
                            out=stg, in_=b_opre.ap()[j * 128:(j + 1) * 128,
                                                     hof:hof + 1024]), 16)
                    need_ld = LDP.n

                    @blk.vector
                    def _(v, j=j, stg=stg, ints=ints, need_ld=need_ld, half=half,
                          pe_now=pe.n, LDP=LDP):
                        v.wait_ge(LDP.sem, need_ld)
                        if half > 0:
                            v.wait_ge(pe.sem, pe_now)
                        v.tensor_scalar(out=ints[:, 0:1024], in0=stg, scalar1=SCL[:, 8:9],
                                        scalar2=None, op0=ALU.mult)
                        ve.tick(v.engine_nop(), 1)
                        v.wait_ge(ve.sem, ve.n)
                        ve.tick(v.tensor_scalar(out=OJH[j], in0=ints[:, 0:1024],
                                                scalar1=SCL[:, 9:10], scalar2=None,
                                                op0=ALU.mult), 1)
                    p5v[j] = ve.n
                ve_p5 = ve.n

                dn_pe = []
                res_ve = {}
                for h in range(8):
                    bk0 = (h % 4) * 2
                    for jg in range(3):
                        gidx = h * 3 + jg
                        par = gidx % 2
                        wsl = GEN[:, 24576 + par * 1024:24576 + par * 1024 + 1024]
                        LDP = ldw2 if par == 0 else ldw2b

                        @blk.sync
                        def _(sy, h=h, jg=jg, wsl=wsl, gidx=gidx, dn_pe=tuple(dn_pe),
                              LDP=LDP):
                            if gidx >= 2:
                                sy.wait_ge(pe.sem, dn_pe[gidx - 2])
                            else:
                                sy.wait_ge(pe.sem, pe.n)
                            for jj in range(8):
                                j = jg * 8 + jj
                                LDP.tick(sy.dma_start(
                                    out=wsl[:, jj * 128:(jj + 1) * 128],
                                    in_=b_wd.ap()[h * FF + j * 128:h * FF + (j + 1) * 128, :]), 16)
                        need_ld = LDP.n

                        @blk.tensor
                        def _(t, h=h, jg=jg, bk0=bk0, wsl=wsl, need_ld=need_ld,
                              ve_p5=ve_p5, res_ve=dict(res_ve), LDP=LDP):
                            t.wait_ge(LDP.sem, need_ld)
                            t.wait_ge(ve.sem, ve_p5)
                            if jg == 0 and (h - 4) in res_ve:
                                t.wait_ge(ve.sem, res_ve[h - 4])
                            last = None
                            for jj in range(8):
                                j = jg * 8 + jj
                                for tc2 in range(2):
                                    last = t.matmul(PB[bk0 + tc2][:, 0:512],
                                                    wsl[:, jj * 128:(jj + 1) * 128],
                                                    OJH[j][:, tc2 * 512:(tc2 + 1) * 512],
                                                    start=(j == 0), stop=(j == NOT - 1))
                            pe.tick(last, 1)
                        dn_pe.append(pe.n)
                    need_pe = pe.n

                    @blk.vector
                    def _(v, h=h, bk0=bk0, hof=hof, need_pe=need_pe, st_now=st.n):
                        v.wait_ge(pe.sem, need_pe)
                        if DBG:
                            v.wait_ge(st.sem, st_now)
                        last = None
                        for tc2 in range(2):
                            sl = slice(h * T + hof + tc2 * 512, h * T + hof + (tc2 + 1) * 512)
                            last = v.tensor_add(out=XFFN[:, sl], in0=XFFN[:, sl],
                                                in1=PB[bk0 + tc2][:, 0:512])
                        ve.tick(last, 1)
                    res_ve[h] = ve.n

                for tc2 in range(2):
                    gsl = slice(hof + tc2 * 512, hof + (tc2 + 1) * 512)
                    ve_r2 = ve.n
                    for h in range(8):
                        @blk.scalar
                        def _(s, h=h, gsl=gsl, ve_r2=ve_r2, se_now=se.n):
                            s.wait_ge(ve.sem, ve_r2)
                            if h > 0:
                                s.wait_ge(pe.sem, pe.n)
                                s.wait_ge(se.sem, se_now)
                            sl = slice(h * T + gsl.start, h * T + gsl.stop)
                            se.tick(s.activation(out=STB2[:, 0:512], in_=XFFN[:, sl],
                                                 func=AF.Identity), 1)
                            s.wait_ge(se.sem, se.n)
                            se.tick(s.activation(out=STB2[:, 512:1024], in_=STB2[:, 0:512],
                                                 func=AF.Square), 1)
                        s13 = se.n

                        @blk.tensor
                        def _(t, h=h, s13=s13):
                            t.wait_ge(se.sem, s13)
                            if h == 0:
                                t.wait_ge(ve.sem, ve.n)
                            t.matmul(PB[0][0:1, 0:512], ONECB, STB2[:, 0:512],
                                     start=(h == 0), stop=(h == 7))
                            last = t.matmul(PB[1][0:1, 0:512], ONECB, STB2[:, 512:1024],
                                            start=(h == 0), stop=(h == 7))
                            pe.tick(last, 1)
                    p26 = pe.n

                    @blk.vector
                    def _(v, p26=p26):
                        v.wait_ge(pe.sem, p26)
                        v.tensor_scalar(out=MROW[:, 0:512], in0=PB[0][0:1, 0:512],
                                        scalar1=1.0 / D, scalar2=None, op0=ALU.mult)
                        ve.tick(v.tensor_scalar(out=SQR[:, 0:512], in0=PB[1][0:1, 0:512],
                                                scalar1=1.0 / D, scalar2=None,
                                                op0=ALU.mult), 1)
                        v.wait_ge(ve.sem, ve.n)
                        ve.tick(v.tensor_mul(out=VARR[:, 0:512], in0=MROW[:, 0:512],
                                             in1=MROW[:, 0:512]), 1)
                        v.wait_ge(ve.sem, ve.n)
                        v.tensor_sub(out=VARR[:, 0:512], in0=SQR[:, 0:512],
                                     in1=VARR[:, 0:512])
                        ve.tick(v.engine_nop(), 1)
                        v.wait_ge(ve.sem, ve.n)
                        ve.tick(v.tensor_scalar_add(out=VARR[:, 0:512],
                                                    in0=VARR[:, 0:512], scalar1=1e-5), 1)
                    v20 = ve.n

                    @blk.scalar
                    def _(s, v20=v20):
                        s.wait_ge(ve.sem, v20)
                        se.tick(s.activation(out=VARR[:, 0:512], in_=VARR[:, 0:512],
                                             func=AF.Sqrt), 1)
                    s14 = se.n

                    @blk.vector
                    def _(v, s14=s14):
                        v.wait_ge(se.sem, s14)
                        ve.tick(v.reciprocal(out=RSTD[:, 0:512], in_=VARR[:, 0:512]), 1)
                        v.wait_ge(ve.sem, ve.n)
                        ve.tick(v.tensor_mul(out=MRSTD[:, 0:512], in0=MROW[:, 0:512],
                                             in1=RSTD[:, 0:512]), 1)
                    v21 = ve.n

                    @blk.tensor
                    def _(t, v21=v21):
                        t.wait_ge(ve.sem, v21)
                        t.matmul(PB[2][:, 0:512], ONE, RSTD[:, 0:512],
                                 start=True, stop=True)
                        pe.tick(t.matmul(PB[3][:, 0:512], ONE, MRSTD[:, 0:512],
                                         start=True, stop=True), 1)
                    p27 = pe.n

                    for h in range(8):
                        @blk.vector
                        def _(v, h=h, gsl=gsl, p27=p27):
                            v.wait_ge(pe.sem, p27)
                            sl = slice(h * T + gsl.start, h * T + gsl.stop)
                            v.tensor_mul(out=XFFN[:, sl], in0=XFFN[:, sl], in1=PB[2][:, 0:512])
                            ve.tick(v.engine_nop(), 1)
                            v.wait_ge(ve.sem, ve.n)
                            ve.tick(v.tensor_sub(out=XFFN[:, sl], in0=XFFN[:, sl],
                                                 in1=PB[3][:, 0:512]), 1)
                        need_ve2 = ve.n

                        @blk.sync
                        def _(sy, h=h, gsl=gsl, need_ve2=need_ve2):
                            sy.wait_ge(ve.sem, need_ve2)
                            sl = slice(h * T + gsl.start, h * T + gsl.stop)
                            st.tick(sy.dma_start(out=o_out[h * 128:(h + 1) * 128, gsl],
                                                 in_=XFFN[:, sl]), 16)

            @blk.sync
            def _(sy):
                sy.wait_ge(st.sem, st.n)
        else:
            @blk.sync
            def _(sy):
                sy.wait_ge(ve.sem, ve.n)
                sy.wait_ge(se.sem, se.n)
                for h in range(8):
                    st.tick(sy.dma_start(out=o_out[h * 128:(h + 1) * 128, :],
                                         in_=XFFN[:, h * T:(h + 1) * T]), 16)
                sy.wait_ge(st.sem, st.n)

    return nc


_NC_CACHE = {}


def _idxdq():
    q = np.zeros((128, 64), np.int32)
    p32 = np.arange(128) // 32
    for it in range(8):
        q[:, it] = 4 * it + p32
        q[:, 8 + it] = 32 + 4 * it + p32
    for it in range(24):
        q[:, 16 + it] = 4 * it + p32
        q[:, 40 + it] = 96 + 4 * it + p32
    return q


def _consts():
    IDENT = np.eye(128, dtype=np.float32)
    s = np.arange(128)
    CUMT = (s[:, None] <= s[None, :]).astype(np.float32)
    MASKT = np.where(s[:, None] < s[None, :], 0.0, -1e30).astype(np.float32)
    ONES1 = np.ones((128, 128), np.float32)
    IDX32 = (np.arange(128) // 32).astype(np.int32).reshape(128, 1)
    return IDENT, CUMT, MASKT, ONES1, IDX32


def kernel(L0_attn_x, L0_attn_Wq, L0_attn_bq, L0_attn_Wk, L0_attn_bk,
           L0_attn_Wv, L0_attn_bv, L0_attn_Wa, L0_attn_ba, L0_attn_Wb,
           L0_attn_bb, L0_attn_Wg, L0_attn_bg, L0_attn_Wo, L0_attn_bo,
           L0_ffn_w_gate, L0_ffn_w_up, L0_ffn_w_down):
    x = np.asarray(L0_attn_x, np.float32)
    f = lambda a: np.ascontiguousarray(np.asarray(a, np.float32))
    Wq, Wk, Wv, Wg = f(L0_attn_Wq), f(L0_attn_Wk), f(L0_attn_Wv), f(L0_attn_Wg)
    Wa, Wb, Wo = f(L0_attn_Wa), f(L0_attn_Wb), f(L0_attn_Wo)
    bq, bk, bv, bg = f(L0_attn_bq), f(L0_attn_bk), f(L0_attn_bv), f(L0_attn_bg)
    ba, bb, bo = f(L0_attn_ba), f(L0_attn_bb), f(L0_attn_bo)
    wgate, wup, wdown = f(L0_ffn_w_gate), f(L0_ffn_w_up), f(L0_ffn_w_down)

    IDENT, CUMT, MASKT, ONES1, IDX32 = _consts()
    shared = {
        "wqT": np.ascontiguousarray(Wq.T), "wkT": np.ascontiguousarray(Wk.T),
        "wvT": np.ascontiguousarray(Wv.T), "wgT": np.ascontiguousarray(Wg.T),
        "wabT": np.ascontiguousarray(np.concatenate([Wa, Wb], 0).T),
        "woT": np.ascontiguousarray(Wo.T),
        "bqc": bq.reshape(C, 1), "bkc": bk.reshape(C, 1),
        "bvc": bv.reshape(C, 1), "bgc": bg.reshape(C, 1),
        "babb": np.ascontiguousarray(
            np.concatenate([np.full((C, 1), ba[0], np.float32),
                            np.full((C, 1), bb[0], np.float32)], 1)),
        "boc": np.ascontiguousarray(bo.reshape(8, 128).T),
        "IDENT": IDENT, "CUMT": CUMT, "MASKT": MASKT, "ONES1": ONES1,
        "IDX32": IDX32, "IDXDQ": _idxdq(),
    }

    if "nc" not in _NC_CACHE:
        _NC_CACHE["nc"] = build_nc()
    nc = _NC_CACHE["nc"]

    in_maps = []
    for r in range(NCORE):
        seq, t0 = r // 4, (r % 4) * T
        grp = r % 4
        base = (r // 4) * 4
        pdidx = np.zeros((128, 6), np.int32)
        for k in range(3):
            rowbase = (base + k) * 256
            pdidx[:, 2 * k] = rowbase + np.arange(128)
            pdidx[:, 2 * k + 1] = rowbase + 128 + np.arange(128)
        msel = np.zeros((128, 4), np.float32)
        msel[:, grp] = 1.0
        m = dict(shared)
        m["xT"] = np.ascontiguousarray(x[seq, t0:t0 + T, :].T)
        m["up_o"] = np.ascontiguousarray(wup[r * 384:(r + 1) * 384, :])
        m["up_t"] = np.ascontiguousarray(wup[r * 384:(r + 1) * 384, :].T)
        m["gt_o"] = np.ascontiguousarray(wgate[r * 384:(r + 1) * 384, :])
        m["gt_t"] = np.ascontiguousarray(wgate[r * 384:(r + 1) * 384, :].T)
        m["dn_o"] = np.ascontiguousarray(wdown[r * 128:(r + 1) * 128, :])
        m["dn_t"] = np.ascontiguousarray(wdown[r * 128:(r + 1) * 128, :].T)
        m["PDIDX"] = pdidx
        m["MSEL"] = msel
        in_maps.append(m)

    res = run_bass_kernel_spmd(nc, in_maps, core_ids=list(range(NCORE)))
    out = np.empty((2, 8192, 1024), np.float32)
    for r in range(NCORE):
        seq, t0 = r // 4, (r % 4) * T
        out[seq, t0:t0 + T, :] = np.asarray(res.results[r]["o_out"]).T
    if DBG:
        kernel.last_results = res.results
    return out


# version 09
# speedup vs baseline: 1.0005x; 1.0005x over previous
import numpy as np
import concourse.bass as bass
import concourse.mybir as mybir
from concourse.bass_utils import run_bass_kernel_spmd
from contextlib import ExitStack

F32 = mybir.dt.float32
BF16 = mybir.dt.bfloat16
I32 = mybir.dt.int32
AF = mybir.ActivationFunctionType
ALU = mybir.AluOpType
AX = mybir.AxisListType

NCORE = 8
T = 2048
D = 1024
C = 128
NCH = 16
ND = 8
FF = 3072
NOT = 24
QMAX = 32767.0

DBG = False
STAGE = 9


class Ctr:
    def __init__(self, sem):
        self.sem = sem
        self.n = 0

    def tick(self, inst, k=1):
        inst.then_inc(self.sem, k)
        self.n += k
        return self.n


def build_nc():
    nc = bass.Bass("TRN2", target_bir_lowering=False, debug=False)
    di = {}

    def inp(name, shape, dt=F32):
        di[name] = nc.dram_tensor(name, list(shape), dt, kind="ExternalInput").ap()
        return di[name]

    xT = inp("xT", [D, T])
    wqT = inp("wqT", [D, C]); wkT = inp("wkT", [D, C])
    wvT = inp("wvT", [D, C]); wgT = inp("wgT", [D, C])
    wabT = inp("wabT", [D, 2])
    woT = inp("woT", [C, D])
    bqc = inp("bqc", [C, 1]); bkc = inp("bkc", [C, 1])
    bvc = inp("bvc", [C, 1]); bgc = inp("bgc", [C, 1])
    babb = inp("babb", [C, 2])
    boc = inp("boc", [128, 8])
    up_o = inp("up_o", [384, D]); up_t = inp("up_t", [D, 384])
    gt_o = inp("gt_o", [384, D]); gt_t = inp("gt_t", [D, 384])
    dn_o = inp("dn_o", [C, FF]); dn_t = inp("dn_t", [FF, C])
    IDENT = inp("IDENT", [128, 128])
    CUMT = inp("CUMT", [128, 128])
    MASKT = inp("MASKT", [128, 128])
    ONES1 = inp("ONES1", [128, 128])
    IDX32 = inp("IDX32", [128, 1], I32)
    IDXDQ = inp("IDXDQ", [128, 64], I32)
    PDIDX = inp("PDIDX", [128, 6], I32)
    MSEL = inp("MSEL", [128, 4])

    o_out = nc.dram_tensor("o_out", [D, T], F32, kind="ExternalOutput").ap()
    dbg = {}
    if DBG:
        for nm, sh, dt in [("dq", [C, T], BF16), ("dkh", [C, T], BF16),
                           ("dv", [C, T], BF16), ("dg", [C, T], F32),
                           ("dla", [128, NCH], F32), ("dbc", [128, NCH], F32),
                           ("dO", [C, T], BF16), ("dxffn", [D, T], F32),
                           ("dwup", [D, 384], BF16), ("duvw", [128, 256], BF16),
                           ("dpct", [128, T], BF16), ("ddcf", [128, T], BF16),
                           ("duv2", [128, T], BF16), ("dwfu", [128, T], BF16),
                           ("dkrf", [128, T], BF16), ("dntl", [128, T], BF16),
                           ("dxtl", [128, T], BF16),
                           ("dmst", [128, 128], BF16)]:
            dbg[nm] = nc.dram_tensor(nm, sh, dt, kind="ExternalOutput").ap()

    b_su = nc.dram_tensor("b_su", [64, 384], F32)
    b_sg = nc.dram_tensor("b_sg", [64, 384], F32)
    b_sd = nc.dram_tensor("b_sd", [192, C], F32)
    b_dqu = nc.dram_tensor("b_dqu", [D, 384], BF16)
    b_dqg = nc.dram_tensor("b_dqg", [D, 384], BF16)
    b_dqd = nc.dram_tensor("b_dqd", [FF, C], BF16)
    b_wu = nc.dram_tensor("b_wu", [NCORE * D, 384], BF16)
    b_wg = nc.dram_tensor("b_wg", [NCORE * D, 384], BF16)
    b_wd = nc.dram_tensor("b_wd", [NCORE * FF, C], BF16)
    b_pd_in = nc.dram_tensor("b_pd_in", [256, 128], BF16)
    b_pd_out = nc.dram_tensor("b_pd_out", [NCORE * 256, 128], BF16)
    b_mx_in = nc.dram_tensor("b_mx_in", [1, 2], F32)
    b_mx_out = nc.dram_tensor("b_mx_out", [1, 2], F32)
    b_up = nc.dram_tensor("b_up", [FF, T], BF16)
    b_gate = nc.dram_tensor("b_gate", [FF, T], BF16)
    b_g2 = nc.dram_tensor("b_g2", [FF, T], BF16)
    b_opre = nc.dram_tensor("b_opre", [FF, T], BF16)

    with ExitStack() as stk:
        e = stk.enter_context
        GEN = e(nc.sbuf_tensor("GEN", [128, 40960], BF16))
        GENF = e(nc.sbuf_tensor("GENF", [128, 6144], F32))
        XFFN = e(nc.sbuf_tensor("XFFN", [128, 8 * T], F32))
        ROWS = e(nc.sbuf_tensor("ROWS", [128, 2048], F32))
        COLS = e(nc.sbuf_tensor("COLS", [128, 256], F32))
        CONST = e(nc.sbuf_tensor("CONST", [128, 512], F32))
        CONB = e(nc.sbuf_tensor("CONB", [128, 260], BF16))
        SM = e(nc.sbuf_tensor("SM", [128, 64], F32))
        SMI = e(nc.sbuf_tensor("SMI", [128, 8], I32))
        SMIQ = e(nc.sbuf_tensor("SMIQ", [128, 64], I32))
        STF = e(nc.sbuf_tensor("STF", [128, 2592], F32))
        STB = e(nc.sbuf_tensor("STB", [128, 4096], BF16))
        MTF = e(nc.sbuf_tensor("MTF", [128, 128], F32))
        MTB = e(nc.sbuf_tensor("MTB", [128, 1664], BF16))
        AWB = e(nc.sbuf_tensor("AWB", [128, 1040], BF16))
        PB = [e(nc.psum_tensor("pb%d" % k, [128, 512], F32)) for k in range(8)]

        STB1 = STB[:, 0:2048]
        STB2 = STB[:, 2048:4096]

        def gb(o, n):
            return GEN[:, o:o + n]
        XBF = [gb(2048 * k, 2048) for k in range(8)]
        QB = gb(16384, 2048); QA = gb(18432, 2048)
        KH = gb(20480, 2048); K2 = gb(22528, 2048)
        VB = gb(24576, 2048)
        UVB = gb(26624, 2048)
        WTN = gb(28672, 2048)
        KTT = gb(30720, 2048)
        OBF = gb(0, 2048)
        SCR = [STB[:, 1024 + 256 * k:1280 + 256 * k] for k in range(10)]
        OJ = [gb(512 * j, 512) for j in range(NOT)]

        GF = GENF[:, 0:2048]
        EIF = GENF[:, 2048:4096]
        FSC = GENF[:, 4096:6144]
        SIGF = FSC

        INT1 = GEN[:, 20480:24576].bitcast(I32)
        INT2 = GEN[:, 16384:20480].bitcast(I32)
        INT1o = GEN[:, 24576:28672].bitcast(I32)
        INT2o = GEN[:, 28672:32768].bitcast(I32)
        SIGFo = GENF[:, 0:2048]

        LA_ROW = ROWS[0:1, :]
        RTMP = GENF[0:1, 2048:4096]
        RTMP2 = ROWS[64:65, :]
        MROW = ROWS[0:1, :]
        SQR = GENF[0:1, 0:2048]
        VARR = GENF[0:1, 2048:4096]
        RSTD = GENF[0:1, 0:2048]
        MRSTD = GENF[0:1, 2048:4096]

        LAC = COLS[:, 0:16]; BCC = COLS[:, 16:32]; ACOL = COLS[:, 32:48]
        BAC = COLS[:, 48:64]; RC_ = COLS[:, 64:80]; ALC = COLS[:, 80:96]
        ZAC = COLS[:, 96:112]; ZBC = COLS[:, 112:128]
        CT1 = COLS[:, 128:144]; CT2 = COLS[:, 144:160]
        BIAS4 = COLS[:, 160:164]
        BAB = COLS[:, 164:166]
        BOC = COLS[:, 166:174]
        SCL = COLS[:, 174:206]
        NEGC = COLS[:, 206:207]
        MSL = COLS[:, 208:212]

        IDE = CONST[:, 0:128]; CUM = CONST[:, 128:256]
        MSK = CONST[:, 256:384]
        ONE = CONST[0:1, 384:512]
        ONE32 = CONST[32:33, 384:512]
        ONE64 = CONST[64:65, 384:512]
        ONE96 = CONST[96:97, 384:512]
        IDEB = CONB[:, 0:128]; ONEB = CONB[0:1, 128:256]
        ONECB = CONB[:, 256:257]

        def wq_t(d): return GEN[:, 26624 + d * 128:26624 + (d + 1) * 128]
        def wk_t(d): return GEN[:, 27648 + d * 128:27648 + (d + 1) * 128]
        def wv_t(d): return GEN[:, 28672 + d * 128:28672 + (d + 1) * 128]
        def wg_t(d): return GEN[:, 29696 + d * 128:29696 + (d + 1) * 128]
        def wo_t(h): return AWB[:, h * 128:(h + 1) * 128]
        def wab_t(d): return AWB[:, 1024 + 2 * d:1026 + 2 * d]

        MT_F = MTF[:, 0:128]
        MT_B = MTB[:, 0:128]
        M_B = MTB[:, 128:256]
        PCM = MTB[:, 256:384]
        PCMT = MTB[:, 384:512]
        DCM = MTB[:, 512:640]
        DCB = MTB[:, 640:768]
        GPD = [MTB[:, 768 + k * 128:896 + k * 128] for k in range(6)]
        MACC = MTB[:, 1536:1664]

        def pbb(k, n=128, off=0):
            return PB[k].ap().bitcast(BF16)[:, off:off + n]

        ld = Ctr(e(nc.semaphore("ld")))
        sw = Ctr(e(nc.semaphore("sw")))
        gq = Ctr(e(nc.semaphore("gq")))
        pe = Ctr(e(nc.semaphore("pe")))
        ve = Ctr(e(nc.semaphore("ve")))
        se = Ctr(e(nc.semaphore("se")))
        po = Ctr(e(nc.semaphore("po")))
        st = Ctr(e(nc.semaphore("st")))
        ldb = Ctr(e(nc.semaphore("ldb")))
        gqb = Ctr(e(nc.semaphore("gqb")))
        swb = Ctr(e(nc.semaphore("swb")))
        lp5 = Ctr(e(nc.semaphore("lp5")))
        lp5b = Ctr(e(nc.semaphore("lp5b")))
        ldw2 = Ctr(e(nc.semaphore("ldw2")))
        ldw2b = Ctr(e(nc.semaphore("ldw2b")))
        sp3 = Ctr(e(nc.semaphore("sp3")))
        sp3b = Ctr(e(nc.semaphore("sp3b")))
        cc = [Ctr(e(nc.semaphore("cc%d" % k))) for k in range(7)]
        blk = e(nc.Block())

        @blk.sync
        def _(sy):
            ld.tick(sy.dma_start(out=IDE, in_=IDENT[:, :]), 16)
            ld.tick(sy.dma_start(out=CUM, in_=CUMT[:, :]), 16)
            ld.tick(sy.dma_start(out=MSK, in_=MASKT[:, :]), 16)
            ld.tick(sy.dma_start(out=CONST[:, 384:512], in_=ONES1[:, :]), 16)
            ld.tick(sy.dma_start(out=SMI[:, 0:1], in_=IDX32[:, :]), 16)
            ld.tick(sy.dma_start(out=SMI[:, 1:7], in_=PDIDX[:, :]), 16)
            ld.tick(sy.dma_start(out=SMIQ[:, :], in_=IDXDQ[:, :]), 16)
            ld.tick(sy.dma_start(out=MSL, in_=MSEL[:, :]), 16)
            ld.tick(sy.dma_start(out=BIAS4[:, 0:1], in_=bqc[:, :]), 16)
            ld.tick(sy.dma_start(out=BIAS4[:, 1:2], in_=bkc[:, :]), 16)
            ld.tick(sy.dma_start(out=BIAS4[:, 2:3], in_=bvc[:, :]), 16)
            ld.tick(sy.dma_start(out=BIAS4[:, 3:4], in_=bgc[:, :]), 16)
            ld.tick(sy.dma_start(out=BAB, in_=babb[:, :]), 16)
            ld.tick(sy.dma_start(out=BOC, in_=boc[:, :]), 16)
            for d in range(8):
                ld.tick(sy.dma_start(out=XFFN[:, d * T:(d + 1) * T],
                                     in_=xT[d * 128:(d + 1) * 128, :]), 16)
            for d in range(8):
                ld.tick(sy.dma_start(out=GENF[:, 512 * d:512 * d + 128],
                                     in_=wqT[d * 128:(d + 1) * 128, :]), 16)
                ld.tick(sy.dma_start(out=GENF[:, 512 * d + 128:512 * d + 256],
                                     in_=wkT[d * 128:(d + 1) * 128, :]), 16)
                ld.tick(sy.dma_start(out=GENF[:, 512 * d + 256:512 * d + 384],
                                     in_=wvT[d * 128:(d + 1) * 128, :]), 16)
                ld.tick(sy.dma_start(out=GENF[:, 512 * d + 384:512 * d + 512],
                                     in_=wgT[d * 128:(d + 1) * 128, :]), 16)
            for d in range(8):
                ld.tick(sy.dma_start(out=FSC[:, 2 * d:2 * d + 2],
                                     in_=wabT[d * 128:(d + 1) * 128, :]), 16)
            for h in range(8):
                ld.tick(sy.dma_start(out=FSC[:, 16 + 128 * h:16 + 128 * (h + 1)],
                                     in_=woT[:, h * 128:(h + 1) * 128]), 16)

        ph0_ld = ld.n

        @blk.vector
        def _(v):
            v.wait_ge(ld.sem, ph0_ld)
            for d in range(8):
                v.tensor_copy(out=XBF[d], in_=XFFN[:, d * T:(d + 1) * T])
                v.tensor_copy(out=wq_t(d), in_=GENF[:, 512 * d:512 * d + 128])
                v.tensor_copy(out=wk_t(d), in_=GENF[:, 512 * d + 128:512 * d + 256])
                v.tensor_copy(out=wv_t(d), in_=GENF[:, 512 * d + 256:512 * d + 384])
                v.tensor_copy(out=wg_t(d), in_=GENF[:, 512 * d + 384:512 * d + 512])
                v.tensor_copy(out=wab_t(d), in_=FSC[:, 2 * d:2 * d + 2])
            for h in range(8):
                v.tensor_copy(out=wo_t(h), in_=FSC[:, 16 + 128 * h:16 + 128 * (h + 1)])
            v.tensor_copy(out=IDEB, in_=IDE)
            v.tensor_copy(out=ONEB, in_=ONE)
            v.memset(ONECB, 1.0)
            v.memset(NEGC, -1.0)
            last = v.memset(SM[:, 0:2], 0.0)
            ve.tick(last, 1)

        ph0_ve = ve.n

        dq_cfg = [(up_o, up_t, b_su, b_dqu, 3, 8, 384, 1024, 32, 0, 8),
                  (gt_o, gt_t, b_sg, b_dqg, 3, 8, 384, 1024, 32, 0, 8),
                  (dn_o, dn_t, b_sd, b_dqd, 1, 24, 128, 3072, 96, 16, 40)]

        for (sh_o, sh_t, bscl, bdq, n_ot, n_it, NO, NI, NB, ixs, ixi) in dq_cfg:
            segw = NI if NI <= 1536 else 1536
            nseg = NI // segw
            nbs = NB // nseg
            for ot3 in range(n_ot):
                for seg in range(nseg):
                    @blk.sync
                    def _(sy, sh_o=sh_o, ot3=ot3, seg=seg, segw=segw):
                        sy.wait_ge(ve.sem, ve.n)
                        ld.tick(sy.dma_start(
                            out=STF[:, 0:segw],
                            in_=sh_o[ot3 * 128:(ot3 + 1) * 128,
                                     seg * segw:(seg + 1) * segw]), 16)
                    need_ld = ld.n

                    @blk.vector
                    def _(v, seg=seg, segw=segw, nbs=nbs, need_ld=need_ld):
                        v.wait_ge(ld.sem, need_ld)
                        v.wait_ge(se.sem, se.n)
                        ve.tick(v.tensor_reduce(
                            out=STF[:, 1600 + seg * nbs:1600 + (seg + 1) * nbs]
                                .rearrange("p (n w) -> p n w", w=1),
                            in_=STF[:, 0:segw].rearrange("p (n w) -> p n w", w=32),
                            axis=AX.X, op=ALU.max, apply_absolute_value=True), 1)

                @blk.vector
                def _(v, NB=NB):
                    v.wait_ge(ve.sem, ve.n)
                    v.tensor_scalar(out=STF[:, 1696:1696 + NB], in0=STF[:, 1600:1600 + NB],
                                    scalar1=1.0 / 7.0, scalar2=1e-12,
                                    op0=ALU.mult, op1=ALU.max)
                    ve.tick(v.engine_nop(), 1)
                    v.wait_ge(ve.sem, ve.n)
                    ve.tick(v.reciprocal(out=STF[:, 1792:1792 + NB],
                                         in_=STF[:, 1696:1696 + NB]), 1)
                need_ve = ve.n

                @blk.tensor
                def _(t, NB=NB, need_ve=need_ve):
                    t.wait_ge(ve.sem, need_ve)
                    t.wait_ge(se.sem, se.n)
                    t.transpose(PB[0][0:NB, 0:128], STF[:, 1696:1696 + NB], IDE)
                    pe.tick(t.transpose(PB[1][0:NB, 0:128], STF[:, 1792:1792 + NB], IDE), 1)
                need_pe = pe.n

                @blk.scalar
                def _(s, NB=NB, need_pe=need_pe):
                    s.wait_ge(pe.sem, need_pe)
                    s.wait_ge(sw.sem, sw.n)
                    s.copy(out=STF[0:NB, 1888:2016], in_=PB[0][0:NB, 0:128])
                    se.tick(s.copy(out=STF[0:NB, 2016:2144], in_=PB[1][0:NB, 0:128]), 1)
                need_se = se.n

                @blk.scalar
                def _(s, bscl=bscl, ot3=ot3, NB=NB, need_se=need_se):
                    s.wait_ge(se.sem, need_se)
                    sw.tick(s.dma_start(out=bscl[0:NB, ot3 * 128:(ot3 + 1) * 128],
                                        in_=STF[0:NB, 1888:2016]), 16)
                    sw.tick(s.dma_start(out=bscl[NB:2 * NB, ot3 * 128:(ot3 + 1) * 128],
                                        in_=STF[0:NB, 2016:2144]), 16)

            dqv = {}
            dqw = {}
            base_sw = sw.n
            for it in range(n_it):
                par = it % 2
                sl0, sl1, sl2 = (0, 512, 1024) if par == 0 else (1408, 1792, 2176)
                stb_off = 0 if par == 0 else 512
                ints = INT1 if par == 0 else INT2
                LDP = ld if par == 0 else ldb
                GQP = gq if par == 0 else gqb
                SWP = sw if par == 0 else swb

                @blk.sync
                def _(sy, sh_t=sh_t, it=it, NO=NO, sl0=sl0, LDP=LDP):
                    if it < 2:
                        sy.wait_ge(ve.sem, ve.n)
                        sy.wait_ge(sw.sem, sw.n)
                        sy.wait_ge(swb.sem, swb.n)
                    else:
                        sy.wait_ge(ve.sem, dqv[it - 2])
                    LDP.tick(sy.dma_start(out=STF[:, sl0:sl0 + NO],
                                          in_=sh_t[it * 128:(it + 1) * 128, :]), 16)
                need_ld = LDP.n

                @blk.gpsimd
                def _(g, bscl=bscl, it=it, NO=NO, NB=NB, ixs=ixs, ixi=ixi,
                      sl1=sl1, sl2=sl2, base_sw=base_sw, GQP=GQP):
                    if it < 2:
                        g.wait_ge(sw.sem, base_sw)
                    else:
                        g.wait_ge(ve.sem, dqv[it - 2])
                    GQP.tick(g.indirect_dma_start(
                        out=STF[:, sl1:sl1 + NO], out_offset=None,
                        in_=bscl.ap(),
                        in_offset=bass.IndirectOffsetOnAxis(
                            ap=SMIQ[:, ixs + it:ixs + it + 1], axis=0)), 16)
                    GQP.tick(g.indirect_dma_start(
                        out=STF[:, sl2:sl2 + NO], out_offset=None,
                        in_=bscl.ap(),
                        in_offset=bass.IndirectOffsetOnAxis(
                            ap=SMIQ[:, ixi + it:ixi + it + 1], axis=0)), 16)
                need_gq = GQP.n

                @blk.vector
                def _(v, it=it, NO=NO, sl0=sl0, sl1=sl1, sl2=sl2, stb_off=stb_off,
                      ints=ints, need_ld=need_ld, need_gq=need_gq, LDP=LDP, GQP=GQP,
                      SWP2=(sw if it % 2 == 0 else swb)):
                    v.wait_ge(LDP.sem, need_ld)
                    v.wait_ge(GQP.sem, need_gq)
                    if it >= 2:
                        v.wait_ge(SWP2.sem, dqw[it - 2])
                    v.tensor_tensor(out=ints[:, 0:NO], in0=STF[:, sl0:sl0 + NO],
                                    in1=STF[:, sl2:sl2 + NO], op=ALU.mult)
                    ve.tick(v.engine_nop(), 1)
                    v.wait_ge(ve.sem, ve.n)
                    ve.tick(v.tensor_tensor(out=STB1[:, stb_off:stb_off + NO],
                                            in0=ints[:, 0:NO],
                                            in1=STF[:, sl1:sl1 + NO],
                                            op=ALU.mult), 1)
                dqv[it] = ve.n
                need_ve = ve.n

                @blk.scalar
                def _(s, bdq=bdq, it=it, NO=NO, stb_off=stb_off, need_ve=need_ve,
                      SWP=SWP):
                    s.wait_ge(ve.sem, need_ve)
                    SWP.tick(s.dma_start(out=bdq[it * 128:(it + 1) * 128, :],
                                         in_=STB1[:, stb_off:stb_off + NO]), 16)
                dqw[it] = SWP.n

        @blk.gpsimd
        def _(g):
            g.wait_ge(sw.sem, sw.n)
            g.wait_ge(swb.sem, swb.n)
            cc[0].tick(g.collective_compute(
                "AllGather", ALU.bypass, replica_groups=[list(range(NCORE))],
                ins=[b_dqu.ap()], outs=[b_wu.ap()]), 1)
            cc[0].tick(g.collective_compute(
                "AllGather", ALU.bypass, replica_groups=[list(range(NCORE))],
                ins=[b_dqg.ap()], outs=[b_wg.ap()]), 1)
            cc[0].tick(g.collective_compute(
                "AllGather", ALU.bypass, replica_groups=[list(range(NCORE))],
                ins=[b_dqd.ap()], outs=[b_wd.ap()]), 1)

        if DBG:
            @blk.sync
            def _(sy):
                sy.wait_ge(sw.sem, sw.n)
                sy.wait_ge(swb.sem, swb.n)
                st.tick(sy.dma_start(out=dbg["dwup"][:, :], in_=b_dqu.ap()[:, :]), 16)

        if STAGE >= 1:
            @blk.tensor
            def _(t):
                t.wait_ge(ve.sem, ph0_ve)
                t.wait_ge(se.sem, se.n)
                last = None
                for c_ in range(16):
                    for d in range(8):
                        last = t.matmul(PB[4][:, 2 * c_:2 * c_ + 2],
                                        XBF[d][:, c_ * 128:(c_ + 1) * 128], wab_t(d),
                                        start=(d == 0), stop=(d == 7))
                pe.tick(last, 1)
            pe_zab = pe.n

            @blk.vector
            def _(v):
                v.wait_ge(pe.sem, pe_zab)
                v.tensor_copy(out=ZAC,
                              in_=PB[4].ap()[:, 0:32].rearrange("p (c t) -> p t c", t=2)[:, 0, :])
                ve.tick(v.tensor_copy(
                    out=ZBC,
                    in_=PB[4].ap()[:, 0:32].rearrange("p (c t) -> p t c", t=2)[:, 1, :]), 1)
            ve_zab = ve.n

            for tc in range(4):
                @blk.tensor
                def _(t, tc=tc):
                    t.wait_ge(ve.sem, ph0_ve)
                    t.wait_ge(se.sem, se.n)
                    last = None
                    for d in range(8):
                        last = t.matmul(PB[5][0:1, 0:512], wab_t(d)[:, 1:2],
                                        XBF[d][:, tc * 512:(tc + 1) * 512],
                                        start=(d == 0), stop=(d == 7))
                    pe.tick(last, 1)
                need_pe = pe.n

                @blk.scalar
                def _(s, tc=tc, need_pe=need_pe):
                    s.wait_ge(pe.sem, need_pe)
                    se.tick(s.activation(out=RTMP[:, tc * 512:(tc + 1) * 512],
                                         in_=PB[5][0:1, 0:512], func=AF.Sigmoid,
                                         bias=BAB[0:1, 1:2]), 1)

            @blk.scalar
            def _(s):
                s.wait_ge(ve.sem, ve_zab)
                se.tick(s.activation(out=CT1, in_=ZAC, func=AF.Abs, bias=BAB[:, 0:1]), 1)
                s.wait_ge(se.sem, se.n)
                se.tick(s.activation(out=CT2, in_=CT1, func=AF.Exp, scale=NEGC), 1)
            se_c1 = se.n

            @blk.vector
            def _(v):
                v.wait_ge(se.sem, se_c1)
                ve.tick(v.tensor_scalar_add(out=CT2, in0=CT2, scalar1=1.0), 1)
            ve_c1 = ve.n

            @blk.scalar
            def _(s):
                s.wait_ge(ve.sem, ve_c1)
                se.tick(s.activation(out=CT2, in_=CT2, func=AF.Ln), 1)
            se_c2 = se.n

            @blk.vector
            def _(v):
                v.wait_ge(se.sem, se_c2)
                v.tensor_scalar(out=CT1, in0=ZAC, scalar1=BAB[:, 0:1], scalar2=0.0,
                                op0=ALU.add, op1=ALU.min)
                ve.tick(v.engine_nop(), 1)
                v.wait_ge(ve.sem, ve.n)
                ve.tick(v.tensor_sub(out=CT1, in0=CT1, in1=CT2), 1)
            ve_la = ve.n

            @blk.tensor
            def _(t):
                t.wait_ge(ve.sem, ve_la)
                pe.tick(t.matmul(PB[6][:, 0:16], CUM, CT1, start=True, stop=True), 1)
            pe_cum = pe.n

            @blk.vector
            def _(v):
                v.wait_ge(pe.sem, pe_cum)
                ve.tick(v.tensor_copy(out=LAC, in_=PB[6][:, 0:16]), 1)
            ve_lac = ve.n

            @blk.scalar
            def _(s):
                s.wait_ge(ve.sem, ve_zab)
                se.tick(s.activation(out=BCC, in_=ZBC, func=AF.Sigmoid, bias=BAB[:, 1:2]), 1)
                s.wait_ge(ve.sem, ve_lac)
                se.tick(s.activation(out=ACOL, in_=LAC, func=AF.Exp), 1)
            se_bc = se.n

            @blk.vector
            def _(v):
                v.wait_ge(se.sem, se_bc)
                ve.tick(v.tensor_mul(out=BAC, in0=BCC, in1=ACOL), 1)

            @blk.gpsimd
            def _(g):
                g.wait_ge(ve.sem, ve_lac)
                gq.tick(g.dma_start(out=SM[0:1, 16:32], in_=LAC[127:128, :]), 16)
            gq_lal = gq.n

            @blk.tensor
            def _(t):
                t.wait_ge(gq.sem, gq_lal)
                t.wait_ge(ve.sem, ve.n)
                pe.tick(t.matmul(PB[6][:, 16:32], ONE, SM[0:1, 16:32],
                                 start=True, stop=True), 1)
            pe_lal = pe.n

            @blk.vector
            def _(v):
                v.wait_ge(pe.sem, pe_lal)
                ve.tick(v.tensor_sub(out=CT1, in0=PB[6][:, 16:32], in1=LAC), 1)
            ve_rc = ve.n

            @blk.scalar
            def _(s):
                s.wait_ge(ve.sem, ve_rc)
                se.tick(s.activation(out=RC_, in_=CT1, func=AF.Exp), 1)
                se.tick(s.activation(out=ALC, in_=PB[6][:, 16:32], func=AF.Exp), 1)
            se_rc = se.n

            for c_ in range(16):
                @blk.tensor
                def _(t, c_=c_):
                    t.wait_ge(ve.sem, ve.n)
                    if c_ == 0:
                        t.wait_ge(ve.sem, ve_lac)
                    pe.tick(t.matmul(PB[7][0:1, 0:128], LAC[:, c_:c_ + 1], IDE,
                                     start=True, stop=True), 1)
                need_pe = pe.n

                @blk.vector
                def _(v, c_=c_, need_pe=need_pe):
                    v.wait_ge(pe.sem, need_pe)
                    ve.tick(v.tensor_copy(out=LA_ROW[:, c_ * 128:(c_ + 1) * 128],
                                          in_=PB[7][0:1, 0:128]), 1)
            ve_larow = ve.n

            ve_k2d = {}
            se_sqd = {}
            for tc in range(4):
                sl = slice(tc * 512, (tc + 1) * 512)

                @blk.tensor
                def _(t, tc=tc, sl=sl, ve_now=ve.n, se_now=se.n):
                    if tc == 0:
                        t.wait_ge(ve.sem, ve_now)
                        t.wait_ge(se.sem, se_now)
                    else:
                        t.wait_ge(ve.sem, ve_k2d[tc - 1])
                        t.wait_ge(se.sem, se_sqd[tc - 1])
                    last = None
                    for d in range(8):
                        st_ = (d == 0); sp = (d == 7)
                        t.matmul(PB[0][:, 0:512], wq_t(d), XBF[d][:, sl], start=st_, stop=sp)
                        t.matmul(PB[1][:, 0:512], wk_t(d), XBF[d][:, sl], start=st_, stop=sp)
                        t.matmul(PB[2][:, 0:512], wv_t(d), XBF[d][:, sl], start=st_, stop=sp)
                        last = t.matmul(PB[3][:, 0:512], wg_t(d), XBF[d][:, sl],
                                        start=st_, stop=sp)
                    pe.tick(last, 1)
                pe_proj = pe.n

                @blk.vector
                def _(v, sl=sl, pe_proj=pe_proj):
                    v.wait_ge(pe.sem, pe_proj)
                    v.tensor_scalar(out=QB[:, sl], in0=PB[0][:, 0:512],
                                    scalar1=BIAS4[:, 0:1], scalar2=None, op0=ALU.add)
                    v.tensor_scalar(out=VB[:, sl], in0=PB[2][:, 0:512],
                                    scalar1=BIAS4[:, 2:3], scalar2=None, op0=ALU.add)
                    ve.tick(v.tensor_scalar(out=STF[:, 0:512], in0=PB[1][:, 0:512],
                                            scalar1=BIAS4[:, 1:2], scalar2=None,
                                            op0=ALU.add), 1)
                ve_kf = ve.n

                @blk.scalar
                def _(s, sl=sl, pe_proj=pe_proj, ve_kf=ve_kf):
                    s.wait_ge(pe.sem, pe_proj)
                    se.tick(s.activation(out=GF[:, sl], in_=PB[3][:, 0:512],
                                         func=AF.Sigmoid, bias=BIAS4[:, 3:4]), 1)
                    s.wait_ge(ve.sem, ve_kf)
                    se.tick(s.activation(out=STB1[:, 0:512], in_=STF[:, 0:512],
                                         func=AF.Square), 1)
                se_sq = se.n
                se_sqd[tc] = se.n

                @blk.tensor
                def _(t, se_sq=se_sq):
                    t.wait_ge(se.sem, se_sq)
                    pe.tick(t.matmul(PB[5][0:1, 0:512], ONECB, STB1[:, 0:512],
                                     start=True, stop=True), 1)
                pe_nrm = pe.n

                @blk.scalar
                def _(s, sl=sl, pe_nrm=pe_nrm):
                    s.wait_ge(pe.sem, pe_nrm)
                    se.tick(s.activation(out=RTMP2[:, sl], in_=PB[5][0:1, 0:512],
                                         func=AF.Sqrt), 1)
                se_nrm = se.n

                @blk.vector
                def _(v, sl=sl, se_nrm=se_nrm):
                    v.wait_ge(se.sem, se_nrm)
                    ve.tick(v.tensor_scalar_max(out=RTMP2[:, sl], in0=RTMP2[:, sl],
                                                scalar1=1e-12), 1)
                    v.wait_ge(ve.sem, ve.n)
                    ve.tick(v.reciprocal(out=RTMP2[:, sl], in_=RTMP2[:, sl]), 1)
                ve_rn = ve.n

                @blk.tensor
                def _(t, sl=sl, ve_rn=ve_rn):
                    t.wait_ge(ve.sem, ve_rn)
                    t.matmul(PB[6][:, 0:512], ONE64, RTMP2[:, sl], start=True, stop=True)
                    pe.tick(t.matmul(PB[7][:, 0:512], ONE, RTMP[:, sl],
                                     start=True, stop=True), 1)
                pe_bc = pe.n

                @blk.vector
                def _(v, sl=sl, pe_bc=pe_bc):
                    v.wait_ge(pe.sem, pe_bc)
                    ve.tick(v.tensor_mul(out=KH[:, sl], in0=STF[:, 0:512],
                                         in1=PB[6][:, 0:512]), 1)
                    v.wait_ge(ve.sem, ve.n)
                    ve.tick(v.tensor_mul(out=K2[:, sl], in0=KH[:, sl],
                                         in1=PB[7][:, 0:512]), 1)
                ve_k2d[tc] = ve.n

            @blk.scalar
            def _(s):
                s.wait_ge(ve.sem, ve.n)
                se.tick(s.activation(out=RTMP, in_=LA_ROW, func=AF.Exp), 1)
            se_ar = se.n

            for tc in range(4):
                sl = slice(tc * 512, (tc + 1) * 512)

                @blk.tensor
                def _(t, sl=sl, se_ar=se_ar):
                    t.wait_ge(se.sem, se_ar)
                    t.wait_ge(ve.sem, ve.n)
                    pe.tick(t.matmul(PB[6][:, 0:512], ONE, RTMP[:, sl],
                                     start=True, stop=True), 1)
                pe_ar = pe.n

                @blk.vector
                def _(v, sl=sl, pe_ar=pe_ar):
                    v.wait_ge(pe.sem, pe_ar)
                    ve.tick(v.tensor_mul(out=QA[:, sl], in0=QB[:, sl],
                                         in1=PB[6][:, 0:512]), 1)

            if DBG:
                @blk.sync
                def _(sy):
                    sy.wait_ge(ve.sem, ve.n)
                    sy.wait_ge(se.sem, se.n)
                    st.tick(sy.dma_start(out=dbg["dq"][:, :], in_=QB), 16)
                    st.tick(sy.dma_start(out=dbg["dkh"][:, :], in_=KH), 16)
                    st.tick(sy.dma_start(out=dbg["dv"][:, :], in_=VB), 16)
                    st.tick(sy.dma_start(out=dbg["dg"][:, :], in_=GF), 16)
                    st.tick(sy.dma_start(out=dbg["dla"][:, :], in_=LAC), 16)
                    st.tick(sy.dma_start(out=dbg["dbc"][:, :], in_=BCC), 16)

        NTIL = GEN[:, 2048:4096]
        XTIL = GEN[:, 4096:6144]
        PPA = GEN[:, 6144:8192]
        TTIL = GEN[:, 8192:10240]
        PPB = GEN[:, 10240:12288]
        RHSF = GEN[:, 12288:16384]
        KRF = STB[:, 1024:3072]
        WFU = GEN[:, 0:2048]
        PCTF = GEN[:, 2048:4096]
        DCF = GEN[:, 4096:6144]
        MBC = GEN[:, 6144:8192]
        MTC = GEN[:, 8192:10240]
        UFU = GEN[:, 10240:12288]
        URF = GEN[:, 12288:14336]
        UTF = GEN[:, 14336:16384]
        PTF = GEN[:, 2048:4096]

        def c128(buf, c_):
            return buf[:, c_ * 128:(c_ + 1) * 128]

        if STAGE >= 2:
            @blk.tensor
            def _(t):
                t.wait_ge(ve.sem, ve.n)
                t.wait_ge(se.sem, se.n)
                last = None
                for c_ in range(NCH):
                    bk, off = c_ % 4, (c_ // 4) * 128
                    t.matmul(PB[bk][:, off:off + 128], KH[:, c_ * 128:(c_ + 1) * 128],
                             K2[:, c_ * 128:(c_ + 1) * 128], start=True, stop=True)
                    last = t.matmul(PB[4 + bk][:, off:off + 128], ONE,
                                    LA_ROW[:, c_ * 128:(c_ + 1) * 128],
                                    start=True, stop=True)
                pe.tick(last, 1)
            p2a = pe.n

            @blk.vector
            def _(v):
                v.wait_ge(pe.sem, p2a)
                last = None
                for c_ in range(NCH):
                    bk, off = c_ % 4, (c_ // 4) * 128
                    last = v.scalar_tensor_tensor(out=c128(FSC, c_),
                                                  in0=PB[4 + bk][:, off:off + 128],
                                                  scalar=LAC[:, c_:c_ + 1], in1=MSK,
                                                  op0=ALU.subtract, op1=ALU.add)
                ve.tick(last, 1)
            v2a = ve.n

            @blk.scalar
            def _(s):
                s.wait_ge(ve.sem, v2a)
                se.tick(s.activation(out=FSC, in_=FSC, func=AF.Exp), 1)
            s2a = se.n

            @blk.vector
            def _(v):
                v.wait_ge(se.sem, s2a)
                last = None
                for c_ in range(NCH):
                    last = v.tensor_add(out=c128(EIF, c_), in0=c128(FSC, c_), in1=IDE)
                for c_ in range(NCH):
                    bk, off = c_ % 4, (c_ // 4) * 128
                    last = v.tensor_mul(out=c128(NTIL, c_), in0=c128(FSC, c_),
                                        in1=PB[bk][:, off:off + 128])
                ve.tick(last, 1)
                v.wait_ge(ve.sem, ve.n)
                last = None
                for c_ in range(NCH):
                    last = v.tensor_sub(out=c128(XTIL, c_), in0=IDEB, in1=c128(NTIL, c_))
                ve.tick(last, 1)
            v2b = ve.n

            @blk.tensor
            def _(t):
                t.wait_ge(ve.sem, v2b)
                last = None
                for c_ in range(NCH):
                    bk, off = c_ % 2, (c_ // 2) * 128
                    last = t.transpose(pbb(bk, 128, off), c128(NTIL, c_), IDEB)
                pe.tick(last, 1)
            p2b = pe.n

            @blk.scalar
            def _(s):
                s.wait_ge(pe.sem, p2b)
                last = None
                for c_ in range(NCH):
                    bk, off = c_ % 2, (c_ // 2) * 128
                    last = s.copy(out=c128(TTIL, c_), in_=pbb(bk, 128, off))
                se.tick(last, 1)
            s2b = se.n

            pcur, pnxt = NTIL, PPA
            prev_apply_v = v2b
            prev_t_evac = s2b
            for j in range(1, 7):
                @blk.tensor
                def _(t, pcur=pcur, prev_t_evac=prev_t_evac, prev_apply_v=prev_apply_v):
                    t.wait_ge(se.sem, prev_t_evac)
                    t.wait_ge(ve.sem, prev_apply_v)
                    last = None
                    for c_ in range(NCH):
                        bk, off = 2 + c_ % 4, (c_ // 4) * 128
                        last = t.matmul(PB[bk][:, off:off + 128], c128(TTIL, c_),
                                        c128(pcur, c_), start=True, stop=True)
                    pe.tick(last, 1)
                ppw = pe.n

                @blk.scalar
                def _(s, pnxt=pnxt, ppw=ppw):
                    s.wait_ge(pe.sem, ppw)
                    last = None
                    for c_ in range(NCH):
                        bk, off = 2 + c_ % 4, (c_ // 4) * 128
                        last = s.copy(out=c128(pnxt, c_), in_=PB[bk][:, off:off + 128])
                    se.tick(last, 1)
                spw = se.n

                @blk.tensor
                def _(t, pnxt=pnxt, spw=spw):
                    t.wait_ge(se.sem, spw)
                    last = None
                    for c_ in range(NCH):
                        bk, off = c_ % 2, (c_ // 2) * 128
                        last = t.transpose(pbb(bk, 128, off), c128(pnxt, c_), IDEB)
                    pe.tick(last, 1)
                ptr = pe.n

                @blk.scalar
                def _(s, ptr=ptr):
                    s.wait_ge(pe.sem, ptr)
                    last = None
                    for c_ in range(NCH):
                        bk, off = c_ % 2, (c_ // 2) * 128
                        last = s.copy(out=c128(TTIL, c_), in_=pbb(bk, 128, off))
                    se.tick(last, 1)
                str_ = se.n

                @blk.tensor
                def _(t, str_=str_):
                    t.wait_ge(se.sem, str_)
                    last = None
                    for c_ in range(NCH):
                        bk, off = 2 + c_ % 4, (c_ // 4) * 128
                        t.matmul(PB[bk][:, off:off + 128], IDEB, c128(XTIL, c_),
                                 start=True, stop=False)
                        last = t.matmul(PB[bk][:, off:off + 128], c128(TTIL, c_),
                                        c128(XTIL, c_), start=False, stop=True)
                    pe.tick(last, 1)
                pap = pe.n

                @blk.vector
                def _(v, pap=pap):
                    v.wait_ge(pe.sem, pap)
                    last = None
                    for c_ in range(NCH):
                        bk, off = 2 + c_ % 4, (c_ // 4) * 128
                        last = v.tensor_copy(out=c128(XTIL, c_), in_=PB[bk][:, off:off + 128])
                    ve.tick(last, 1)
                prev_apply_v = ve.n
                prev_t_evac = str_
                pcur, pnxt = pnxt, (PPB if pnxt is PPA else PPA)

            @blk.tensor
            def _(t):
                t.wait_ge(ve.sem, prev_apply_v)
                last = None
                for c_ in range(NCH):
                    bk, off = c_ % 2, (c_ // 2) * 128
                    t.transpose(pbb(bk, 128, off), VB[:, c_ * 128:(c_ + 1) * 128], IDEB)
                    last = t.transpose(pbb(2 + bk, 128, off), KH[:, c_ * 128:(c_ + 1) * 128],
                                       IDEB)
                pe.tick(last, 1)
            p2c = pe.n

            @blk.scalar
            def _(s):
                s.wait_ge(pe.sem, p2c)
                last = None
                for c_ in range(NCH):
                    bk, off = c_ % 2, (c_ // 2) * 128
                    last = s.copy(out=c128(KTT, c_), in_=pbb(2 + bk, 128, off))
                se.tick(last, 1)
            s2c = se.n

            @blk.vector
            def _(v):
                v.wait_ge(pe.sem, p2c)
                v.wait_ge(se.sem, s2c)
                last = None
                for c_ in range(NCH):
                    bk, off = c_ % 2, (c_ // 2) * 128
                    v.tensor_scalar(out=RHSF[:, c_ * 256:c_ * 256 + 128],
                                    in0=pbb(bk, 128, off), scalar1=BCC[:, c_:c_ + 1],
                                    scalar2=None, op0=ALU.mult)
                    v.tensor_scalar(out=RHSF[:, c_ * 256 + 128:c_ * 256 + 256],
                                    in0=pbb(2 + bk, 128, off), scalar1=BAC[:, c_:c_ + 1],
                                    scalar2=None, op0=ALU.mult)
                    last = v.tensor_scalar(out=c128(KRF, c_), in0=pbb(2 + bk, 128, off),
                                           scalar1=RC_[:, c_:c_ + 1], scalar2=None,
                                           op0=ALU.mult)
                ve.tick(last, 1)
            v2c = ve.n

            @blk.tensor
            def _(t):
                t.wait_ge(ve.sem, v2c)
                t.wait_ge(se.sem, s2c)
                last = None
                for c_ in range(NCH):
                    bk, off = c_ % 8, (c_ // 8) * 256
                    last = t.matmul(PB[bk][:, off:off + 256], c128(XTIL, c_),
                                    RHSF[:, c_ * 256:(c_ + 1) * 256],
                                    start=True, stop=True)
                pe.tick(last, 1)
            p2d = pe.n

            @blk.scalar
            def _(s):
                s.wait_ge(pe.sem, p2d)
                last = None
                for c_ in range(NCH):
                    bk, off = c_ % 8, (c_ // 8) * 256
                    last = s.copy(out=c128(UVB, c_), in_=PB[bk][:, off:off + 128])
                se.tick(last, 1)
            s2d = se.n

            @blk.vector
            def _(v):
                v.wait_ge(pe.sem, p2d)
                v.wait_ge(se.sem, s2d)
                last = None
                for c_ in range(NCH):
                    bk, off = c_ % 8, (c_ // 8) * 256
                    last = v.tensor_scalar(out=c128(WFU, c_),
                                           in0=PB[bk][:, off + 128:off + 256],
                                           scalar1=-1.0, scalar2=None, op0=ALU.mult)
                ve.tick(last, 1)
            v2d = ve.n

            @blk.tensor
            def _(t):
                t.wait_ge(ve.sem, v2d)
                t.wait_ge(se.sem, s2d)
                last = None
                for c_ in range(NCH):
                    bk, off = c_ % 2, (c_ // 2) * 128
                    last = t.transpose(pbb(bk, 128, off), c128(WFU, c_), IDEB)
                pe.tick(last, 1)
            p2e = pe.n

            @blk.scalar
            def _(s):
                s.wait_ge(pe.sem, p2e)
                last = None
                for c_ in range(NCH):
                    bk, off = c_ % 2, (c_ // 2) * 128
                    last = s.copy(out=c128(WTN, c_), in_=pbb(bk, 128, off))
                se.tick(last, 1)
            s2e = se.n

            @blk.tensor
            def _(t):
                t.wait_ge(se.sem, s2e)
                last = None
                for c_ in range(NCH):
                    bk, off = 2 + c_ % 4, (c_ // 4) * 128
                    last = t.matmul(PB[bk][:, off:off + 128], c128(WFU, c_),
                                    c128(KRF, c_), start=True, stop=True)
                pe.tick(last, 1)
            p2f = pe.n

            @blk.vector
            def _(v):
                v.wait_ge(pe.sem, p2f)
                last = None
                for c_ in range(NCH):
                    bk, off = 2 + c_ % 4, (c_ // 4) * 128
                    last = v.scalar_tensor_tensor(out=c128(PCTF, c_), in0=IDE,
                                                  scalar=ALC[:, c_:c_ + 1],
                                                  in1=PB[bk][:, off:off + 128],
                                                  op0=ALU.mult, op1=ALU.add)
                ve.tick(last, 1)
            v2f = ve.n

            @blk.tensor
            def _(t):
                t.wait_ge(ve.sem, v2f)
                last = None
                for c_ in range(NCH):
                    bk, off = 2 + c_ % 4, (c_ // 4) * 128
                    last = t.matmul(PB[bk][:, off:off + 128], c128(KRF, c_),
                                    c128(UVB, c_), start=True, stop=True)
                pe.tick(last, 1)
            p2g = pe.n

            @blk.scalar
            def _(s):
                s.wait_ge(pe.sem, p2g)
                last = None
                for c_ in range(NCH):
                    bk, off = 2 + c_ % 4, (c_ // 4) * 128
                    last = s.copy(out=c128(DCF, c_), in_=PB[bk][:, off:off + 128])
                se.tick(last, 1)
            s2g = se.n

            vcomp = max(v2f, 0)
            scomp = s2g
            for c_ in range(NCH):
                if c_ == 0:
                    @blk.tensor
                    def _(t, scomp=scomp):
                        t.wait_ge(ve.sem, v2f)
                        t.wait_ge(se.sem, scomp)
                        pe.tick(t.transpose(pbb(6, 128, 768), c128(PCTF, 0), IDEB), 1)
                    pc0 = pe.n

                    @blk.vector
                    def _(v, pc0=pc0):
                        v.wait_ge(pe.sem, pc0)
                        v.tensor_copy(out=PCM, in_=pbb(6, 128, 768))
                        v.tensor_copy(out=PCMT, in_=c128(PCTF, 0))
                        ve.tick(v.tensor_copy(out=DCM, in_=c128(DCF, 0)), 1)
                    vcomp = ve.n
                else:
                    @blk.tensor
                    def _(t, c_=c_, vcomp=vcomp):
                        t.wait_ge(ve.sem, vcomp)
                        t.matmul(PB[6][:, 384:512], c128(PCTF, c_), PCM,
                                 start=True, stop=True)
                        t.matmul(PB[7][:, 256:384], PCM, c128(PCTF, c_),
                                 start=True, stop=True)
                        t.matmul(PB[7][:, 384:512], c128(PCTF, c_), DCM,
                                 start=True, stop=False)
                        pe.tick(t.matmul(PB[7][:, 384:512], IDEB, c128(DCF, c_),
                                         start=False, stop=True), 1)
                    pcmp = pe.n

                    @blk.vector
                    def _(v, pcmp=pcmp):
                        v.wait_ge(pe.sem, pcmp)
                        v.tensor_copy(out=PCM, in_=PB[6][:, 384:512])
                        v.tensor_copy(out=PCMT, in_=PB[7][:, 256:384])
                        ve.tick(v.tensor_copy(out=DCM, in_=PB[7][:, 384:512]), 1)
                    vcomp = ve.n

        if STAGE >= 3:
            @blk.scalar
            def _(s):
                s.wait_ge(ve.sem, ve.n)
                sw.tick(s.dma_start(out=b_pd_in.ap()[0:128, :], in_=PCMT), 16)
                sw.tick(s.dma_start(out=b_pd_in.ap()[128:256, :], in_=DCM), 16)
            sw_pd = sw.n

            @blk.gpsimd
            def _(g):
                g.wait_ge(sw.sem, sw_pd)
                cc[1].tick(g.collective_compute(
                    "AllGather", ALU.bypass, replica_groups=[list(range(NCORE))],
                    ins=[b_pd_in.ap()], outs=[b_pd_out.ap()]), 1)
                g.wait_ge(cc[1].sem, 1)
                for k in range(6):
                    gq.tick(g.indirect_dma_start(
                        out=GPD[k], out_offset=None,
                        in_=b_pd_out.ap()[:, :],
                        in_offset=bass.IndirectOffsetOnAxis(ap=SMI[:, 1 + k:2 + k], axis=0)), 16)
            gq_pd = gq.n

            @blk.vector
            def _(v):
                v.wait_ge(gq.sem, gq_pd)
                v.memset(MACC, 0.0)
                ve.tick(v.memset(FSC[:, 0:128], 0.0), 1)
            v6 = ve.n

            for jstep in range(3):
                @blk.tensor
                def _(t, jstep=jstep, v6=v6):
                    t.wait_ge(ve.sem, v6)
                    t.matmul(PB[0][:, 0:128], GPD[2 * jstep], MACC, start=True, stop=False)
                    pe.tick(t.matmul(PB[0][:, 0:128], IDEB, GPD[2 * jstep + 1],
                                     start=False, stop=True), 1)
                p11 = pe.n

                @blk.vector
                def _(v, jstep=jstep, p11=p11):
                    v.wait_ge(pe.sem, p11)
                    ve.tick(v.tensor_copy(out=MACC, in_=PB[0][:, 0:128]), 1)
                    v.wait_ge(ve.sem, ve.n)
                    ve.tick(v.scalar_tensor_tensor(out=FSC[:, 0:128], in0=PB[0][:, 0:128],
                                                   scalar=MSL[:, jstep + 1:jstep + 2],
                                                   in1=FSC[:, 0:128],
                                                   op0=ALU.mult, op1=ALU.add), 1)
                v6 = ve.n

            @blk.vector
            def _(v):
                v.wait_ge(ve.sem, v6)
                ve.tick(v.tensor_copy(out=c128(MBC, 0), in_=FSC[:, 0:128]), 1)
            v7 = ve.n

            if DBG:
                @blk.sync
                def _(sy):
                    sy.wait_ge(ve.sem, v7)
                    st.tick(sy.dma_start(out=dbg["dmst"][:, :], in_=c128(MBC, 0)), 16)

        if STAGE >= 4:
            vch = v7
            sch = se.n
            for c_ in range(NCH - 1):
                @blk.tensor
                def _(t, c_=c_, vch=vch, sch=sch):
                    t.wait_ge(ve.sem, vch)
                    if c_ > 0:
                        t.wait_ge(se.sem, sch)
                    bk = c_ % 2
                    t.matmul(PB[bk][:, 0:128], c128(PCTF, c_), c128(MBC, c_),
                             start=True, stop=False)
                    pe.tick(t.matmul(PB[bk][:, 0:128], IDEB, c128(DCF, c_),
                                     start=False, stop=True), 1)
                pch = pe.n

                @blk.scalar
                def _(s, c_=c_, pch=pch):
                    s.wait_ge(pe.sem, pch)
                    se.tick(s.copy(out=c128(MBC, c_ + 1), in_=PB[c_ % 2][:, 0:128]), 1)
                sch = se.n

            @blk.tensor
            def _(t):
                t.wait_ge(se.sem, sch)
                last = None
                for c_ in range(NCH):
                    bk, off = 2 + c_ % 2, (c_ // 2) * 128
                    last = t.transpose(pbb(bk, 128, off), c128(MBC, c_), IDEB)
                pe.tick(last, 1)
            p4a = pe.n

            @blk.scalar
            def _(s):
                s.wait_ge(pe.sem, p4a)
                last = None
                for c_ in range(NCH):
                    bk, off = 2 + c_ % 2, (c_ // 2) * 128
                    last = s.copy(out=c128(MTC, c_), in_=pbb(bk, 128, off))
                se.tick(last, 1)
            s4a = se.n

            @blk.tensor
            def _(t):
                t.wait_ge(se.sem, s4a)
                last = None
                for c_ in range(NCH):
                    bk, off = 4 + c_ % 4, (c_ // 4) * 128
                    t.matmul(PB[bk][:, off:off + 128], IDEB, c128(UVB, c_),
                             start=True, stop=False)
                    last = t.matmul(PB[bk][:, off:off + 128], c128(WTN, c_), c128(MBC, c_),
                                    start=False, stop=True)
                pe.tick(last, 1)
            p4b = pe.n

            @blk.scalar
            def _(s):
                s.wait_ge(pe.sem, p4b)
                last = None
                for c_ in range(NCH):
                    bk, off = 4 + c_ % 4, (c_ // 4) * 128
                    last = s.copy(out=c128(UFU, c_), in_=PB[bk][:, off:off + 128])
                se.tick(last, 1)
            s4b = se.n

            @blk.vector
            def _(v):
                v.wait_ge(pe.sem, p4b)
                v.wait_ge(se.sem, s4b)
                last = None
                for c_ in range(NCH):
                    bk, off = 4 + c_ % 4, (c_ // 4) * 128
                    last = v.tensor_scalar(out=c128(URF, c_), in0=PB[bk][:, off:off + 128],
                                           scalar1=RC_[:, c_:c_ + 1], scalar2=None,
                                           op0=ALU.mult)
                ve.tick(last, 1)
            v4b = ve.n

            @blk.tensor
            def _(t):
                t.wait_ge(se.sem, s4b)
                last = None
                for c_ in range(NCH):
                    bk, off = c_ % 2, (c_ // 2) * 128
                    last = t.transpose(pbb(bk, 128, off), c128(UFU, c_), IDEB)
                pe.tick(last, 1)
            p4c = pe.n

            @blk.scalar
            def _(s):
                s.wait_ge(pe.sem, p4c)
                last = None
                for c_ in range(NCH):
                    bk, off = c_ % 2, (c_ // 2) * 128
                    last = s.copy(out=c128(UTF, c_), in_=pbb(bk, 128, off))
                se.tick(last, 1)
            s4c = se.n

            @blk.tensor
            def _(t):
                t.wait_ge(se.sem, s4c)
                t.wait_ge(ve.sem, v4b)
                last = None
                for c_ in range(NCH):
                    bk, off = 4 + c_ % 4, (c_ // 4) * 128
                    last = t.matmul(PB[bk][:, off:off + 128], c128(UTF, c_),
                                    QB[:, c_ * 128:(c_ + 1) * 128], start=True, stop=True)
                pe.tick(last, 1)
            p4d = pe.n

            @blk.vector
            def _(v):
                v.wait_ge(pe.sem, p4d)
                last = None
                for c_ in range(NCH):
                    bk, off = 4 + c_ % 4, (c_ // 4) * 128
                    last = v.tensor_mul(out=c128(PTF, c_), in0=c128(EIF, c_),
                                        in1=PB[bk][:, off:off + 128])
                ve.tick(last, 1)
            v4d = ve.n

            @blk.tensor
            def _(t):
                t.wait_ge(ve.sem, v4d)
                last = None
                for c_ in range(NCH):
                    bk, off = 4 + c_ % 4, (c_ // 4) * 128
                    t.matmul(PB[bk][:, off:off + 128], c128(KTT, c_), c128(PTF, c_),
                             start=True, stop=False)
                    last = t.matmul(PB[bk][:, off:off + 128], c128(MTC, c_),
                                    QA[:, c_ * 128:(c_ + 1) * 128], start=False, stop=True)
                pe.tick(last, 1)
            p4e = pe.n

            @blk.scalar
            def _(s):
                s.wait_ge(pe.sem, p4e)
                last = None
                for c_ in range(NCH):
                    bk, off = 4 + c_ % 4, (c_ // 4) * 128
                    last = s.copy(out=OBF[:, c_ * 128:(c_ + 1) * 128],
                                  in_=PB[bk][:, off:off + 128])
                se.tick(last, 1)

            if DBG:
                @blk.sync
                def _(sy):
                    sy.wait_ge(se.sem, se.n)
                    st.tick(sy.dma_start(out=dbg["dO"][:, :], in_=OBF), 16)

        if STAGE >= 5:
            @blk.vector
            def _(v, st_now=st.n):
                v.wait_ge(se.sem, se.n)
                if DBG:
                    v.wait_ge(st.sem, st_now)
                ve.tick(v.tensor_mul(out=OBF, in0=OBF, in1=GF), 1)
            v11 = ve.n

            vwo = {}
            for h in range(8):
                bk0 = 0 if h % 2 == 0 else 4

                @blk.tensor
                def _(t, h=h, bk0=bk0, v11=v11):
                    if h == 0:
                        t.wait_ge(ve.sem, v11)
                    if h >= 2:
                        t.wait_ge(ve.sem, vwo[h - 2])
                    pe_i = None
                    for tc in range(4):
                        pe_i = t.matmul(PB[bk0 + tc][:, 0:512], wo_t(h),
                                        OBF[:, tc * 512:(tc + 1) * 512],
                                        start=True, stop=True)
                    pe.tick(pe_i, 1)
                p19 = pe.n

                @blk.vector
                def _(v, h=h, bk0=bk0, p19=p19):
                    v.wait_ge(pe.sem, p19)
                    last = None
                    for tc in range(4):
                        sl2 = slice(h * T + tc * 512, h * T + (tc + 1) * 512)
                        last = v.scalar_tensor_tensor(out=XFFN[:, sl2],
                                                      in0=PB[bk0 + tc][:, 0:512],
                                                      scalar=BOC[:, h:h + 1],
                                                      in1=XFFN[:, sl2],
                                                      op0=ALU.add, op1=ALU.add)
                    ve.tick(last, 1)
                vwo[h] = ve.n
            lastv = ve.n

            for h in range(8):
                @blk.scalar
                def _(s, h=h, lastv=lastv, se_now=se.n):
                    s.wait_ge(ve.sem, lastv)
                    if h > 0:
                        s.wait_ge(pe.sem, pe.n)
                        s.wait_ge(se.sem, se_now)
                    se.tick(s.activation(out=STB1, in_=XFFN[:, h * T:(h + 1) * T],
                                           func=AF.Identity), 1)
                    s.wait_ge(se.sem, se.n)
                    se.tick(s.activation(out=STB2, in_=STB1, func=AF.Square), 1)
                s10 = se.n

                @blk.tensor
                def _(t, h=h, s10=s10):
                    t.wait_ge(se.sem, s10)
                    if h == 0:
                        t.wait_ge(ve.sem, ve.n)
                    last = None
                    for tc in range(4):
                        last = t.matmul(PB[4 + tc][0:1, 0:512], ONECB,
                                        STB1[:, tc * 512:(tc + 1) * 512],
                                        start=(h == 0), stop=(h == 7))
                        last = t.matmul(PB[tc][0:1, 0:512], ONECB,
                                        STB2[:, tc * 512:(tc + 1) * 512],
                                        start=(h == 0), stop=(h == 7))
                    pe.tick(last, 1)
            p20 = pe.n

            @blk.vector
            def _(v):
                v.wait_ge(pe.sem, p20)
                for tc in range(4):
                    v.tensor_scalar(out=MROW[:, tc * 512:(tc + 1) * 512],
                                    in0=PB[4 + tc][0:1, 0:512], scalar1=1.0 / D,
                                    scalar2=None, op0=ALU.mult)
                    v.tensor_scalar(out=SQR[:, tc * 512:(tc + 1) * 512],
                                    in0=PB[tc][0:1, 0:512], scalar1=1.0 / D,
                                    scalar2=None, op0=ALU.mult)
                ve.tick(v.engine_nop(), 1)
                v.wait_ge(ve.sem, ve.n)
                ve.tick(v.tensor_mul(out=VARR, in0=MROW, in1=MROW), 1)
                v.wait_ge(ve.sem, ve.n)
                v.tensor_sub(out=VARR, in0=SQR, in1=VARR)
                ve.tick(v.engine_nop(), 1)
                v.wait_ge(ve.sem, ve.n)
                ve.tick(v.tensor_scalar_add(out=VARR, in0=VARR, scalar1=1e-5), 1)
            v12 = ve.n

            @blk.scalar
            def _(s):
                s.wait_ge(ve.sem, v12)
                se.tick(s.activation(out=VARR, in_=VARR, func=AF.Sqrt), 1)
            s11 = se.n

            @blk.vector
            def _(v):
                v.wait_ge(se.sem, s11)
                ve.tick(v.reciprocal(out=RSTD, in_=VARR), 1)
                v.wait_ge(ve.sem, ve.n)
                ve.tick(v.tensor_mul(out=MRSTD, in0=MROW, in1=RSTD), 1)
            v13 = ve.n

            @blk.tensor
            def _(t):
                t.wait_ge(ve.sem, v13)
                last = None
                for tc in range(4):
                    last = t.matmul(PB[4 + tc][:, 0:512], ONE,
                                    RSTD[:, tc * 512:(tc + 1) * 512],
                                    start=True, stop=True)
                    last = t.matmul(PB[tc][:, 0:512], ONE,
                                    MRSTD[:, tc * 512:(tc + 1) * 512],
                                    start=True, stop=True)
                pe.tick(last, 1)
            p21 = pe.n

            @blk.scalar
            def _(s):
                s.wait_ge(pe.sem, p21)
                last = None
                for tc in range(4):
                    last = s.copy(out=STF[:, tc * 512:(tc + 1) * 512], in_=PB[4 + tc][:, 0:512])
                se.tick(last, 1)
                s.wait_ge(se.sem, se.n)
                last = None
                for tc in range(4):
                    last = s.copy(out=FSC[:, tc * 512:(tc + 1) * 512], in_=PB[tc][:, 0:512])
                se.tick(last, 1)
            s12 = se.n

            for h in range(8):
                @blk.vector
                def _(v, h=h, s12=s12):
                    v.wait_ge(se.sem, s12)
                    sl = slice(h * T, (h + 1) * T)
                    v.tensor_mul(out=XFFN[:, sl], in0=XFFN[:, sl], in1=STF[:, 0:2048])
                    ve.tick(v.engine_nop(), 1)
                    v.wait_ge(ve.sem, ve.n)
                    ve.tick(v.tensor_sub(out=XFFN[:, sl], in0=XFFN[:, sl],
                                         in1=FSC[:, 0:2048]), 1)
                    v.wait_ge(ve.sem, ve.n)
                    ve.tick(v.tensor_reduce(out=SM[:, 40 + h:41 + h], in_=XFFN[:, sl],
                                            axis=AX.X, op=ALU.max,
                                            apply_absolute_value=True), 1)
                lastv = ve.n

            @blk.vector
            def _(v):
                v.wait_ge(ve.sem, lastv)
                ve.tick(v.tensor_reduce(out=SM[:, 48:49], in_=SM[:, 40:48], axis=AX.X,
                                        op=ALU.max, apply_absolute_value=False), 1)
            v14 = ve.n

            @blk.tensor
            def _(t):
                t.wait_ge(ve.sem, v14)
                t.wait_ge(se.sem, se.n)
                pe.tick(t.transpose(PB[5][0:1, 0:128], SM[:, 48:49], IDE), 1)
            pe_sx = pe.n

            @blk.vector
            def _(v):
                v.wait_ge(pe.sem, pe_sx)
                ve.tick(v.tensor_reduce(out=SM[0:1, 49:50], in_=PB[5][0:1, 0:128],
                                        axis=AX.X, op=ALU.max), 1)
            ve_sx = ve.n

            @blk.gpsimd
            def _(g):
                g.wait_ge(ve.sem, ve_sx)
                gq.tick(g.dma_start(out=b_mx_in.ap()[0:1, 0:1], in_=SM[0:1, 49:50]), 16)
                g.wait_ge(gq.sem, gq.n)
                cc[2].tick(g.collective_compute(
                    "AllReduce", ALU.max, replica_groups=[list(range(NCORE))],
                    ins=[b_mx_in.ap()[0:1, 0:1]], outs=[b_mx_out.ap()[0:1, 0:1]]), 1)
                g.wait_ge(cc[2].sem, cc[2].n)
                gq.tick(g.dma_start(out=SM[0:1, 50:51], in_=b_mx_out.ap()[0:1, 0:1]), 16)
            gq_sx = gq.n

            @blk.vector
            def _(v):
                v.wait_ge(gq.sem, gq_sx)
                v.tensor_scalar(out=SM[0:1, 51:52], in0=SM[0:1, 50:51],
                                scalar1=1.0 / QMAX, scalar2=1e-12, op0=ALU.mult, op1=ALU.max)
                ve.tick(v.engine_nop(), 1)
                v.wait_ge(ve.sem, ve.n)
                ve.tick(v.reciprocal(out=SM[0:1, 52:53], in_=SM[0:1, 51:52]), 1)
            v15 = ve.n

            @blk.tensor
            def _(t):
                t.wait_ge(ve.sem, v15)
                t.wait_ge(se.sem, se.n)
                pe.tick(t.matmul(PB[5][:, 0:2], ONE, SM[0:1, 51:53], start=True, stop=True), 1)
            p22 = pe.n

            @blk.vector
            def _(v):
                v.wait_ge(pe.sem, p22)
                ve.tick(v.tensor_copy(out=SCL[:, 0:2], in_=PB[5][:, 0:2]), 1)
            v16 = ve.n

            xqv = {}
            for h in range(8):
                ints = INT1 if h % 2 == 0 else INT1o

                @blk.vector
                def _(v, h=h, ints=ints, v16=v16):
                    if h < 2:
                        v.wait_ge(ve.sem, v16)
                    else:
                        v.wait_ge(ve.sem, xqv[h - 2])
                    sl = slice(h * T, (h + 1) * T)
                    v.tensor_scalar(out=ints, in0=XFFN[:, sl], scalar1=SCL[:, 1:2],
                                    scalar2=None, op0=ALU.mult)
                    ve.tick(v.engine_nop(), 1)
                    v.wait_ge(ve.sem, ve.n)
                    ve.tick(v.tensor_scalar(out=XBF[h], in0=ints, scalar1=SCL[:, 0:1],
                                            scalar2=None, op0=ALU.mult), 1)
                xqv[h] = ve.n

            if DBG:
                @blk.sync
                def _(sy):
                    sy.wait_ge(ve.sem, ve.n)
                    st.tick(sy.dma_start(
                        out=dbg["dxffn"].rearrange("(h p) t -> p h t", p=128),
                        in_=XFFN.ap().rearrange("p (h t) -> p h t", h=8)), 16)

        def emit_mm_ot(ot, bsrc, bdst, acc_col, pe_used, ve_red, se_evac, sw_spill,
                       first_extra_wait=None):
            par = ot % 2
            wsl = GEN[:, 20480 + par * 1024:20480 + par * 1024 + 1024]
            rank, oc = ot // 3, ot % 3
            LDP = ld if par == 0 else ldb
            SWP = sw if par == 0 else swb

            @blk.sync
            def _(sy, ot=ot, rank=rank, oc=oc, wsl=wsl, LDP=LDP, ve_now=ve.n):
                sy.wait_ge(pe.sem, pe.n if ot < 2 else pe_used[ot - 2])
                if ot < 2:
                    sy.wait_ge(ve.sem, ve_now)
                for i in range(8):
                    LDP.tick(sy.dma_start(
                        out=wsl[:, i * 128:(i + 1) * 128],
                        in_=bsrc[rank * D + i * 128:rank * D + (i + 1) * 128,
                                 oc * 128:(oc + 1) * 128]), 16)
            need_ld = LDP.n

            @blk.tensor
            def _(t, ot=ot, par=par, wsl=wsl, need_ld=need_ld, LDP=LDP):
                t.wait_ge(LDP.sem, need_ld)
                if first_extra_wait is not None:
                    first_extra_wait(t)
                if ot == 0:
                    t.wait_ge(cc[0].sem, 3)
                    t.wait_ge(ve.sem, ve.n)
                    t.wait_ge(se.sem, se.n)
                if ot >= 2:
                    t.wait_ge(se.sem, se_evac[ot - 2])
                bk = [0, 1, 2, 3] if par == 0 else [4, 5, 6, 7]
                last = None
                for i in range(8):
                    for tc in range(4):
                        last = t.matmul(PB[bk[tc]][:, 0:512], wsl[:, i * 128:(i + 1) * 128],
                                        XBF[i][:, tc * 512:(tc + 1) * 512],
                                        start=(i == 0), stop=(i == 7))
                pe.tick(last, 1)
            pe_used[ot] = pe.n
            need_pe = pe.n

            @blk.scalar
            def _(s, ot=ot, par=par, need_pe=need_pe, SWP=SWP, st_now=st.n,
                  sw_now=sw.n, swb_now=swb.n):
                s.wait_ge(pe.sem, need_pe)
                if DBG and ot < 2:
                    s.wait_ge(st.sem, st_now)
                if ot < 2:
                    s.wait_ge(sw.sem, sw_now)
                    s.wait_ge(swb.sem, swb_now)
                if ot >= 2:
                    s.wait_ge(SWP.sem, sw_spill[ot - 2])
                    s.wait_ge(ve.sem, ve_red[ot - 2])
                stg = GEN[:, 16384:18432] if par == 0 else GEN[:, 18432:20480]
                bk = [0, 1, 2, 3] if par == 0 else [4, 5, 6, 7]
                last = None
                for tc in range(4):
                    last = s.copy(out=stg[:, tc * 512:(tc + 1) * 512],
                                  in_=PB[bk[tc]][:, 0:512])
                se.tick(last, 1)
            se_evac[ot] = se.n
            need_se = se.n

            @blk.vector
            def _(v, ot=ot, par=par, acc_col=acc_col, need_se=need_se, ve_now=ve.n):
                v.wait_ge(se.sem, need_se)
                if ot >= 1:
                    v.wait_ge(ve.sem, ve_red[ot - 1])
                else:
                    v.wait_ge(ve.sem, ve_now)
                stg = GEN[:, 16384:18432] if par == 0 else GEN[:, 18432:20480]
                v.tensor_reduce(out=SM[:, 54:55], in_=stg, axis=AX.X,
                                op=ALU.max, apply_absolute_value=True)
                ve.tick(v.engine_nop(), 1)
                v.wait_ge(ve.sem, ve.n)
                if ot == 0:
                    ve.tick(v.tensor_copy(out=SM[:, acc_col:acc_col + 1],
                                          in_=SM[:, 54:55]), 1)
                else:
                    ve.tick(v.tensor_tensor(out=SM[:, acc_col:acc_col + 1],
                                            in0=SM[:, acc_col:acc_col + 1],
                                            in1=SM[:, 54:55], op=ALU.max), 1)
            ve_red[ot] = ve.n

            @blk.scalar
            def _(s, ot=ot, par=par, need_se=need_se, SWP=SWP):
                s.wait_ge(se.sem, need_se)
                stg = GEN[:, 16384:18432] if par == 0 else GEN[:, 18432:20480]
                SWP.tick(s.dma_start(out=bdst[ot * 128:(ot + 1) * 128, :], in_=stg), 16)
            sw_spill[ot] = SWP.n

        def bc_cols(rsrc, k, dstc, waitv):
            @blk.tensor
            def _(t, rsrc=rsrc, k=k, waitv=waitv):
                t.wait_ge(ve.sem, waitv)
                t.wait_ge(se.sem, se.n)
                pe.tick(t.matmul(PB[5][:, 0:k], ONE, SM[0:1, rsrc:rsrc + k],
                                 start=True, stop=True), 1)
            pw = pe.n

            @blk.vector
            def _(v, k=k, dstc=dstc, pw=pw):
                v.wait_ge(pe.sem, pw)
                ve.tick(v.tensor_copy(out=SCL[:, dstc:dstc + k], in_=PB[5][:, 0:k]), 1)
            return ve.n

        def allred_max(accc, rowdst, ccx):
            @blk.tensor
            def _(t, accc=accc):
                t.wait_ge(ve.sem, ve.n)
                t.wait_ge(se.sem, se.n)
                pe.tick(t.transpose(PB[5][0:1, 0:128], SM[:, accc:accc + 1], IDE), 1)
            pw = pe.n

            @blk.vector
            def _(v, pw=pw, rowdst=rowdst):
                v.wait_ge(pe.sem, pw)
                ve.tick(v.tensor_reduce(out=SM[0:1, rowdst:rowdst + 1],
                                        in_=PB[5][0:1, 0:128],
                                        axis=AX.X, op=ALU.max), 1)
            vw = ve.n

            @blk.gpsimd
            def _(g, vw=vw, rowdst=rowdst, ccx=ccx):
                g.wait_ge(ve.sem, vw)
                gq.tick(g.dma_start(out=b_mx_in.ap()[0:1, 0:1],
                                    in_=SM[0:1, rowdst:rowdst + 1]), 16)
                g.wait_ge(gq.sem, gq.n)
                cc[ccx].tick(g.collective_compute(
                    "AllReduce", ALU.max, replica_groups=[list(range(NCORE))],
                    ins=[b_mx_in.ap()[0:1, 0:1]], outs=[b_mx_out.ap()[0:1, 0:1]]), 1)
                g.wait_ge(cc[ccx].sem, cc[ccx].n)
                gq.tick(g.dma_start(out=SM[0:1, rowdst:rowdst + 1],
                                    in_=b_mx_out.ap()[0:1, 0:1]), 16)
            return gq.n

        def emit_p3_ot(ot, ld_in, po_g2, ve_g2, sw_g2):
            par = ot % 2
            stg_in = STB1 if par == 0 else STB2
            stfb = STF.ap().bitcast(BF16)
            stg_out = stfb[:, 0:2048] if par == 0 else stfb[:, 2048:4096]
            i1 = (GEN[:, 24576:28672].bitcast(I32) if par == 0
                  else GEN[:, 28672:32768].bitcast(I32))
            i2 = GENF[:, 2048:4096].bitcast(I32)
            sgf = SIGF if par == 0 else SIGFo
            LDP = lp5 if par == 0 else lp5b
            SWP = sp3 if par == 0 else sp3b

            @blk.sync
            def _(sy, ot=ot, stg_in=stg_in, LDP=LDP, sw_now=sw.n, swb_now=swb.n):
                if ot < 2:
                    sy.wait_ge(ve.sem, ve.n)
                    sy.wait_ge(sw.sem, sw_now)
                    sy.wait_ge(swb.sem, swb_now)
                else:
                    sy.wait_ge(ve.sem, ve_g2[ot - 2])
                LDP.tick(sy.dma_start(out=stg_in,
                                      in_=b_gate.ap()[ot * 128:(ot + 1) * 128, :]), 16)
            ld_in[ot] = LDP.n

            @blk.vector
            def _(v, ot=ot, stg_in=stg_in, i1=i1, LDP=LDP):
                v.wait_ge(LDP.sem, ld_in[ot])
                if ot >= 2:
                    v.wait_ge(ve.sem, po_g2[ot - 2])
                ve.tick(v.tensor_scalar(out=i1, in0=stg_in, scalar1=SCL[:, 4:5],
                                        scalar2=None, op0=ALU.mult), 1)
            vv = ve.n

            @blk.scalar
            def _(s, vv=vv, i1=i1, sgf=sgf):
                s.wait_ge(ve.sem, vv)
                se.tick(s.activation(out=sgf, in_=i1, func=AF.Sigmoid,
                                     scale=SCL[:, 2:3]), 1)
            ss = se.n

            @blk.vector
            def _(v, ot=ot, ss=ss, i2=i2, sgf=sgf):
                v.wait_ge(se.sem, ss)
                if ot >= 1:
                    v.wait_ge(ve.sem, po_g2[ot - 1])
                ve.tick(v.tensor_scalar(out=i2, in0=sgf, scalar1=65536.0,
                                        scalar2=65535.0, op0=ALU.mult, op1=ALU.min), 1)
            vv2 = ve.n

            @blk.vector
            def _(v, ot=ot, stg_out=stg_out, vv2=vv2, i1=i1, i2=i2, SWP=SWP):
                v.wait_ge(ve.sem, vv2)
                if ot >= 2:
                    v.wait_ge(SWP.sem, sw_g2[ot - 2])
                ve.tick(v.tensor_tensor(out=stg_out, in0=i1, in1=i2, op=ALU.mult), 1)
            po_g2[ot] = ve.n
            pp = ve.n

            @blk.vector
            def _(v, ot=ot, stg_out=stg_out, pp=pp):
                v.wait_ge(ve.sem, pp)
                if ot >= 1:
                    v.wait_ge(ve.sem, ve_g2[ot - 1])
                v.tensor_reduce(out=SM[:, 45:46], in_=stg_out, axis=AX.X,
                                op=ALU.max, apply_absolute_value=True)
                ve.tick(v.engine_nop(), 1)
                v.wait_ge(ve.sem, ve.n)
                if ot == 0:
                    ve.tick(v.tensor_copy(out=SM[:, 55:56], in_=SM[:, 45:46]), 1)
                else:
                    ve.tick(v.tensor_tensor(out=SM[:, 55:56], in0=SM[:, 55:56],
                                            in1=SM[:, 45:46], op=ALU.max), 1)
            ve_g2[ot] = ve.n

            @blk.scalar
            def _(s, ot=ot, stg_out=stg_out, pp=pp, SWP=SWP):
                s.wait_ge(ve.sem, pp)
                SWP.tick(s.dma_start(out=b_g2.ap()[ot * 128:(ot + 1) * 128, :],
                                     in_=stg_out), 16)
            sw_g2[ot] = SWP.n

        if STAGE >= 6:
            pe_used = {}
            ve_red = {}
            se_evac = {}
            sw_spill = {}
            mm_pass_env = dict(bsrc=b_wg.ap(), bdst=b_gate.ap())
            for ot in range(NOT):
                emit_mm_ot(ot, b_wg.ap(), b_gate.ap(), 56, pe_used, ve_red,
                           se_evac, sw_spill)

            gq_sg = allred_max(56, 58, 3)

            pe_used2 = {}
            ve_red2 = {}
            se_evac2 = {}
            sw_spill2 = {}
            ld_in = {}
            po_g2 = {}
            ve_g2 = {}
            sw_g2 = {}

            for ot in range(6):
                emit_mm_ot(ot, b_wu.ap(), b_up.ap(), 57, pe_used2, ve_red2,
                           se_evac2, sw_spill2)

            @blk.vector
            def _(v, gq_sg=gq_sg):
                v.wait_ge(gq.sem, gq_sg)
                v.tensor_scalar(out=SM[0:1, 60:61], in0=SM[0:1, 58:59],
                                scalar1=1.0 / QMAX, scalar2=1e-12, op0=ALU.mult, op1=ALU.max)
                ve.tick(v.engine_nop(), 1)
                v.wait_ge(ve.sem, ve.n)
                ve.tick(v.reciprocal(out=SM[0:1, 61:62], in_=SM[0:1, 60:61]), 1)
            v_sg = ve.n
            bc_cols(60, 1, 2, v_sg)
            ve_sg2 = bc_cols(61, 1, 4, ve.n)


            def up7_wait(t, vebc=ve.n):
                t.wait_ge(ve.sem, vebc)

            for ot in range(6, NOT):
                emit_mm_ot(ot, b_wu.ap(), b_up.ap(), 57, pe_used2, ve_red2,
                           se_evac2, sw_spill2,
                           first_extra_wait=(up7_wait if ot == 6 else None))
                emit_p3_ot(ot - 6, ld_in, po_g2, ve_g2, sw_g2)
            for ot2 in range(NOT - 6, NOT):
                emit_p3_ot(ot2, ld_in, po_g2, ve_g2, sw_g2)

            gq_su = allred_max(57, 59, 6)

            @blk.vector
            def _(v, gq_su=gq_su):
                v.wait_ge(gq.sem, gq_su)
                v.tensor_scalar(out=SM[0:1, 62:63], in0=SM[0:1, 59:60],
                                scalar1=1.0 / QMAX, scalar2=1e-12, op0=ALU.mult, op1=ALU.max)
                ve.tick(v.engine_nop(), 1)
                v.wait_ge(ve.sem, ve.n)
                ve.tick(v.reciprocal(out=SM[0:1, 63:64], in_=SM[0:1, 62:63]), 1)
            v_su = ve.n
            bc_cols(62, 1, 3, v_su)
            ve_scl6 = bc_cols(63, 1, 5, ve.n)

            gq_g2 = allred_max(55, 54, 4)

            @blk.vector
            def _(v, gq_g2=gq_g2):
                v.wait_ge(gq.sem, gq_g2)
                v.tensor_scalar(out=SM[0:1, 61:62], in0=SM[0:1, 60:61],
                                scalar1=1.0 / 65536.0, scalar2=None, op0=ALU.mult)
                ve.tick(v.engine_nop(), 1)
                v.wait_ge(ve.sem, ve.n)
                v.tensor_mul(out=SM[0:1, 54:55], in0=SM[0:1, 54:55], in1=SM[0:1, 61:62])
                ve.tick(v.engine_nop(), 1)
                v.wait_ge(ve.sem, ve.n)
                v.tensor_scalar(out=SM[0:1, 54:55], in0=SM[0:1, 54:55],
                                scalar1=1.0 / QMAX, scalar2=1e-12, op0=ALU.mult, op1=ALU.max)
                ve.tick(v.engine_nop(), 1)
                v.wait_ge(ve.sem, ve.n)
                ve.tick(v.reciprocal(out=SM[0:1, 53:54], in_=SM[0:1, 54:55]), 1)
                v.wait_ge(ve.sem, ve.n)
                ve.tick(v.tensor_mul(out=SM[0:1, 53:54], in0=SM[0:1, 53:54],
                                     in1=SM[0:1, 61:62]), 1)
            v18 = ve.n
            ve_scl8 = bc_cols(53, 2, 6, v18)

        if STAGE >= 8:
            ld_g2 = {}
            ld_up = {}
            po_op = {}
            ve_op = {}
            sw_op = {}
            for ot in range(NOT):
                par = ot % 2
                stg_g2 = STB1 if par == 0 else STB2
                stg_up = XBF[2] if par == 0 else XBF[3]
                stg_out = XBF[4] if par == 0 else XBF[5]

                LDP = ld if par == 0 else ldb
                SWP = sw if par == 0 else swb

                @blk.sync
                def _(sy, ot=ot, stg_g2=stg_g2, stg_up=stg_up, LDP=LDP):
                    if ot < 2:
                        sy.wait_ge(ve.sem, ve_scl8)
                        sy.wait_ge(sw.sem, sw.n)
                        sy.wait_ge(swb.sem, swb.n)
                        sy.wait_ge(sp3.sem, sp3.n)
                        sy.wait_ge(sp3b.sem, sp3b.n)
                    else:
                        sy.wait_ge(ve.sem, ve_op[ot - 2])
                        sy.wait_ge(ve.sem, po_op[ot - 2])
                    LDP.tick(sy.dma_start(out=stg_g2, in_=b_g2.ap()[ot * 128:(ot + 1) * 128, :]), 16)
                    LDP.tick(sy.dma_start(out=stg_up, in_=b_up.ap()[ot * 128:(ot + 1) * 128, :]), 16)
                need_ld = LDP.n

                i1 = INT1 if par == 0 else INT1o
                i2 = INT2 if par == 0 else INT2o

                @blk.vector
                def _(v, ot=ot, stg_g2=stg_g2, stg_up=stg_up, need_ld=need_ld,
                      i1=i1, i2=i2, LDP=LDP):
                    v.wait_ge(LDP.sem, need_ld)
                    if ot >= 2:
                        v.wait_ge(ve.sem, po_op[ot - 2])
                    v.tensor_scalar(out=i1, in0=stg_g2, scalar1=SCL[:, 6:7],
                                    scalar2=None, op0=ALU.mult)
                    ve.tick(v.tensor_scalar(out=i2, in0=stg_up, scalar1=SCL[:, 5:6],
                                            scalar2=None, op0=ALU.mult), 1)
                vv = ve.n

                @blk.vector
                def _(v, ot=ot, stg_out=stg_out, vv=vv, i1=i1, i2=i2, SWP=SWP):
                    v.wait_ge(ve.sem, vv)
                    if ot >= 2:
                        v.wait_ge(SWP.sem, sw_op[ot - 2])
                    ve.tick(v.tensor_tensor(out=stg_out, in0=i1, in1=i2,
                                            op=ALU.mult), 1)
                po_op[ot] = ve.n
                pp = ve.n

                @blk.vector
                def _(v, ot=ot, stg_out=stg_out, pp=pp):
                    v.wait_ge(ve.sem, pp)
                    if ot >= 1:
                        v.wait_ge(ve.sem, ve_op[ot - 1])
                    v.tensor_reduce(out=SM[:, 54:55], in_=stg_out, axis=AX.X,
                                    op=ALU.max, apply_absolute_value=True)
                    ve.tick(v.engine_nop(), 1)
                    v.wait_ge(ve.sem, ve.n)
                    if ot == 0:
                        ve.tick(v.tensor_copy(out=SM[:, 55:56], in_=SM[:, 54:55]), 1)
                    else:
                        ve.tick(v.tensor_tensor(out=SM[:, 55:56], in0=SM[:, 55:56],
                                                in1=SM[:, 54:55], op=ALU.max), 1)
                ve_op[ot] = ve.n

                @blk.scalar
                def _(s, ot=ot, stg_out=stg_out, pp=pp, SWP=SWP):
                    s.wait_ge(ve.sem, pp)
                    SWP.tick(s.dma_start(out=b_opre.ap()[ot * 128:(ot + 1) * 128, :],
                                         in_=stg_out), 16)
                sw_op[ot] = SWP.n

            @blk.tensor
            def _(t):
                t.wait_ge(ve.sem, ve.n)
                t.wait_ge(se.sem, se.n)
                pe.tick(t.transpose(PB[5][0:1, 0:128], SM[:, 55:56], IDE), 1)
            pe_ox = pe.n

            @blk.vector
            def _(v):
                v.wait_ge(pe.sem, pe_ox)
                ve.tick(v.tensor_reduce(out=SM[0:1, 54:55], in_=PB[5][0:1, 0:128],
                                        axis=AX.X, op=ALU.max), 1)
            ve_ox = ve.n

            @blk.gpsimd
            def _(g):
                g.wait_ge(ve.sem, ve_ox)
                gq.tick(g.dma_start(out=b_mx_in.ap()[0:1, 0:1], in_=SM[0:1, 54:55]), 16)
                g.wait_ge(gq.sem, gq.n)
                cc[5].tick(g.collective_compute(
                    "AllReduce", ALU.max, replica_groups=[list(range(NCORE))],
                    ins=[b_mx_in.ap()[0:1, 0:1]], outs=[b_mx_out.ap()[0:1, 0:1]]), 1)
                g.wait_ge(cc[5].sem, cc[5].n)
                gq.tick(g.dma_start(out=SM[0:1, 54:55], in_=b_mx_out.ap()[0:1, 0:1]), 16)
            gq_o = gq.n

            @blk.vector
            def _(v):
                v.wait_ge(gq.sem, gq_o)
                v.tensor_mul(out=SM[:, 60:61], in0=SCL[:, 3:4], in1=SCL[:, 7:8])
                ve.tick(v.engine_nop(), 1)
                v.wait_ge(ve.sem, ve.n)
                v.tensor_mul(out=SM[0:1, 54:55], in0=SM[0:1, 54:55], in1=SM[0:1, 60:61])
                ve.tick(v.engine_nop(), 1)
                v.wait_ge(ve.sem, ve.n)
                v.tensor_scalar(out=SM[0:1, 54:55], in0=SM[0:1, 54:55],
                                scalar1=1.0 / QMAX, scalar2=1e-12, op0=ALU.mult, op1=ALU.max)
                ve.tick(v.engine_nop(), 1)
                v.wait_ge(ve.sem, ve.n)
                ve.tick(v.reciprocal(out=SM[0:1, 53:54], in_=SM[0:1, 54:55]), 1)
                v.wait_ge(ve.sem, ve.n)
                ve.tick(v.tensor_mul(out=SM[0:1, 53:54], in0=SM[0:1, 53:54],
                                     in1=SM[0:1, 60:61]), 1)
            v19 = ve.n

            @blk.tensor
            def _(t):
                t.wait_ge(ve.sem, v19)
                t.wait_ge(se.sem, se.n)
                pe.tick(t.matmul(PB[5][:, 0:2], ONE, SM[0:1, 53:55], start=True, stop=True), 1)
            p25 = pe.n

            @blk.vector
            def _(v):
                v.wait_ge(pe.sem, p25)
                ve.tick(v.tensor_copy(out=SCL[:, 8:10], in_=PB[5][:, 0:2]), 1)
            ve_scl10 = ve.n

        if STAGE >= 9:
            OJH = [gb(1024 * j, 1024) for j in range(NOT)]
            for half in range(2):
                hof = half * 1024
                p5v = {}
                for j in range(NOT):
                    par = j % 2
                    stg = STB1[:, 0:1024] if par == 0 else STB1[:, 1024:2048]
                    ints = (GEN[:, 26624:28672].bitcast(I32) if par == 0
                            else GEN[:, 28672:30720].bitcast(I32))
                    LDP = lp5 if par == 0 else lp5b

                    @blk.sync
                    def _(sy, j=j, hof=hof, stg=stg, half=half, LDP=LDP):
                        if half == 0 and j < 2:
                            sy.wait_ge(ve.sem, ve_scl10)
                            sy.wait_ge(sw.sem, sw.n)
                            sy.wait_ge(swb.sem, swb.n)
                        elif j >= 2:
                            sy.wait_ge(ve.sem, p5v[j - 2])
                        else:
                            sy.wait_ge(ve.sem, ve.n)
                        LDP.tick(sy.dma_start(
                            out=stg, in_=b_opre.ap()[j * 128:(j + 1) * 128,
                                                     hof:hof + 1024]), 16)
                    need_ld = LDP.n

                    @blk.vector
                    def _(v, j=j, stg=stg, ints=ints, need_ld=need_ld, half=half,
                          pe_now=pe.n, LDP=LDP):
                        v.wait_ge(LDP.sem, need_ld)
                        if half > 0:
                            v.wait_ge(pe.sem, pe_now)
                        v.tensor_scalar(out=ints[:, 0:1024], in0=stg, scalar1=SCL[:, 8:9],
                                        scalar2=None, op0=ALU.mult)
                        ve.tick(v.engine_nop(), 1)
                        v.wait_ge(ve.sem, ve.n)
                        ve.tick(v.tensor_scalar(out=OJH[j], in0=ints[:, 0:1024],
                                                scalar1=SCL[:, 9:10], scalar2=None,
                                                op0=ALU.mult), 1)
                    p5v[j] = ve.n
                ve_p5 = ve.n

                dn_pe = []
                res_ve = {}
                for h in range(8):
                    bk0 = (h % 4) * 2
                    for jg in range(3):
                        gidx = h * 3 + jg
                        par = gidx % 2
                        wsl = GEN[:, 24576 + par * 1024:24576 + par * 1024 + 1024]
                        LDP = ldw2 if par == 0 else ldw2b

                        @blk.sync
                        def _(sy, h=h, jg=jg, wsl=wsl, gidx=gidx, dn_pe=tuple(dn_pe),
                              LDP=LDP):
                            if gidx >= 2:
                                sy.wait_ge(pe.sem, dn_pe[gidx - 2])
                            else:
                                sy.wait_ge(pe.sem, pe.n)
                            for jj in range(8):
                                j = jg * 8 + jj
                                LDP.tick(sy.dma_start(
                                    out=wsl[:, jj * 128:(jj + 1) * 128],
                                    in_=b_wd.ap()[h * FF + j * 128:h * FF + (j + 1) * 128, :]), 16)
                        need_ld = LDP.n

                        @blk.tensor
                        def _(t, h=h, jg=jg, bk0=bk0, wsl=wsl, need_ld=need_ld,
                              ve_p5=ve_p5, res_ve=dict(res_ve), LDP=LDP):
                            t.wait_ge(LDP.sem, need_ld)
                            t.wait_ge(ve.sem, ve_p5)
                            if jg == 0 and (h - 4) in res_ve:
                                t.wait_ge(ve.sem, res_ve[h - 4])
                            last = None
                            for jj in range(8):
                                j = jg * 8 + jj
                                for tc2 in range(2):
                                    last = t.matmul(PB[bk0 + tc2][:, 0:512],
                                                    wsl[:, jj * 128:(jj + 1) * 128],
                                                    OJH[j][:, tc2 * 512:(tc2 + 1) * 512],
                                                    start=(j == 0), stop=(j == NOT - 1))
                            pe.tick(last, 1)
                        dn_pe.append(pe.n)
                    need_pe = pe.n

                    @blk.vector
                    def _(v, h=h, bk0=bk0, hof=hof, need_pe=need_pe, st_now=st.n):
                        v.wait_ge(pe.sem, need_pe)
                        if DBG:
                            v.wait_ge(st.sem, st_now)
                        last = None
                        for tc2 in range(2):
                            sl = slice(h * T + hof + tc2 * 512, h * T + hof + (tc2 + 1) * 512)
                            last = v.tensor_add(out=XFFN[:, sl], in0=XFFN[:, sl],
                                                in1=PB[bk0 + tc2][:, 0:512])
                        ve.tick(last, 1)
                    res_ve[h] = ve.n

                for tc2 in range(2):
                    gsl = slice(hof + tc2 * 512, hof + (tc2 + 1) * 512)
                    ve_r2 = ve.n
                    for h in range(8):
                        @blk.scalar
                        def _(s, h=h, gsl=gsl, ve_r2=ve_r2, se_now=se.n):
                            s.wait_ge(ve.sem, ve_r2)
                            if h > 0:
                                s.wait_ge(pe.sem, pe.n)
                                s.wait_ge(se.sem, se_now)
                            sl = slice(h * T + gsl.start, h * T + gsl.stop)
                            se.tick(s.activation(out=STB2[:, 0:512], in_=XFFN[:, sl],
                                                 func=AF.Identity), 1)
                            s.wait_ge(se.sem, se.n)
                            se.tick(s.activation(out=STB2[:, 512:1024], in_=STB2[:, 0:512],
                                                 func=AF.Square), 1)
                        s13 = se.n

                        @blk.tensor
                        def _(t, h=h, s13=s13):
                            t.wait_ge(se.sem, s13)
                            if h == 0:
                                t.wait_ge(ve.sem, ve.n)
                            t.matmul(PB[0][0:1, 0:512], ONECB, STB2[:, 0:512],
                                     start=(h == 0), stop=(h == 7))
                            last = t.matmul(PB[1][0:1, 0:512], ONECB, STB2[:, 512:1024],
                                            start=(h == 0), stop=(h == 7))
                            pe.tick(last, 1)
                    p26 = pe.n

                    @blk.vector
                    def _(v, p26=p26):
                        v.wait_ge(pe.sem, p26)
                        v.tensor_scalar(out=MROW[:, 0:512], in0=PB[0][0:1, 0:512],
                                        scalar1=1.0 / D, scalar2=None, op0=ALU.mult)
                        ve.tick(v.tensor_scalar(out=SQR[:, 0:512], in0=PB[1][0:1, 0:512],
                                                scalar1=1.0 / D, scalar2=None,
                                                op0=ALU.mult), 1)
                        v.wait_ge(ve.sem, ve.n)
                        ve.tick(v.tensor_mul(out=VARR[:, 0:512], in0=MROW[:, 0:512],
                                             in1=MROW[:, 0:512]), 1)
                        v.wait_ge(ve.sem, ve.n)
                        v.tensor_sub(out=VARR[:, 0:512], in0=SQR[:, 0:512],
                                     in1=VARR[:, 0:512])
                        ve.tick(v.engine_nop(), 1)
                        v.wait_ge(ve.sem, ve.n)
                        ve.tick(v.tensor_scalar_add(out=VARR[:, 0:512],
                                                    in0=VARR[:, 0:512], scalar1=1e-5), 1)
                    v20 = ve.n

                    @blk.scalar
                    def _(s, v20=v20):
                        s.wait_ge(ve.sem, v20)
                        se.tick(s.activation(out=VARR[:, 0:512], in_=VARR[:, 0:512],
                                             func=AF.Sqrt), 1)
                    s14 = se.n

                    @blk.vector
                    def _(v, s14=s14):
                        v.wait_ge(se.sem, s14)
                        ve.tick(v.reciprocal(out=RSTD[:, 0:512], in_=VARR[:, 0:512]), 1)
                        v.wait_ge(ve.sem, ve.n)
                        ve.tick(v.tensor_mul(out=MRSTD[:, 0:512], in0=MROW[:, 0:512],
                                             in1=RSTD[:, 0:512]), 1)
                    v21 = ve.n

                    @blk.tensor
                    def _(t, v21=v21):
                        t.wait_ge(ve.sem, v21)
                        t.matmul(PB[2][:, 0:512], ONE, RSTD[:, 0:512],
                                 start=True, stop=True)
                        pe.tick(t.matmul(PB[3][:, 0:512], ONE, MRSTD[:, 0:512],
                                         start=True, stop=True), 1)
                    p27 = pe.n

                    for h in range(8):
                        @blk.vector
                        def _(v, h=h, gsl=gsl, p27=p27):
                            v.wait_ge(pe.sem, p27)
                            sl = slice(h * T + gsl.start, h * T + gsl.stop)
                            v.tensor_mul(out=XFFN[:, sl], in0=XFFN[:, sl], in1=PB[2][:, 0:512])
                            ve.tick(v.engine_nop(), 1)
                            v.wait_ge(ve.sem, ve.n)
                            ve.tick(v.tensor_sub(out=XFFN[:, sl], in0=XFFN[:, sl],
                                                 in1=PB[3][:, 0:512]), 1)
                        need_ve2 = ve.n

                        @blk.sync
                        def _(sy, h=h, gsl=gsl, need_ve2=need_ve2):
                            sy.wait_ge(ve.sem, need_ve2)
                            sl = slice(h * T + gsl.start, h * T + gsl.stop)
                            st.tick(sy.dma_start(out=o_out[h * 128:(h + 1) * 128, gsl],
                                                 in_=XFFN[:, sl]), 16)

            @blk.sync
            def _(sy):
                sy.wait_ge(st.sem, st.n)
        else:
            @blk.sync
            def _(sy):
                sy.wait_ge(ve.sem, ve.n)
                sy.wait_ge(se.sem, se.n)
                for h in range(8):
                    st.tick(sy.dma_start(out=o_out[h * 128:(h + 1) * 128, :],
                                         in_=XFFN[:, h * T:(h + 1) * T]), 16)
                sy.wait_ge(st.sem, st.n)

    return nc


_NC_CACHE = {}


def _idxdq():
    q = np.zeros((128, 64), np.int32)
    p32 = np.arange(128) // 32
    for it in range(8):
        q[:, it] = 4 * it + p32
        q[:, 8 + it] = 32 + 4 * it + p32
    for it in range(24):
        q[:, 16 + it] = 4 * it + p32
        q[:, 40 + it] = 96 + 4 * it + p32
    return q


def _consts():
    IDENT = np.eye(128, dtype=np.float32)
    s = np.arange(128)
    CUMT = (s[:, None] <= s[None, :]).astype(np.float32)
    MASKT = np.where(s[:, None] < s[None, :], 0.0, -1e30).astype(np.float32)
    ONES1 = np.ones((128, 128), np.float32)
    IDX32 = (np.arange(128) // 32).astype(np.int32).reshape(128, 1)
    return IDENT, CUMT, MASKT, ONES1, IDX32


def kernel(L0_attn_x, L0_attn_Wq, L0_attn_bq, L0_attn_Wk, L0_attn_bk,
           L0_attn_Wv, L0_attn_bv, L0_attn_Wa, L0_attn_ba, L0_attn_Wb,
           L0_attn_bb, L0_attn_Wg, L0_attn_bg, L0_attn_Wo, L0_attn_bo,
           L0_ffn_w_gate, L0_ffn_w_up, L0_ffn_w_down):
    x = np.asarray(L0_attn_x, np.float32)
    f = lambda a: np.ascontiguousarray(np.asarray(a, np.float32))
    Wq, Wk, Wv, Wg = f(L0_attn_Wq), f(L0_attn_Wk), f(L0_attn_Wv), f(L0_attn_Wg)
    Wa, Wb, Wo = f(L0_attn_Wa), f(L0_attn_Wb), f(L0_attn_Wo)
    bq, bk, bv, bg = f(L0_attn_bq), f(L0_attn_bk), f(L0_attn_bv), f(L0_attn_bg)
    ba, bb, bo = f(L0_attn_ba), f(L0_attn_bb), f(L0_attn_bo)
    wgate, wup, wdown = f(L0_ffn_w_gate), f(L0_ffn_w_up), f(L0_ffn_w_down)

    IDENT, CUMT, MASKT, ONES1, IDX32 = _consts()
    shared = {
        "wqT": np.ascontiguousarray(Wq.T), "wkT": np.ascontiguousarray(Wk.T),
        "wvT": np.ascontiguousarray(Wv.T), "wgT": np.ascontiguousarray(Wg.T),
        "wabT": np.ascontiguousarray(np.concatenate([Wa, Wb], 0).T),
        "woT": np.ascontiguousarray(Wo.T),
        "bqc": bq.reshape(C, 1), "bkc": bk.reshape(C, 1),
        "bvc": bv.reshape(C, 1), "bgc": bg.reshape(C, 1),
        "babb": np.ascontiguousarray(
            np.concatenate([np.full((C, 1), ba[0], np.float32),
                            np.full((C, 1), bb[0], np.float32)], 1)),
        "boc": np.ascontiguousarray(bo.reshape(8, 128).T),
        "IDENT": IDENT, "CUMT": CUMT, "MASKT": MASKT, "ONES1": ONES1,
        "IDX32": IDX32, "IDXDQ": _idxdq(),
    }

    if "nc" not in _NC_CACHE:
        _NC_CACHE["nc"] = build_nc()
    nc = _NC_CACHE["nc"]

    in_maps = []
    for r in range(NCORE):
        seq, t0 = r // 4, (r % 4) * T
        grp = r % 4
        base = (r // 4) * 4
        pdidx = np.zeros((128, 6), np.int32)
        for k in range(3):
            rowbase = (base + k) * 256
            pdidx[:, 2 * k] = rowbase + np.arange(128)
            pdidx[:, 2 * k + 1] = rowbase + 128 + np.arange(128)
        msel = np.zeros((128, 4), np.float32)
        msel[:, grp] = 1.0
        m = dict(shared)
        m["xT"] = np.ascontiguousarray(x[seq, t0:t0 + T, :].T)
        m["up_o"] = np.ascontiguousarray(wup[r * 384:(r + 1) * 384, :])
        m["up_t"] = np.ascontiguousarray(wup[r * 384:(r + 1) * 384, :].T)
        m["gt_o"] = np.ascontiguousarray(wgate[r * 384:(r + 1) * 384, :])
        m["gt_t"] = np.ascontiguousarray(wgate[r * 384:(r + 1) * 384, :].T)
        m["dn_o"] = np.ascontiguousarray(wdown[r * 128:(r + 1) * 128, :])
        m["dn_t"] = np.ascontiguousarray(wdown[r * 128:(r + 1) * 128, :].T)
        m["PDIDX"] = pdidx
        m["MSEL"] = msel
        in_maps.append(m)

    res = run_bass_kernel_spmd(nc, in_maps, core_ids=list(range(NCORE)))
    out = np.empty((2, 8192, 1024), np.float32)
    for r in range(NCORE):
        seq, t0 = r // 4, (r % 4) * T
        out[seq, t0:t0 + T, :] = np.asarray(res.results[r]["o_out"]).T
    if DBG:
        kernel.last_results = res.results
    return out


# version 10
# speedup vs baseline: 1.0261x; 1.0255x over previous
import numpy as np
import concourse.bass as bass
import concourse.mybir as mybir
from concourse.bass_utils import run_bass_kernel_spmd
from contextlib import ExitStack

F32 = mybir.dt.float32
BF16 = mybir.dt.bfloat16
I32 = mybir.dt.int32
AF = mybir.ActivationFunctionType
ALU = mybir.AluOpType
AX = mybir.AxisListType

NCORE = 8
T = 2048
D = 1024
C = 128
NCH = 16
ND = 8
FF = 3072
NOT = 24
QMAX = 32767.0

DBG = False
STAGE = 9


class Ctr:
    def __init__(self, sem):
        self.sem = sem
        self.n = 0

    def tick(self, inst, k=1):
        inst.then_inc(self.sem, k)
        self.n += k
        return self.n


def build_nc():
    nc = bass.Bass("TRN2", target_bir_lowering=False, debug=False)
    di = {}

    def inp(name, shape, dt=F32):
        di[name] = nc.dram_tensor(name, list(shape), dt, kind="ExternalInput").ap()
        return di[name]

    xT = inp("xT", [D, T])
    wqT = inp("wqT", [D, C]); wkT = inp("wkT", [D, C])
    wvT = inp("wvT", [D, C]); wgT = inp("wgT", [D, C])
    wabT = inp("wabT", [D, 2])
    woT = inp("woT", [C, D])
    bqc = inp("bqc", [C, 1]); bkc = inp("bkc", [C, 1])
    bvc = inp("bvc", [C, 1]); bgc = inp("bgc", [C, 1])
    babb = inp("babb", [C, 2])
    boc = inp("boc", [128, 8])
    up_o = inp("up_o", [384, D]); up_t = inp("up_t", [D, 384])
    gt_o = inp("gt_o", [384, D]); gt_t = inp("gt_t", [D, 384])
    dn_o = inp("dn_o", [C, FF]); dn_t = inp("dn_t", [FF, C])
    IDENT = inp("IDENT", [128, 128])
    CUMT = inp("CUMT", [128, 128])
    MASKT = inp("MASKT", [128, 128])
    ONES1 = inp("ONES1", [128, 128])
    IDX32 = inp("IDX32", [128, 1], I32)
    IDXDQ = inp("IDXDQ", [128, 64], I32)
    PDIDX = inp("PDIDX", [128, 6], I32)
    MSEL = inp("MSEL", [128, 4])

    o_out = nc.dram_tensor("o_out", [D, T], F32, kind="ExternalOutput").ap()
    dbg = {}
    if DBG:
        for nm, sh, dt in [("dq", [C, T], BF16), ("dkh", [C, T], BF16),
                           ("dv", [C, T], BF16), ("dg", [C, T], F32),
                           ("dla", [128, NCH], F32), ("dbc", [128, NCH], F32),
                           ("dO", [C, T], BF16), ("dxffn", [D, T], F32),
                           ("dwup", [D, 384], BF16), ("duvw", [128, 256], BF16),
                           ("dpct", [128, T], BF16), ("ddcf", [128, T], BF16),
                           ("duv2", [128, T], BF16), ("dwfu", [128, T], BF16),
                           ("dkrf", [128, T], BF16), ("dntl", [128, T], BF16),
                           ("dxtl", [128, T], BF16),
                           ("dmst", [128, 128], BF16)]:
            dbg[nm] = nc.dram_tensor(nm, sh, dt, kind="ExternalOutput").ap()

    b_su = nc.dram_tensor("b_su", [64, 384], F32)
    b_sg = nc.dram_tensor("b_sg", [64, 384], F32)
    b_sd = nc.dram_tensor("b_sd", [192, C], F32)
    b_dqu = nc.dram_tensor("b_dqu", [D, 384], BF16)
    b_dqg = nc.dram_tensor("b_dqg", [D, 384], BF16)
    b_dqd = nc.dram_tensor("b_dqd", [FF, C], BF16)
    b_wu = nc.dram_tensor("b_wu", [NCORE * D, 384], BF16)
    b_wg = nc.dram_tensor("b_wg", [NCORE * D, 384], BF16)
    b_wd = nc.dram_tensor("b_wd", [NCORE * FF, C], BF16)
    b_pd_in = nc.dram_tensor("b_pd_in", [256, 128], BF16)
    b_pd_out = nc.dram_tensor("b_pd_out", [NCORE * 256, 128], BF16)
    b_mx_in = nc.dram_tensor("b_mx_in", [1, 2], F32)
    b_mx_out = nc.dram_tensor("b_mx_out", [1, 2], F32)
    b_up = nc.dram_tensor("b_up", [FF, T], BF16)
    b_gate = nc.dram_tensor("b_gate", [FF, T], BF16)
    b_g2 = nc.dram_tensor("b_g2", [FF, T], BF16)
    b_opre = nc.dram_tensor("b_opre", [FF, T], BF16)

    with ExitStack() as stk:
        e = stk.enter_context
        GEN = e(nc.sbuf_tensor("GEN", [128, 40960], BF16))
        GENF = e(nc.sbuf_tensor("GENF", [128, 6144], F32))
        XFFN = e(nc.sbuf_tensor("XFFN", [128, 8 * T], F32))
        ROWS = e(nc.sbuf_tensor("ROWS", [128, 2048], F32))
        COLS = e(nc.sbuf_tensor("COLS", [128, 256], F32))
        CONST = e(nc.sbuf_tensor("CONST", [128, 512], F32))
        CONB = e(nc.sbuf_tensor("CONB", [128, 260], BF16))
        SM = e(nc.sbuf_tensor("SM", [128, 64], F32))
        SMI = e(nc.sbuf_tensor("SMI", [128, 8], I32))
        SMIQ = e(nc.sbuf_tensor("SMIQ", [128, 64], I32))
        STF = e(nc.sbuf_tensor("STF", [128, 2592], F32))
        STB = e(nc.sbuf_tensor("STB", [128, 4096], BF16))
        MTF = e(nc.sbuf_tensor("MTF", [128, 128], F32))
        MTB = e(nc.sbuf_tensor("MTB", [128, 1664], BF16))
        AWB = e(nc.sbuf_tensor("AWB", [128, 1040], BF16))
        PB = [e(nc.psum_tensor("pb%d" % k, [128, 512], F32)) for k in range(8)]

        STB1 = STB[:, 0:2048]
        STB2 = STB[:, 2048:4096]

        def gb(o, n):
            return GEN[:, o:o + n]
        XBF = [gb(2048 * k, 2048) for k in range(8)]
        QB = gb(16384, 2048); QA = gb(18432, 2048)
        KH = gb(20480, 2048); K2 = gb(22528, 2048)
        VB = gb(24576, 2048)
        UVB = gb(26624, 2048)
        WTN = gb(28672, 2048)
        KTT = gb(30720, 2048)
        OBF = gb(0, 2048)
        SCR = [STB[:, 1024 + 256 * k:1280 + 256 * k] for k in range(10)]
        OJ = [gb(512 * j, 512) for j in range(NOT)]

        GF = GENF[:, 0:2048]
        EIF = GENF[:, 2048:4096]
        FSC = GENF[:, 4096:6144]
        SIGF = FSC

        INT1 = GEN[:, 20480:24576].bitcast(I32)
        INT2 = GEN[:, 16384:20480].bitcast(I32)
        INT1o = GEN[:, 24576:28672].bitcast(I32)
        INT2o = GEN[:, 28672:32768].bitcast(I32)
        SIGFo = GENF[:, 0:2048]

        LA_ROW = ROWS[0:1, :]
        RTMP = GENF[0:1, 2048:4096]
        RTMP2 = ROWS[64:65, :]
        MROW = ROWS[0:1, :]
        SQR = GENF[0:1, 0:2048]
        VARR = GENF[0:1, 2048:4096]
        RSTD = GENF[0:1, 0:2048]
        MRSTD = GENF[0:1, 2048:4096]

        LAC = COLS[:, 0:16]; BCC = COLS[:, 16:32]; ACOL = COLS[:, 32:48]
        BAC = COLS[:, 48:64]; RC_ = COLS[:, 64:80]; ALC = COLS[:, 80:96]
        ZAC = COLS[:, 96:112]; ZBC = COLS[:, 112:128]
        CT1 = COLS[:, 128:144]; CT2 = COLS[:, 144:160]
        BIAS4 = COLS[:, 160:164]
        BAB = COLS[:, 164:166]
        BOC = COLS[:, 166:174]
        SCL = COLS[:, 174:206]
        NEGC = COLS[:, 206:207]
        MSL = COLS[:, 208:212]

        IDE = CONST[:, 0:128]; CUM = CONST[:, 128:256]
        MSK = CONST[:, 256:384]
        ONE = CONST[0:1, 384:512]
        ONE32 = CONST[32:33, 384:512]
        ONE64 = CONST[64:65, 384:512]
        ONE96 = CONST[96:97, 384:512]
        IDEB = CONB[:, 0:128]; ONEB = CONB[0:1, 128:256]
        ONECB = CONB[:, 256:257]

        def wq_t(d): return GEN[:, 26624 + d * 128:26624 + (d + 1) * 128]
        def wk_t(d): return GEN[:, 27648 + d * 128:27648 + (d + 1) * 128]
        def wv_t(d): return GEN[:, 28672 + d * 128:28672 + (d + 1) * 128]
        def wg_t(d): return GEN[:, 29696 + d * 128:29696 + (d + 1) * 128]
        def wo_t(h): return AWB[:, h * 128:(h + 1) * 128]
        def wab_t(d): return AWB[:, 1024 + 2 * d:1026 + 2 * d]

        MT_F = MTF[:, 0:128]
        MT_B = MTB[:, 0:128]
        M_B = MTB[:, 128:256]
        PCM = MTB[:, 256:384]
        PCMT = MTB[:, 384:512]
        DCM = MTB[:, 512:640]
        DCB = MTB[:, 640:768]
        GPD = [MTB[:, 768 + k * 128:896 + k * 128] for k in range(6)]
        MACC = MTB[:, 1536:1664]

        def pbb(k, n=128, off=0):
            return PB[k].ap().bitcast(BF16)[:, off:off + n]

        ld = Ctr(e(nc.semaphore("ld")))
        sw = Ctr(e(nc.semaphore("sw")))
        gq = Ctr(e(nc.semaphore("gq")))
        pe = Ctr(e(nc.semaphore("pe")))
        ve = Ctr(e(nc.semaphore("ve")))
        se = Ctr(e(nc.semaphore("se")))
        po = Ctr(e(nc.semaphore("po")))
        st = Ctr(e(nc.semaphore("st")))
        ldb = Ctr(e(nc.semaphore("ldb")))
        gqb = Ctr(e(nc.semaphore("gqb")))
        swb = Ctr(e(nc.semaphore("swb")))
        lp5 = Ctr(e(nc.semaphore("lp5")))
        lp5b = Ctr(e(nc.semaphore("lp5b")))
        ldw2 = Ctr(e(nc.semaphore("ldw2")))
        ldw2b = Ctr(e(nc.semaphore("ldw2b")))
        sp3 = Ctr(e(nc.semaphore("sp3")))
        sp3b = Ctr(e(nc.semaphore("sp3b")))
        cc = [Ctr(e(nc.semaphore("cc%d" % k))) for k in range(7)]
        blk = e(nc.Block())

        @blk.sync
        def _(sy):
            ld.tick(sy.dma_start(out=IDE, in_=IDENT[:, :]), 16)
            ld.tick(sy.dma_start(out=CUM, in_=CUMT[:, :]), 16)
            ld.tick(sy.dma_start(out=MSK, in_=MASKT[:, :]), 16)
            ld.tick(sy.dma_start(out=CONST[:, 384:512], in_=ONES1[:, :]), 16)
            ld.tick(sy.dma_start(out=SMI[:, 0:1], in_=IDX32[:, :]), 16)
            ld.tick(sy.dma_start(out=SMI[:, 1:7], in_=PDIDX[:, :]), 16)
            ld.tick(sy.dma_start(out=SMIQ[:, :], in_=IDXDQ[:, :]), 16)
            ld.tick(sy.dma_start(out=MSL, in_=MSEL[:, :]), 16)
            ld.tick(sy.dma_start(out=BIAS4[:, 0:1], in_=bqc[:, :]), 16)
            ld.tick(sy.dma_start(out=BIAS4[:, 1:2], in_=bkc[:, :]), 16)
            ld.tick(sy.dma_start(out=BIAS4[:, 2:3], in_=bvc[:, :]), 16)
            ld.tick(sy.dma_start(out=BIAS4[:, 3:4], in_=bgc[:, :]), 16)
            ld.tick(sy.dma_start(out=BAB, in_=babb[:, :]), 16)
            ld.tick(sy.dma_start(out=BOC, in_=boc[:, :]), 16)
            for d in range(8):
                ld.tick(sy.dma_start(out=XFFN[:, d * T:(d + 1) * T],
                                     in_=xT[d * 128:(d + 1) * 128, :]), 16)
            for d in range(8):
                ld.tick(sy.dma_start(out=GENF[:, 512 * d:512 * d + 128],
                                     in_=wqT[d * 128:(d + 1) * 128, :]), 16)
                ld.tick(sy.dma_start(out=GENF[:, 512 * d + 128:512 * d + 256],
                                     in_=wkT[d * 128:(d + 1) * 128, :]), 16)
                ld.tick(sy.dma_start(out=GENF[:, 512 * d + 256:512 * d + 384],
                                     in_=wvT[d * 128:(d + 1) * 128, :]), 16)
                ld.tick(sy.dma_start(out=GENF[:, 512 * d + 384:512 * d + 512],
                                     in_=wgT[d * 128:(d + 1) * 128, :]), 16)
            for d in range(8):
                ld.tick(sy.dma_start(out=FSC[:, 2 * d:2 * d + 2],
                                     in_=wabT[d * 128:(d + 1) * 128, :]), 16)
            for h in range(8):
                ld.tick(sy.dma_start(out=FSC[:, 16 + 128 * h:16 + 128 * (h + 1)],
                                     in_=woT[:, h * 128:(h + 1) * 128]), 16)

        ph0_ld = ld.n

        @blk.vector
        def _(v):
            v.wait_ge(ld.sem, ph0_ld)
            for d in range(8):
                v.tensor_copy(out=XBF[d], in_=XFFN[:, d * T:(d + 1) * T])
                v.tensor_copy(out=wq_t(d), in_=GENF[:, 512 * d:512 * d + 128])
                v.tensor_copy(out=wk_t(d), in_=GENF[:, 512 * d + 128:512 * d + 256])
                v.tensor_copy(out=wv_t(d), in_=GENF[:, 512 * d + 256:512 * d + 384])
                v.tensor_copy(out=wg_t(d), in_=GENF[:, 512 * d + 384:512 * d + 512])
                v.tensor_copy(out=wab_t(d), in_=FSC[:, 2 * d:2 * d + 2])
            for h in range(8):
                v.tensor_copy(out=wo_t(h), in_=FSC[:, 16 + 128 * h:16 + 128 * (h + 1)])
            v.tensor_copy(out=IDEB, in_=IDE)
            v.tensor_copy(out=ONEB, in_=ONE)
            v.memset(ONECB, 1.0)
            v.memset(NEGC, -1.0)
            last = v.memset(SM[:, 0:2], 0.0)
            ve.tick(last, 1)

        ph0_ve = ve.n

        dq_cfg = [(up_o, up_t, b_su, b_dqu, 3, 8, 384, 1024, 32, 0, 8),
                  (gt_o, gt_t, b_sg, b_dqg, 3, 8, 384, 1024, 32, 0, 8),
                  (dn_o, dn_t, b_sd, b_dqd, 1, 24, 128, 3072, 96, 16, 40)]

        for (sh_o, sh_t, bscl, bdq, n_ot, n_it, NO, NI, NB, ixs, ixi) in dq_cfg:
            segw = NI if NI <= 1536 else 1536
            nseg = NI // segw
            nbs = NB // nseg
            for ot3 in range(n_ot):
                for seg in range(nseg):
                    @blk.sync
                    def _(sy, sh_o=sh_o, ot3=ot3, seg=seg, segw=segw):
                        sy.wait_ge(ve.sem, ve.n)
                        ld.tick(sy.dma_start(
                            out=STF[:, 0:segw],
                            in_=sh_o[ot3 * 128:(ot3 + 1) * 128,
                                     seg * segw:(seg + 1) * segw]), 16)
                    need_ld = ld.n

                    @blk.vector
                    def _(v, seg=seg, segw=segw, nbs=nbs, need_ld=need_ld):
                        v.wait_ge(ld.sem, need_ld)
                        v.wait_ge(se.sem, se.n)
                        ve.tick(v.tensor_reduce(
                            out=STF[:, 1600 + seg * nbs:1600 + (seg + 1) * nbs]
                                .rearrange("p (n w) -> p n w", w=1),
                            in_=STF[:, 0:segw].rearrange("p (n w) -> p n w", w=32),
                            axis=AX.X, op=ALU.max, apply_absolute_value=True), 1)

                @blk.vector
                def _(v, NB=NB):
                    v.wait_ge(ve.sem, ve.n)
                    v.tensor_scalar(out=STF[:, 1696:1696 + NB], in0=STF[:, 1600:1600 + NB],
                                    scalar1=1.0 / 7.0, scalar2=1e-12,
                                    op0=ALU.mult, op1=ALU.max)
                    ve.tick(v.engine_nop(), 1)
                    v.wait_ge(ve.sem, ve.n)
                    ve.tick(v.reciprocal(out=STF[:, 1792:1792 + NB],
                                         in_=STF[:, 1696:1696 + NB]), 1)
                need_ve = ve.n

                @blk.tensor
                def _(t, NB=NB, need_ve=need_ve):
                    t.wait_ge(ve.sem, need_ve)
                    t.wait_ge(se.sem, se.n)
                    t.transpose(PB[0][0:NB, 0:128], STF[:, 1696:1696 + NB], IDE)
                    pe.tick(t.transpose(PB[1][0:NB, 0:128], STF[:, 1792:1792 + NB], IDE), 1)
                need_pe = pe.n

                @blk.scalar
                def _(s, NB=NB, need_pe=need_pe):
                    s.wait_ge(pe.sem, need_pe)
                    s.wait_ge(sw.sem, sw.n)
                    s.copy(out=STF[0:NB, 1888:2016], in_=PB[0][0:NB, 0:128])
                    se.tick(s.copy(out=STF[0:NB, 2016:2144], in_=PB[1][0:NB, 0:128]), 1)
                need_se = se.n

                @blk.scalar
                def _(s, bscl=bscl, ot3=ot3, NB=NB, need_se=need_se):
                    s.wait_ge(se.sem, need_se)
                    sw.tick(s.dma_start(out=bscl[0:NB, ot3 * 128:(ot3 + 1) * 128],
                                        in_=STF[0:NB, 1888:2016]), 16)
                    sw.tick(s.dma_start(out=bscl[NB:2 * NB, ot3 * 128:(ot3 + 1) * 128],
                                        in_=STF[0:NB, 2016:2144]), 16)

            dqv = {}
            dqw = {}
            base_sw = sw.n
            for it in range(n_it):
                par = it % 2
                sl0, sl1, sl2 = (0, 512, 1024) if par == 0 else (1408, 1792, 2176)
                stb_off = 0 if par == 0 else 512
                ints = INT1 if par == 0 else INT2
                LDP = ld if par == 0 else ldb
                GQP = gq if par == 0 else gqb
                SWP = sw if par == 0 else swb

                @blk.sync
                def _(sy, sh_t=sh_t, it=it, NO=NO, sl0=sl0, LDP=LDP):
                    if it < 2:
                        sy.wait_ge(ve.sem, ve.n)
                        sy.wait_ge(sw.sem, sw.n)
                        sy.wait_ge(swb.sem, swb.n)
                    else:
                        sy.wait_ge(ve.sem, dqv[it - 2])
                    LDP.tick(sy.dma_start(out=STF[:, sl0:sl0 + NO],
                                          in_=sh_t[it * 128:(it + 1) * 128, :]), 16)
                need_ld = LDP.n

                @blk.gpsimd
                def _(g, bscl=bscl, it=it, NO=NO, NB=NB, ixs=ixs, ixi=ixi,
                      sl1=sl1, sl2=sl2, base_sw=base_sw, GQP=GQP):
                    if it < 2:
                        g.wait_ge(sw.sem, base_sw)
                    else:
                        g.wait_ge(ve.sem, dqv[it - 2])
                    GQP.tick(g.indirect_dma_start(
                        out=STF[:, sl1:sl1 + NO], out_offset=None,
                        in_=bscl.ap(),
                        in_offset=bass.IndirectOffsetOnAxis(
                            ap=SMIQ[:, ixs + it:ixs + it + 1], axis=0)), 16)
                    GQP.tick(g.indirect_dma_start(
                        out=STF[:, sl2:sl2 + NO], out_offset=None,
                        in_=bscl.ap(),
                        in_offset=bass.IndirectOffsetOnAxis(
                            ap=SMIQ[:, ixi + it:ixi + it + 1], axis=0)), 16)
                need_gq = GQP.n

                @blk.vector
                def _(v, it=it, NO=NO, sl0=sl0, sl1=sl1, sl2=sl2, stb_off=stb_off,
                      ints=ints, need_ld=need_ld, need_gq=need_gq, LDP=LDP, GQP=GQP,
                      SWP2=(sw if it % 2 == 0 else swb)):
                    v.wait_ge(LDP.sem, need_ld)
                    v.wait_ge(GQP.sem, need_gq)
                    if it >= 2:
                        v.wait_ge(SWP2.sem, dqw[it - 2])
                    v.tensor_tensor(out=ints[:, 0:NO], in0=STF[:, sl0:sl0 + NO],
                                    in1=STF[:, sl2:sl2 + NO], op=ALU.mult)
                    ve.tick(v.engine_nop(), 1)
                    v.wait_ge(ve.sem, ve.n)
                    ve.tick(v.tensor_tensor(out=STB1[:, stb_off:stb_off + NO],
                                            in0=ints[:, 0:NO],
                                            in1=STF[:, sl1:sl1 + NO],
                                            op=ALU.mult), 1)
                dqv[it] = ve.n
                need_ve = ve.n

                @blk.scalar
                def _(s, bdq=bdq, it=it, NO=NO, stb_off=stb_off, need_ve=need_ve,
                      SWP=SWP):
                    s.wait_ge(ve.sem, need_ve)
                    SWP.tick(s.dma_start(out=bdq[it * 128:(it + 1) * 128, :],
                                         in_=STB1[:, stb_off:stb_off + NO]), 16)
                dqw[it] = SWP.n

        @blk.gpsimd
        def _(g):
            g.wait_ge(sw.sem, sw.n)
            g.wait_ge(swb.sem, swb.n)
            cc[0].tick(g.collective_compute(
                "AllGather", ALU.bypass, replica_groups=[list(range(NCORE))],
                ins=[b_dqu.ap()], outs=[b_wu.ap()]), 1)
            cc[0].tick(g.collective_compute(
                "AllGather", ALU.bypass, replica_groups=[list(range(NCORE))],
                ins=[b_dqg.ap()], outs=[b_wg.ap()]), 1)
            cc[0].tick(g.collective_compute(
                "AllGather", ALU.bypass, replica_groups=[list(range(NCORE))],
                ins=[b_dqd.ap()], outs=[b_wd.ap()]), 1)

        if DBG:
            @blk.sync
            def _(sy):
                sy.wait_ge(sw.sem, sw.n)
                sy.wait_ge(swb.sem, swb.n)
                st.tick(sy.dma_start(out=dbg["dwup"][:, :], in_=b_dqu.ap()[:, :]), 16)

        if STAGE >= 1:
            @blk.tensor
            def _(t):
                t.wait_ge(ve.sem, ph0_ve)
                t.wait_ge(se.sem, se.n)
                last = None
                for c_ in range(16):
                    for d in range(8):
                        last = t.matmul(PB[4][:, 2 * c_:2 * c_ + 2],
                                        XBF[d][:, c_ * 128:(c_ + 1) * 128], wab_t(d),
                                        start=(d == 0), stop=(d == 7))
                pe.tick(last, 1)
            pe_zab = pe.n

            @blk.vector
            def _(v):
                v.wait_ge(pe.sem, pe_zab)
                v.tensor_copy(out=ZAC,
                              in_=PB[4].ap()[:, 0:32].rearrange("p (c t) -> p t c", t=2)[:, 0, :])
                ve.tick(v.tensor_copy(
                    out=ZBC,
                    in_=PB[4].ap()[:, 0:32].rearrange("p (c t) -> p t c", t=2)[:, 1, :]), 1)
            ve_zab = ve.n

            for tc in range(4):
                @blk.tensor
                def _(t, tc=tc):
                    t.wait_ge(ve.sem, ph0_ve)
                    t.wait_ge(se.sem, se.n)
                    last = None
                    for d in range(8):
                        last = t.matmul(PB[5][0:1, 0:512], wab_t(d)[:, 1:2],
                                        XBF[d][:, tc * 512:(tc + 1) * 512],
                                        start=(d == 0), stop=(d == 7))
                    pe.tick(last, 1)
                need_pe = pe.n

                @blk.scalar
                def _(s, tc=tc, need_pe=need_pe):
                    s.wait_ge(pe.sem, need_pe)
                    se.tick(s.activation(out=RTMP[:, tc * 512:(tc + 1) * 512],
                                         in_=PB[5][0:1, 0:512], func=AF.Sigmoid,
                                         bias=BAB[0:1, 1:2]), 1)

            @blk.scalar
            def _(s):
                s.wait_ge(ve.sem, ve_zab)
                se.tick(s.activation(out=CT1, in_=ZAC, func=AF.Abs, bias=BAB[:, 0:1]), 1)
                s.wait_ge(se.sem, se.n)
                se.tick(s.activation(out=CT2, in_=CT1, func=AF.Exp, scale=NEGC), 1)
            se_c1 = se.n

            @blk.vector
            def _(v):
                v.wait_ge(se.sem, se_c1)
                ve.tick(v.tensor_scalar_add(out=CT2, in0=CT2, scalar1=1.0), 1)
            ve_c1 = ve.n

            @blk.scalar
            def _(s):
                s.wait_ge(ve.sem, ve_c1)
                se.tick(s.activation(out=CT2, in_=CT2, func=AF.Ln), 1)
            se_c2 = se.n

            @blk.vector
            def _(v):
                v.wait_ge(se.sem, se_c2)
                v.tensor_scalar(out=CT1, in0=ZAC, scalar1=BAB[:, 0:1], scalar2=0.0,
                                op0=ALU.add, op1=ALU.min)
                ve.tick(v.engine_nop(), 1)
                v.wait_ge(ve.sem, ve.n)
                ve.tick(v.tensor_sub(out=CT1, in0=CT1, in1=CT2), 1)
            ve_la = ve.n

            @blk.tensor
            def _(t):
                t.wait_ge(ve.sem, ve_la)
                pe.tick(t.matmul(PB[6][:, 0:16], CUM, CT1, start=True, stop=True), 1)
            pe_cum = pe.n

            @blk.vector
            def _(v):
                v.wait_ge(pe.sem, pe_cum)
                ve.tick(v.tensor_copy(out=LAC, in_=PB[6][:, 0:16]), 1)
            ve_lac = ve.n

            @blk.scalar
            def _(s):
                s.wait_ge(ve.sem, ve_zab)
                se.tick(s.activation(out=BCC, in_=ZBC, func=AF.Sigmoid, bias=BAB[:, 1:2]), 1)
                s.wait_ge(ve.sem, ve_lac)
                se.tick(s.activation(out=ACOL, in_=LAC, func=AF.Exp), 1)
            se_bc = se.n

            @blk.vector
            def _(v):
                v.wait_ge(se.sem, se_bc)
                ve.tick(v.tensor_mul(out=BAC, in0=BCC, in1=ACOL), 1)

            @blk.gpsimd
            def _(g):
                g.wait_ge(ve.sem, ve_lac)
                gq.tick(g.dma_start(out=SM[0:1, 16:32], in_=LAC[127:128, :]), 16)
            gq_lal = gq.n

            @blk.tensor
            def _(t):
                t.wait_ge(gq.sem, gq_lal)
                t.wait_ge(ve.sem, ve.n)
                pe.tick(t.matmul(PB[6][:, 16:32], ONE, SM[0:1, 16:32],
                                 start=True, stop=True), 1)
            pe_lal = pe.n

            @blk.vector
            def _(v):
                v.wait_ge(pe.sem, pe_lal)
                ve.tick(v.tensor_sub(out=CT1, in0=PB[6][:, 16:32], in1=LAC), 1)
            ve_rc = ve.n

            @blk.scalar
            def _(s):
                s.wait_ge(ve.sem, ve_rc)
                se.tick(s.activation(out=RC_, in_=CT1, func=AF.Exp), 1)
                se.tick(s.activation(out=ALC, in_=PB[6][:, 16:32], func=AF.Exp), 1)
            se_rc = se.n

            for c_ in range(16):
                @blk.tensor
                def _(t, c_=c_):
                    t.wait_ge(ve.sem, ve.n)
                    if c_ == 0:
                        t.wait_ge(ve.sem, ve_lac)
                    pe.tick(t.matmul(PB[7][0:1, 0:128], LAC[:, c_:c_ + 1], IDE,
                                     start=True, stop=True), 1)
                need_pe = pe.n

                @blk.vector
                def _(v, c_=c_, need_pe=need_pe):
                    v.wait_ge(pe.sem, need_pe)
                    ve.tick(v.tensor_copy(out=LA_ROW[:, c_ * 128:(c_ + 1) * 128],
                                          in_=PB[7][0:1, 0:128]), 1)
            ve_larow = ve.n

            ve_k2d = {}
            se_sqd = {}
            for tc in range(4):
                sl = slice(tc * 512, (tc + 1) * 512)

                @blk.tensor
                def _(t, tc=tc, sl=sl, ve_now=ve.n, se_now=se.n):
                    if tc == 0:
                        t.wait_ge(ve.sem, ve_now)
                        t.wait_ge(se.sem, se_now)
                    else:
                        t.wait_ge(ve.sem, ve_k2d[tc - 1])
                        t.wait_ge(se.sem, se_sqd[tc - 1])
                    last = None
                    for d in range(8):
                        st_ = (d == 0); sp = (d == 7)
                        t.matmul(PB[0][:, 0:512], wq_t(d), XBF[d][:, sl], start=st_, stop=sp)
                        t.matmul(PB[1][:, 0:512], wk_t(d), XBF[d][:, sl], start=st_, stop=sp)
                        t.matmul(PB[2][:, 0:512], wv_t(d), XBF[d][:, sl], start=st_, stop=sp)
                        last = t.matmul(PB[3][:, 0:512], wg_t(d), XBF[d][:, sl],
                                        start=st_, stop=sp)
                    pe.tick(last, 1)
                pe_proj = pe.n

                @blk.vector
                def _(v, sl=sl, pe_proj=pe_proj):
                    v.wait_ge(pe.sem, pe_proj)
                    v.tensor_scalar(out=QB[:, sl], in0=PB[0][:, 0:512],
                                    scalar1=BIAS4[:, 0:1], scalar2=None, op0=ALU.add)
                    v.tensor_scalar(out=VB[:, sl], in0=PB[2][:, 0:512],
                                    scalar1=BIAS4[:, 2:3], scalar2=None, op0=ALU.add)
                    ve.tick(v.tensor_scalar(out=STF[:, 0:512], in0=PB[1][:, 0:512],
                                            scalar1=BIAS4[:, 1:2], scalar2=None,
                                            op0=ALU.add), 1)
                ve_kf = ve.n

                @blk.scalar
                def _(s, sl=sl, pe_proj=pe_proj, ve_kf=ve_kf):
                    s.wait_ge(pe.sem, pe_proj)
                    se.tick(s.activation(out=GF[:, sl], in_=PB[3][:, 0:512],
                                         func=AF.Sigmoid, bias=BIAS4[:, 3:4]), 1)
                    s.wait_ge(ve.sem, ve_kf)
                    se.tick(s.activation(out=STB1[:, 0:512], in_=STF[:, 0:512],
                                         func=AF.Square), 1)
                se_sq = se.n
                se_sqd[tc] = se.n

                @blk.tensor
                def _(t, se_sq=se_sq):
                    t.wait_ge(se.sem, se_sq)
                    pe.tick(t.matmul(PB[5][0:1, 0:512], ONECB, STB1[:, 0:512],
                                     start=True, stop=True), 1)
                pe_nrm = pe.n

                @blk.scalar
                def _(s, sl=sl, pe_nrm=pe_nrm):
                    s.wait_ge(pe.sem, pe_nrm)
                    se.tick(s.activation(out=RTMP2[:, sl], in_=PB[5][0:1, 0:512],
                                         func=AF.Sqrt), 1)
                se_nrm = se.n

                @blk.vector
                def _(v, sl=sl, se_nrm=se_nrm):
                    v.wait_ge(se.sem, se_nrm)
                    ve.tick(v.tensor_scalar_max(out=RTMP2[:, sl], in0=RTMP2[:, sl],
                                                scalar1=1e-12), 1)
                    v.wait_ge(ve.sem, ve.n)
                    ve.tick(v.reciprocal(out=RTMP2[:, sl], in_=RTMP2[:, sl]), 1)
                ve_rn = ve.n

                @blk.tensor
                def _(t, sl=sl, ve_rn=ve_rn):
                    t.wait_ge(ve.sem, ve_rn)
                    t.matmul(PB[6][:, 0:512], ONE64, RTMP2[:, sl], start=True, stop=True)
                    pe.tick(t.matmul(PB[7][:, 0:512], ONE, RTMP[:, sl],
                                     start=True, stop=True), 1)
                pe_bc = pe.n

                @blk.vector
                def _(v, sl=sl, pe_bc=pe_bc):
                    v.wait_ge(pe.sem, pe_bc)
                    ve.tick(v.tensor_mul(out=KH[:, sl], in0=STF[:, 0:512],
                                         in1=PB[6][:, 0:512]), 1)
                    v.wait_ge(ve.sem, ve.n)
                    ve.tick(v.tensor_mul(out=K2[:, sl], in0=KH[:, sl],
                                         in1=PB[7][:, 0:512]), 1)
                ve_k2d[tc] = ve.n

            @blk.scalar
            def _(s):
                s.wait_ge(ve.sem, ve.n)
                se.tick(s.activation(out=RTMP, in_=LA_ROW, func=AF.Exp), 1)
            se_ar = se.n

            for tc in range(4):
                sl = slice(tc * 512, (tc + 1) * 512)

                @blk.tensor
                def _(t, sl=sl, se_ar=se_ar):
                    t.wait_ge(se.sem, se_ar)
                    t.wait_ge(ve.sem, ve.n)
                    pe.tick(t.matmul(PB[6][:, 0:512], ONE, RTMP[:, sl],
                                     start=True, stop=True), 1)
                pe_ar = pe.n

                @blk.vector
                def _(v, sl=sl, pe_ar=pe_ar):
                    v.wait_ge(pe.sem, pe_ar)
                    ve.tick(v.tensor_mul(out=QA[:, sl], in0=QB[:, sl],
                                         in1=PB[6][:, 0:512]), 1)

            if DBG:
                @blk.sync
                def _(sy):
                    sy.wait_ge(ve.sem, ve.n)
                    sy.wait_ge(se.sem, se.n)
                    st.tick(sy.dma_start(out=dbg["dq"][:, :], in_=QB), 16)
                    st.tick(sy.dma_start(out=dbg["dkh"][:, :], in_=KH), 16)
                    st.tick(sy.dma_start(out=dbg["dv"][:, :], in_=VB), 16)
                    st.tick(sy.dma_start(out=dbg["dg"][:, :], in_=GF), 16)
                    st.tick(sy.dma_start(out=dbg["dla"][:, :], in_=LAC), 16)
                    st.tick(sy.dma_start(out=dbg["dbc"][:, :], in_=BCC), 16)

        NTIL = GEN[:, 2048:4096]
        XTIL = GEN[:, 4096:6144]
        PPA = GEN[:, 6144:8192]
        TTIL = GEN[:, 8192:10240]
        PPB = GEN[:, 10240:12288]
        RHSF = GEN[:, 12288:16384]
        KRF = STB[:, 1024:3072]
        WFU = GEN[:, 0:2048]
        PCTF = GEN[:, 2048:4096]
        DCF = GEN[:, 4096:6144]
        MBC = GEN[:, 6144:8192]
        MTC = GEN[:, 8192:10240]
        UFU = GEN[:, 10240:12288]
        URF = GEN[:, 12288:14336]
        UTF = GEN[:, 14336:16384]
        PTF = GEN[:, 2048:4096]

        def c128(buf, c_):
            return buf[:, c_ * 128:(c_ + 1) * 128]

        if STAGE >= 2:
            @blk.tensor
            def _(t):
                t.wait_ge(ve.sem, ve.n)
                t.wait_ge(se.sem, se.n)
                last = None
                for c_ in range(NCH):
                    bk, off = c_ % 4, (c_ // 4) * 128
                    t.matmul(PB[bk][:, off:off + 128], KH[:, c_ * 128:(c_ + 1) * 128],
                             K2[:, c_ * 128:(c_ + 1) * 128], start=True, stop=True)
                    last = t.matmul(PB[4 + bk][:, off:off + 128], ONE,
                                    LA_ROW[:, c_ * 128:(c_ + 1) * 128],
                                    start=True, stop=True)
                pe.tick(last, 1)
            p2a = pe.n

            @blk.vector
            def _(v):
                v.wait_ge(pe.sem, p2a)
                last = None
                for c_ in range(NCH):
                    bk, off = c_ % 4, (c_ // 4) * 128
                    last = v.scalar_tensor_tensor(out=c128(FSC, c_),
                                                  in0=PB[4 + bk][:, off:off + 128],
                                                  scalar=LAC[:, c_:c_ + 1], in1=MSK,
                                                  op0=ALU.subtract, op1=ALU.add)
                ve.tick(last, 1)
            v2a = ve.n

            @blk.scalar
            def _(s):
                s.wait_ge(ve.sem, v2a)
                se.tick(s.activation(out=FSC, in_=FSC, func=AF.Exp), 1)
            s2a = se.n

            @blk.vector
            def _(v):
                v.wait_ge(se.sem, s2a)
                last = None
                for c_ in range(NCH):
                    last = v.tensor_add(out=c128(EIF, c_), in0=c128(FSC, c_), in1=IDE)
                for c_ in range(NCH):
                    bk, off = c_ % 4, (c_ // 4) * 128
                    last = v.tensor_mul(out=c128(NTIL, c_), in0=c128(FSC, c_),
                                        in1=PB[bk][:, off:off + 128])
                ve.tick(last, 1)
                v.wait_ge(ve.sem, ve.n)
                last = None
                for c_ in range(NCH):
                    last = v.tensor_sub(out=c128(XTIL, c_), in0=IDEB, in1=c128(NTIL, c_))
                ve.tick(last, 1)
            v2b = ve.n

            @blk.tensor
            def _(t):
                t.wait_ge(ve.sem, v2b)
                last = None
                for c_ in range(NCH):
                    bk, off = c_ // 8, (c_ % 8) * 128
                    last = t.transpose(pbb(bk, 128, off), c128(NTIL, c_), IDEB)
                pe.tick(last, 1)
            p2b = pe.n

            @blk.scalar
            def _(s):
                s.wait_ge(pe.sem, p2b)
                s.copy(out=TTIL[:, 0:1024], in_=pbb(0, 1024, 0))
                se.tick(s.copy(out=TTIL[:, 1024:2048], in_=pbb(1, 1024, 0)), 1)
            s2b = se.n

            pcur, pnxt = NTIL, PPA
            prev_apply_v = v2b
            prev_t_evac = s2b
            for j in range(1, 7):
                @blk.tensor
                def _(t, pcur=pcur, prev_t_evac=prev_t_evac, prev_apply_v=prev_apply_v):
                    t.wait_ge(se.sem, prev_t_evac)
                    t.wait_ge(ve.sem, prev_apply_v)
                    last = None
                    for c_ in range(NCH):
                        bk, off = 2 + c_ // 4, (c_ % 4) * 128
                        last = t.matmul(PB[bk][:, off:off + 128], c128(TTIL, c_),
                                        c128(pcur, c_), start=True, stop=True)
                    pe.tick(last, 1)
                ppw = pe.n

                @blk.scalar
                def _(s, pnxt=pnxt, ppw=ppw):
                    s.wait_ge(pe.sem, ppw)
                    last = None
                    for g4 in range(4):
                        last = s.copy(out=pnxt[:, g4 * 512:(g4 + 1) * 512],
                                      in_=PB[2 + g4][:, 0:512])
                    se.tick(last, 1)
                spw = se.n

                @blk.tensor
                def _(t, pnxt=pnxt, spw=spw):
                    t.wait_ge(se.sem, spw)
                    last = None
                    for c_ in range(NCH):
                        bk, off = c_ // 8, (c_ % 8) * 128
                        last = t.transpose(pbb(bk, 128, off), c128(pnxt, c_), IDEB)
                    pe.tick(last, 1)
                ptr = pe.n

                @blk.scalar
                def _(s, ptr=ptr):
                    s.wait_ge(pe.sem, ptr)
                    s.copy(out=TTIL[:, 0:1024], in_=pbb(0, 1024, 0))
                    se.tick(s.copy(out=TTIL[:, 1024:2048], in_=pbb(1, 1024, 0)), 1)
                str_ = se.n

                @blk.tensor
                def _(t, str_=str_):
                    t.wait_ge(se.sem, str_)
                    last = None
                    for c_ in range(NCH):
                        bk, off = 2 + c_ // 4, (c_ % 4) * 128
                        t.matmul(PB[bk][:, off:off + 128], IDEB, c128(XTIL, c_),
                                 start=True, stop=False)
                        last = t.matmul(PB[bk][:, off:off + 128], c128(TTIL, c_),
                                        c128(XTIL, c_), start=False, stop=True)
                    pe.tick(last, 1)
                pap = pe.n

                @blk.vector
                def _(v, pap=pap):
                    v.wait_ge(pe.sem, pap)
                    last = None
                    for g4 in range(4):
                        last = v.tensor_copy(out=XTIL[:, g4 * 512:(g4 + 1) * 512],
                                             in_=PB[2 + g4][:, 0:512])
                    ve.tick(last, 1)
                prev_apply_v = ve.n
                prev_t_evac = str_
                pcur, pnxt = pnxt, (PPB if pnxt is PPA else PPA)

            @blk.tensor
            def _(t):
                t.wait_ge(ve.sem, prev_apply_v)
                last = None
                for c_ in range(NCH):
                    bk, off = c_ % 2, (c_ // 2) * 128
                    t.transpose(pbb(bk, 128, off), VB[:, c_ * 128:(c_ + 1) * 128], IDEB)
                    last = t.transpose(pbb(2 + bk, 128, off), KH[:, c_ * 128:(c_ + 1) * 128],
                                       IDEB)
                pe.tick(last, 1)
            p2c = pe.n

            @blk.scalar
            def _(s):
                s.wait_ge(pe.sem, p2c)
                last = None
                for c_ in range(NCH):
                    bk, off = c_ % 2, (c_ // 2) * 128
                    last = s.copy(out=c128(KTT, c_), in_=pbb(2 + bk, 128, off))
                se.tick(last, 1)
            s2c = se.n

            @blk.vector
            def _(v):
                v.wait_ge(pe.sem, p2c)
                v.wait_ge(se.sem, s2c)
                last = None
                for c_ in range(NCH):
                    bk, off = c_ % 2, (c_ // 2) * 128
                    v.tensor_scalar(out=RHSF[:, c_ * 256:c_ * 256 + 128],
                                    in0=pbb(bk, 128, off), scalar1=BCC[:, c_:c_ + 1],
                                    scalar2=None, op0=ALU.mult)
                    v.tensor_scalar(out=RHSF[:, c_ * 256 + 128:c_ * 256 + 256],
                                    in0=pbb(2 + bk, 128, off), scalar1=BAC[:, c_:c_ + 1],
                                    scalar2=None, op0=ALU.mult)
                    last = v.tensor_scalar(out=c128(KRF, c_), in0=pbb(2 + bk, 128, off),
                                           scalar1=RC_[:, c_:c_ + 1], scalar2=None,
                                           op0=ALU.mult)
                ve.tick(last, 1)
            v2c = ve.n

            @blk.tensor
            def _(t):
                t.wait_ge(ve.sem, v2c)
                t.wait_ge(se.sem, s2c)
                last = None
                for c_ in range(NCH):
                    bk, off = c_ % 8, (c_ // 8) * 256
                    last = t.matmul(PB[bk][:, off:off + 256], c128(XTIL, c_),
                                    RHSF[:, c_ * 256:(c_ + 1) * 256],
                                    start=True, stop=True)
                pe.tick(last, 1)
            p2d = pe.n

            @blk.scalar
            def _(s):
                s.wait_ge(pe.sem, p2d)
                last = None
                for c_ in range(NCH):
                    bk, off = c_ % 8, (c_ // 8) * 256
                    last = s.copy(out=c128(UVB, c_), in_=PB[bk][:, off:off + 128])
                se.tick(last, 1)
            s2d = se.n

            @blk.vector
            def _(v):
                v.wait_ge(pe.sem, p2d)
                v.wait_ge(se.sem, s2d)
                last = None
                for c_ in range(NCH):
                    bk, off = c_ % 8, (c_ // 8) * 256
                    last = v.tensor_scalar(out=c128(WFU, c_),
                                           in0=PB[bk][:, off + 128:off + 256],
                                           scalar1=-1.0, scalar2=None, op0=ALU.mult)
                ve.tick(last, 1)
            v2d = ve.n

            @blk.tensor
            def _(t):
                t.wait_ge(ve.sem, v2d)
                t.wait_ge(se.sem, s2d)
                last = None
                for c_ in range(NCH):
                    bk, off = c_ % 2, (c_ // 2) * 128
                    last = t.transpose(pbb(bk, 128, off), c128(WFU, c_), IDEB)
                pe.tick(last, 1)
            p2e = pe.n

            @blk.scalar
            def _(s):
                s.wait_ge(pe.sem, p2e)
                last = None
                for c_ in range(NCH):
                    bk, off = c_ % 2, (c_ // 2) * 128
                    last = s.copy(out=c128(WTN, c_), in_=pbb(bk, 128, off))
                se.tick(last, 1)
            s2e = se.n

            @blk.tensor
            def _(t):
                t.wait_ge(se.sem, s2e)
                last = None
                for c_ in range(NCH):
                    bk, off = 2 + c_ % 4, (c_ // 4) * 128
                    last = t.matmul(PB[bk][:, off:off + 128], c128(WFU, c_),
                                    c128(KRF, c_), start=True, stop=True)
                pe.tick(last, 1)
            p2f = pe.n

            @blk.vector
            def _(v):
                v.wait_ge(pe.sem, p2f)
                last = None
                for c_ in range(NCH):
                    bk, off = 2 + c_ % 4, (c_ // 4) * 128
                    last = v.scalar_tensor_tensor(out=c128(PCTF, c_), in0=IDE,
                                                  scalar=ALC[:, c_:c_ + 1],
                                                  in1=PB[bk][:, off:off + 128],
                                                  op0=ALU.mult, op1=ALU.add)
                ve.tick(last, 1)
            v2f = ve.n

            @blk.tensor
            def _(t):
                t.wait_ge(ve.sem, v2f)
                last = None
                for c_ in range(NCH):
                    bk, off = 2 + c_ % 4, (c_ // 4) * 128
                    last = t.matmul(PB[bk][:, off:off + 128], c128(KRF, c_),
                                    c128(UVB, c_), start=True, stop=True)
                pe.tick(last, 1)
            p2g = pe.n

            @blk.scalar
            def _(s):
                s.wait_ge(pe.sem, p2g)
                last = None
                for c_ in range(NCH):
                    bk, off = 2 + c_ % 4, (c_ // 4) * 128
                    last = s.copy(out=c128(DCF, c_), in_=PB[bk][:, off:off + 128])
                se.tick(last, 1)
            s2g = se.n

            vcomp = max(v2f, 0)
            scomp = s2g
            for c_ in range(NCH):
                if c_ == 0:
                    @blk.tensor
                    def _(t, scomp=scomp):
                        t.wait_ge(ve.sem, v2f)
                        t.wait_ge(se.sem, scomp)
                        pe.tick(t.transpose(pbb(6, 128, 768), c128(PCTF, 0), IDEB), 1)
                    pc0 = pe.n

                    @blk.vector
                    def _(v, pc0=pc0):
                        v.wait_ge(pe.sem, pc0)
                        v.tensor_copy(out=PCM, in_=pbb(6, 128, 768))
                        v.tensor_copy(out=PCMT, in_=c128(PCTF, 0))
                        ve.tick(v.tensor_copy(out=DCM, in_=c128(DCF, 0)), 1)
                    vcomp = ve.n
                else:
                    @blk.tensor
                    def _(t, c_=c_, vcomp=vcomp):
                        t.wait_ge(ve.sem, vcomp)
                        t.matmul(PB[6][:, 384:512], c128(PCTF, c_), PCM,
                                 start=True, stop=True)
                        t.matmul(PB[7][:, 256:384], PCM, c128(PCTF, c_),
                                 start=True, stop=True)
                        t.matmul(PB[7][:, 384:512], c128(PCTF, c_), DCM,
                                 start=True, stop=False)
                        pe.tick(t.matmul(PB[7][:, 384:512], IDEB, c128(DCF, c_),
                                         start=False, stop=True), 1)
                    pcmp = pe.n

                    @blk.vector
                    def _(v, pcmp=pcmp):
                        v.wait_ge(pe.sem, pcmp)
                        v.tensor_copy(out=PCM, in_=PB[6][:, 384:512])
                        v.tensor_copy(out=PCMT, in_=PB[7][:, 256:384])
                        ve.tick(v.tensor_copy(out=DCM, in_=PB[7][:, 384:512]), 1)
                    vcomp = ve.n

        if STAGE >= 3:
            @blk.scalar
            def _(s):
                s.wait_ge(ve.sem, ve.n)
                sw.tick(s.dma_start(out=b_pd_in.ap()[0:128, :], in_=PCMT), 16)
                sw.tick(s.dma_start(out=b_pd_in.ap()[128:256, :], in_=DCM), 16)
            sw_pd = sw.n

            @blk.gpsimd
            def _(g):
                g.wait_ge(sw.sem, sw_pd)
                cc[1].tick(g.collective_compute(
                    "AllGather", ALU.bypass, replica_groups=[list(range(NCORE))],
                    ins=[b_pd_in.ap()], outs=[b_pd_out.ap()]), 1)
                g.wait_ge(cc[1].sem, 1)
                for k in range(6):
                    gq.tick(g.indirect_dma_start(
                        out=GPD[k], out_offset=None,
                        in_=b_pd_out.ap()[:, :],
                        in_offset=bass.IndirectOffsetOnAxis(ap=SMI[:, 1 + k:2 + k], axis=0)), 16)
            gq_pd = gq.n

            @blk.vector
            def _(v):
                v.wait_ge(gq.sem, gq_pd)
                v.memset(MACC, 0.0)
                ve.tick(v.memset(FSC[:, 0:128], 0.0), 1)
            v6 = ve.n

            for jstep in range(3):
                @blk.tensor
                def _(t, jstep=jstep, v6=v6):
                    t.wait_ge(ve.sem, v6)
                    t.matmul(PB[0][:, 0:128], GPD[2 * jstep], MACC, start=True, stop=False)
                    pe.tick(t.matmul(PB[0][:, 0:128], IDEB, GPD[2 * jstep + 1],
                                     start=False, stop=True), 1)
                p11 = pe.n

                @blk.vector
                def _(v, jstep=jstep, p11=p11):
                    v.wait_ge(pe.sem, p11)
                    ve.tick(v.tensor_copy(out=MACC, in_=PB[0][:, 0:128]), 1)
                    v.wait_ge(ve.sem, ve.n)
                    ve.tick(v.scalar_tensor_tensor(out=FSC[:, 0:128], in0=PB[0][:, 0:128],
                                                   scalar=MSL[:, jstep + 1:jstep + 2],
                                                   in1=FSC[:, 0:128],
                                                   op0=ALU.mult, op1=ALU.add), 1)
                v6 = ve.n

            @blk.vector
            def _(v):
                v.wait_ge(ve.sem, v6)
                ve.tick(v.tensor_copy(out=c128(MBC, 0), in_=FSC[:, 0:128]), 1)
            v7 = ve.n

            if DBG:
                @blk.sync
                def _(sy):
                    sy.wait_ge(ve.sem, v7)
                    st.tick(sy.dma_start(out=dbg["dmst"][:, :], in_=c128(MBC, 0)), 16)

        if STAGE >= 4:
            vch = v7
            sch = se.n
            for c_ in range(NCH - 1):
                @blk.tensor
                def _(t, c_=c_, vch=vch, sch=sch):
                    t.wait_ge(ve.sem, vch)
                    if c_ > 0:
                        t.wait_ge(se.sem, sch)
                    bk = c_ % 2
                    t.matmul(PB[bk][:, 0:128], c128(PCTF, c_), c128(MBC, c_),
                             start=True, stop=False)
                    pe.tick(t.matmul(PB[bk][:, 0:128], IDEB, c128(DCF, c_),
                                     start=False, stop=True), 1)
                pch = pe.n

                @blk.scalar
                def _(s, c_=c_, pch=pch):
                    s.wait_ge(pe.sem, pch)
                    se.tick(s.copy(out=c128(MBC, c_ + 1), in_=PB[c_ % 2][:, 0:128]), 1)
                sch = se.n

            @blk.tensor
            def _(t):
                t.wait_ge(se.sem, sch)
                last = None
                for c_ in range(NCH):
                    bk, off = 2 + c_ % 2, (c_ // 2) * 128
                    last = t.transpose(pbb(bk, 128, off), c128(MBC, c_), IDEB)
                pe.tick(last, 1)
            p4a = pe.n

            @blk.scalar
            def _(s):
                s.wait_ge(pe.sem, p4a)
                last = None
                for c_ in range(NCH):
                    bk, off = 2 + c_ % 2, (c_ // 2) * 128
                    last = s.copy(out=c128(MTC, c_), in_=pbb(bk, 128, off))
                se.tick(last, 1)
            s4a = se.n

            @blk.tensor
            def _(t):
                t.wait_ge(se.sem, s4a)
                last = None
                for c_ in range(NCH):
                    bk, off = 4 + c_ // 4, (c_ % 4) * 128
                    t.matmul(PB[bk][:, off:off + 128], IDEB, c128(UVB, c_),
                             start=True, stop=False)
                    last = t.matmul(PB[bk][:, off:off + 128], c128(WTN, c_), c128(MBC, c_),
                                    start=False, stop=True)
                pe.tick(last, 1)
            p4b = pe.n

            @blk.scalar
            def _(s):
                s.wait_ge(pe.sem, p4b)
                last = None
                for g4 in range(4):
                    last = s.copy(out=UFU[:, g4 * 512:(g4 + 1) * 512],
                                  in_=PB[4 + g4][:, 0:512])
                se.tick(last, 1)
            s4b = se.n

            @blk.vector
            def _(v):
                v.wait_ge(pe.sem, p4b)
                v.wait_ge(se.sem, s4b)
                last = None
                for c_ in range(NCH):
                    bk, off = 4 + c_ // 4, (c_ % 4) * 128
                    last = v.tensor_scalar(out=c128(URF, c_), in0=PB[bk][:, off:off + 128],
                                           scalar1=RC_[:, c_:c_ + 1], scalar2=None,
                                           op0=ALU.mult)
                ve.tick(last, 1)
            v4b = ve.n

            @blk.tensor
            def _(t):
                t.wait_ge(se.sem, s4b)
                last = None
                for c_ in range(NCH):
                    bk, off = c_ // 8, (c_ % 8) * 128
                    last = t.transpose(pbb(bk, 128, off), c128(UFU, c_), IDEB)
                pe.tick(last, 1)
            p4c = pe.n

            @blk.scalar
            def _(s):
                s.wait_ge(pe.sem, p4c)
                s.copy(out=UTF[:, 0:1024], in_=pbb(0, 1024, 0))
                se.tick(s.copy(out=UTF[:, 1024:2048], in_=pbb(1, 1024, 0)), 1)
            s4c = se.n

            @blk.tensor
            def _(t):
                t.wait_ge(se.sem, s4c)
                t.wait_ge(ve.sem, v4b)
                last = None
                for c_ in range(NCH):
                    bk, off = 4 + c_ // 4, (c_ % 4) * 128
                    last = t.matmul(PB[bk][:, off:off + 128], c128(UTF, c_),
                                    QB[:, c_ * 128:(c_ + 1) * 128], start=True, stop=True)
                pe.tick(last, 1)
            p4d = pe.n

            @blk.vector
            def _(v):
                v.wait_ge(pe.sem, p4d)
                last = None
                for g4 in range(4):
                    last = v.tensor_mul(out=PTF[:, g4 * 512:(g4 + 1) * 512],
                                        in0=EIF[:, g4 * 512:(g4 + 1) * 512],
                                        in1=PB[4 + g4][:, 0:512])
                ve.tick(last, 1)
            v4d = ve.n

            @blk.tensor
            def _(t):
                t.wait_ge(ve.sem, v4d)
                last = None
                for c_ in range(NCH):
                    bk, off = 4 + c_ // 4, (c_ % 4) * 128
                    t.matmul(PB[bk][:, off:off + 128], c128(KTT, c_), c128(PTF, c_),
                             start=True, stop=False)
                    last = t.matmul(PB[bk][:, off:off + 128], c128(MTC, c_),
                                    QA[:, c_ * 128:(c_ + 1) * 128], start=False, stop=True)
                pe.tick(last, 1)
            p4e = pe.n

            @blk.scalar
            def _(s):
                s.wait_ge(pe.sem, p4e)
                last = None
                for g4 in range(4):
                    last = s.copy(out=OBF[:, g4 * 512:(g4 + 1) * 512],
                                  in_=PB[4 + g4][:, 0:512])
                se.tick(last, 1)

            if DBG:
                @blk.sync
                def _(sy):
                    sy.wait_ge(se.sem, se.n)
                    st.tick(sy.dma_start(out=dbg["dO"][:, :], in_=OBF), 16)

        if STAGE >= 5:
            @blk.vector
            def _(v, st_now=st.n):
                v.wait_ge(se.sem, se.n)
                if DBG:
                    v.wait_ge(st.sem, st_now)
                ve.tick(v.tensor_mul(out=OBF, in0=OBF, in1=GF), 1)
            v11 = ve.n

            vwo = {}
            for h in range(8):
                bk0 = 0 if h % 2 == 0 else 4

                @blk.tensor
                def _(t, h=h, bk0=bk0, v11=v11):
                    if h == 0:
                        t.wait_ge(ve.sem, v11)
                    if h >= 2:
                        t.wait_ge(ve.sem, vwo[h - 2])
                    pe_i = None
                    for tc in range(4):
                        pe_i = t.matmul(PB[bk0 + tc][:, 0:512], wo_t(h),
                                        OBF[:, tc * 512:(tc + 1) * 512],
                                        start=True, stop=True)
                    pe.tick(pe_i, 1)
                p19 = pe.n

                @blk.vector
                def _(v, h=h, bk0=bk0, p19=p19):
                    v.wait_ge(pe.sem, p19)
                    last = None
                    for tc in range(4):
                        sl2 = slice(h * T + tc * 512, h * T + (tc + 1) * 512)
                        last = v.scalar_tensor_tensor(out=XFFN[:, sl2],
                                                      in0=PB[bk0 + tc][:, 0:512],
                                                      scalar=BOC[:, h:h + 1],
                                                      in1=XFFN[:, sl2],
                                                      op0=ALU.add, op1=ALU.add)
                    ve.tick(last, 1)
                vwo[h] = ve.n
            lastv = ve.n

            for h in range(8):
                @blk.scalar
                def _(s, h=h, lastv=lastv, se_now=se.n):
                    s.wait_ge(ve.sem, lastv)
                    if h > 0:
                        s.wait_ge(pe.sem, pe.n)
                        s.wait_ge(se.sem, se_now)
                    se.tick(s.activation(out=STB1, in_=XFFN[:, h * T:(h + 1) * T],
                                           func=AF.Identity), 1)
                    s.wait_ge(se.sem, se.n)
                    se.tick(s.activation(out=STB2, in_=STB1, func=AF.Square), 1)
                s10 = se.n

                @blk.tensor
                def _(t, h=h, s10=s10):
                    t.wait_ge(se.sem, s10)
                    if h == 0:
                        t.wait_ge(ve.sem, ve.n)
                    last = None
                    for tc in range(4):
                        last = t.matmul(PB[4 + tc][0:1, 0:512], ONECB,
                                        STB1[:, tc * 512:(tc + 1) * 512],
                                        start=(h == 0), stop=(h == 7))
                        last = t.matmul(PB[tc][0:1, 0:512], ONECB,
                                        STB2[:, tc * 512:(tc + 1) * 512],
                                        start=(h == 0), stop=(h == 7))
                    pe.tick(last, 1)
            p20 = pe.n

            @blk.vector
            def _(v):
                v.wait_ge(pe.sem, p20)
                for tc in range(4):
                    v.tensor_scalar(out=MROW[:, tc * 512:(tc + 1) * 512],
                                    in0=PB[4 + tc][0:1, 0:512], scalar1=1.0 / D,
                                    scalar2=None, op0=ALU.mult)
                    v.tensor_scalar(out=SQR[:, tc * 512:(tc + 1) * 512],
                                    in0=PB[tc][0:1, 0:512], scalar1=1.0 / D,
                                    scalar2=None, op0=ALU.mult)
                ve.tick(v.engine_nop(), 1)
                v.wait_ge(ve.sem, ve.n)
                ve.tick(v.tensor_mul(out=VARR, in0=MROW, in1=MROW), 1)
                v.wait_ge(ve.sem, ve.n)
                v.tensor_sub(out=VARR, in0=SQR, in1=VARR)
                ve.tick(v.engine_nop(), 1)
                v.wait_ge(ve.sem, ve.n)
                ve.tick(v.tensor_scalar_add(out=VARR, in0=VARR, scalar1=1e-5), 1)
            v12 = ve.n

            @blk.scalar
            def _(s):
                s.wait_ge(ve.sem, v12)
                se.tick(s.activation(out=VARR, in_=VARR, func=AF.Sqrt), 1)
            s11 = se.n

            @blk.vector
            def _(v):
                v.wait_ge(se.sem, s11)
                ve.tick(v.reciprocal(out=RSTD, in_=VARR), 1)
                v.wait_ge(ve.sem, ve.n)
                ve.tick(v.tensor_mul(out=MRSTD, in0=MROW, in1=RSTD), 1)
            v13 = ve.n

            @blk.tensor
            def _(t):
                t.wait_ge(ve.sem, v13)
                last = None
                for tc in range(4):
                    last = t.matmul(PB[4 + tc][:, 0:512], ONE,
                                    RSTD[:, tc * 512:(tc + 1) * 512],
                                    start=True, stop=True)
                    last = t.matmul(PB[tc][:, 0:512], ONE,
                                    MRSTD[:, tc * 512:(tc + 1) * 512],
                                    start=True, stop=True)
                pe.tick(last, 1)
            p21 = pe.n

            @blk.scalar
            def _(s):
                s.wait_ge(pe.sem, p21)
                last = None
                for tc in range(4):
                    last = s.copy(out=STF[:, tc * 512:(tc + 1) * 512], in_=PB[4 + tc][:, 0:512])
                se.tick(last, 1)
                s.wait_ge(se.sem, se.n)
                last = None
                for tc in range(4):
                    last = s.copy(out=FSC[:, tc * 512:(tc + 1) * 512], in_=PB[tc][:, 0:512])
                se.tick(last, 1)
            s12 = se.n

            for h in range(8):
                @blk.vector
                def _(v, h=h, s12=s12):
                    v.wait_ge(se.sem, s12)
                    sl = slice(h * T, (h + 1) * T)
                    v.tensor_mul(out=XFFN[:, sl], in0=XFFN[:, sl], in1=STF[:, 0:2048])
                    ve.tick(v.engine_nop(), 1)
                    v.wait_ge(ve.sem, ve.n)
                    ve.tick(v.tensor_sub(out=XFFN[:, sl], in0=XFFN[:, sl],
                                         in1=FSC[:, 0:2048]), 1)
                    v.wait_ge(ve.sem, ve.n)
                    ve.tick(v.tensor_reduce(out=SM[:, 40 + h:41 + h], in_=XFFN[:, sl],
                                            axis=AX.X, op=ALU.max,
                                            apply_absolute_value=True), 1)
                lastv = ve.n

            @blk.vector
            def _(v):
                v.wait_ge(ve.sem, lastv)
                ve.tick(v.tensor_reduce(out=SM[:, 48:49], in_=SM[:, 40:48], axis=AX.X,
                                        op=ALU.max, apply_absolute_value=False), 1)
            v14 = ve.n

            @blk.tensor
            def _(t):
                t.wait_ge(ve.sem, v14)
                t.wait_ge(se.sem, se.n)
                pe.tick(t.transpose(PB[5][0:1, 0:128], SM[:, 48:49], IDE), 1)
            pe_sx = pe.n

            @blk.vector
            def _(v):
                v.wait_ge(pe.sem, pe_sx)
                ve.tick(v.tensor_reduce(out=SM[0:1, 49:50], in_=PB[5][0:1, 0:128],
                                        axis=AX.X, op=ALU.max), 1)
            ve_sx = ve.n

            @blk.gpsimd
            def _(g):
                g.wait_ge(ve.sem, ve_sx)
                gq.tick(g.dma_start(out=b_mx_in.ap()[0:1, 0:1], in_=SM[0:1, 49:50]), 16)
                g.wait_ge(gq.sem, gq.n)
                cc[2].tick(g.collective_compute(
                    "AllReduce", ALU.max, replica_groups=[list(range(NCORE))],
                    ins=[b_mx_in.ap()[0:1, 0:1]], outs=[b_mx_out.ap()[0:1, 0:1]]), 1)
                g.wait_ge(cc[2].sem, cc[2].n)
                gq.tick(g.dma_start(out=SM[0:1, 50:51], in_=b_mx_out.ap()[0:1, 0:1]), 16)
            gq_sx = gq.n

            @blk.vector
            def _(v):
                v.wait_ge(gq.sem, gq_sx)
                v.tensor_scalar(out=SM[0:1, 51:52], in0=SM[0:1, 50:51],
                                scalar1=1.0 / QMAX, scalar2=1e-12, op0=ALU.mult, op1=ALU.max)
                ve.tick(v.engine_nop(), 1)
                v.wait_ge(ve.sem, ve.n)
                ve.tick(v.reciprocal(out=SM[0:1, 52:53], in_=SM[0:1, 51:52]), 1)
            v15 = ve.n

            @blk.tensor
            def _(t):
                t.wait_ge(ve.sem, v15)
                t.wait_ge(se.sem, se.n)
                pe.tick(t.matmul(PB[5][:, 0:2], ONE, SM[0:1, 51:53], start=True, stop=True), 1)
            p22 = pe.n

            @blk.vector
            def _(v):
                v.wait_ge(pe.sem, p22)
                ve.tick(v.tensor_copy(out=SCL[:, 0:2], in_=PB[5][:, 0:2]), 1)
            v16 = ve.n

            xqv = {}
            for h in range(8):
                ints = INT1 if h % 2 == 0 else INT1o

                @blk.vector
                def _(v, h=h, ints=ints, v16=v16):
                    if h < 2:
                        v.wait_ge(ve.sem, v16)
                    else:
                        v.wait_ge(ve.sem, xqv[h - 2])
                    sl = slice(h * T, (h + 1) * T)
                    v.tensor_scalar(out=ints, in0=XFFN[:, sl], scalar1=SCL[:, 1:2],
                                    scalar2=None, op0=ALU.mult)
                    ve.tick(v.engine_nop(), 1)
                    v.wait_ge(ve.sem, ve.n)
                    ve.tick(v.tensor_scalar(out=XBF[h], in0=ints, scalar1=SCL[:, 0:1],
                                            scalar2=None, op0=ALU.mult), 1)
                xqv[h] = ve.n

            if DBG:
                @blk.sync
                def _(sy):
                    sy.wait_ge(ve.sem, ve.n)
                    st.tick(sy.dma_start(
                        out=dbg["dxffn"].rearrange("(h p) t -> p h t", p=128),
                        in_=XFFN.ap().rearrange("p (h t) -> p h t", h=8)), 16)

        def emit_mm_ot(ot, bsrc, bdst, acc_col, pe_used, ve_red, se_evac, sw_spill,
                       first_extra_wait=None):
            par = ot % 2
            wsl = GEN[:, 20480 + par * 1024:20480 + par * 1024 + 1024]
            rank, oc = ot // 3, ot % 3
            LDP = ld if par == 0 else ldb
            SWP = sw if par == 0 else swb

            @blk.sync
            def _(sy, ot=ot, rank=rank, oc=oc, wsl=wsl, LDP=LDP, ve_now=ve.n):
                sy.wait_ge(pe.sem, pe.n if ot < 2 else pe_used[ot - 2])
                if ot < 2:
                    sy.wait_ge(ve.sem, ve_now)
                for i in range(8):
                    LDP.tick(sy.dma_start(
                        out=wsl[:, i * 128:(i + 1) * 128],
                        in_=bsrc[rank * D + i * 128:rank * D + (i + 1) * 128,
                                 oc * 128:(oc + 1) * 128]), 16)
            need_ld = LDP.n

            @blk.tensor
            def _(t, ot=ot, par=par, wsl=wsl, need_ld=need_ld, LDP=LDP):
                t.wait_ge(LDP.sem, need_ld)
                if first_extra_wait is not None:
                    first_extra_wait(t)
                if ot == 0:
                    t.wait_ge(cc[0].sem, 3)
                    t.wait_ge(ve.sem, ve.n)
                    t.wait_ge(se.sem, se.n)
                if ot >= 2:
                    t.wait_ge(se.sem, se_evac[ot - 2])
                bk = [0, 1, 2, 3] if par == 0 else [4, 5, 6, 7]
                last = None
                for i in range(8):
                    for tc in range(4):
                        last = t.matmul(PB[bk[tc]][:, 0:512], wsl[:, i * 128:(i + 1) * 128],
                                        XBF[i][:, tc * 512:(tc + 1) * 512],
                                        start=(i == 0), stop=(i == 7))
                pe.tick(last, 1)
            pe_used[ot] = pe.n
            need_pe = pe.n

            @blk.scalar
            def _(s, ot=ot, par=par, need_pe=need_pe, SWP=SWP, st_now=st.n,
                  sw_now=sw.n, swb_now=swb.n):
                s.wait_ge(pe.sem, need_pe)
                if DBG and ot < 2:
                    s.wait_ge(st.sem, st_now)
                if ot < 2:
                    s.wait_ge(sw.sem, sw_now)
                    s.wait_ge(swb.sem, swb_now)
                if ot >= 2:
                    s.wait_ge(SWP.sem, sw_spill[ot - 2])
                    s.wait_ge(ve.sem, ve_red[ot - 2])
                stg = GEN[:, 16384:18432] if par == 0 else GEN[:, 18432:20480]
                bk = [0, 1, 2, 3] if par == 0 else [4, 5, 6, 7]
                last = None
                for tc in range(4):
                    last = s.copy(out=stg[:, tc * 512:(tc + 1) * 512],
                                  in_=PB[bk[tc]][:, 0:512])
                se.tick(last, 1)
            se_evac[ot] = se.n
            need_se = se.n

            @blk.vector
            def _(v, ot=ot, par=par, acc_col=acc_col, need_se=need_se, ve_now=ve.n):
                v.wait_ge(se.sem, need_se)
                if ot >= 1:
                    v.wait_ge(ve.sem, ve_red[ot - 1])
                else:
                    v.wait_ge(ve.sem, ve_now)
                stg = GEN[:, 16384:18432] if par == 0 else GEN[:, 18432:20480]
                v.tensor_reduce(out=SM[:, 54:55], in_=stg, axis=AX.X,
                                op=ALU.max, apply_absolute_value=True)
                ve.tick(v.engine_nop(), 1)
                v.wait_ge(ve.sem, ve.n)
                if ot == 0:
                    ve.tick(v.tensor_copy(out=SM[:, acc_col:acc_col + 1],
                                          in_=SM[:, 54:55]), 1)
                else:
                    ve.tick(v.tensor_tensor(out=SM[:, acc_col:acc_col + 1],
                                            in0=SM[:, acc_col:acc_col + 1],
                                            in1=SM[:, 54:55], op=ALU.max), 1)
            ve_red[ot] = ve.n

            @blk.scalar
            def _(s, ot=ot, par=par, need_se=need_se, SWP=SWP):
                s.wait_ge(se.sem, need_se)
                stg = GEN[:, 16384:18432] if par == 0 else GEN[:, 18432:20480]
                SWP.tick(s.dma_start(out=bdst[ot * 128:(ot + 1) * 128, :], in_=stg), 16)
            sw_spill[ot] = SWP.n

        def bc_cols(rsrc, k, dstc, waitv):
            @blk.tensor
            def _(t, rsrc=rsrc, k=k, waitv=waitv):
                t.wait_ge(ve.sem, waitv)
                t.wait_ge(se.sem, se.n)
                pe.tick(t.matmul(PB[5][:, 0:k], ONE, SM[0:1, rsrc:rsrc + k],
                                 start=True, stop=True), 1)
            pw = pe.n

            @blk.vector
            def _(v, k=k, dstc=dstc, pw=pw):
                v.wait_ge(pe.sem, pw)
                ve.tick(v.tensor_copy(out=SCL[:, dstc:dstc + k], in_=PB[5][:, 0:k]), 1)
            return ve.n

        def allred_max(accc, rowdst, ccx):
            @blk.tensor
            def _(t, accc=accc):
                t.wait_ge(ve.sem, ve.n)
                t.wait_ge(se.sem, se.n)
                pe.tick(t.transpose(PB[5][0:1, 0:128], SM[:, accc:accc + 1], IDE), 1)
            pw = pe.n

            @blk.vector
            def _(v, pw=pw, rowdst=rowdst):
                v.wait_ge(pe.sem, pw)
                ve.tick(v.tensor_reduce(out=SM[0:1, rowdst:rowdst + 1],
                                        in_=PB[5][0:1, 0:128],
                                        axis=AX.X, op=ALU.max), 1)
            vw = ve.n

            @blk.gpsimd
            def _(g, vw=vw, rowdst=rowdst, ccx=ccx):
                g.wait_ge(ve.sem, vw)
                gq.tick(g.dma_start(out=b_mx_in.ap()[0:1, 0:1],
                                    in_=SM[0:1, rowdst:rowdst + 1]), 16)
                g.wait_ge(gq.sem, gq.n)
                cc[ccx].tick(g.collective_compute(
                    "AllReduce", ALU.max, replica_groups=[list(range(NCORE))],
                    ins=[b_mx_in.ap()[0:1, 0:1]], outs=[b_mx_out.ap()[0:1, 0:1]]), 1)
                g.wait_ge(cc[ccx].sem, cc[ccx].n)
                gq.tick(g.dma_start(out=SM[0:1, rowdst:rowdst + 1],
                                    in_=b_mx_out.ap()[0:1, 0:1]), 16)
            return gq.n

        def emit_p3_ot(ot, ld_in, po_g2, ve_g2, sw_g2):
            par = ot % 2
            stg_in = STB1 if par == 0 else STB2
            stfb = STF.ap().bitcast(BF16)
            stg_out = stfb[:, 0:2048] if par == 0 else stfb[:, 2048:4096]
            i1 = (GEN[:, 24576:28672].bitcast(I32) if par == 0
                  else GEN[:, 28672:32768].bitcast(I32))
            i2 = GENF[:, 2048:4096].bitcast(I32)
            sgf = SIGF if par == 0 else SIGFo
            LDP = lp5 if par == 0 else lp5b
            SWP = sp3 if par == 0 else sp3b

            @blk.sync
            def _(sy, ot=ot, stg_in=stg_in, LDP=LDP, sw_now=sw.n, swb_now=swb.n):
                if ot < 2:
                    sy.wait_ge(ve.sem, ve.n)
                    sy.wait_ge(sw.sem, sw_now)
                    sy.wait_ge(swb.sem, swb_now)
                else:
                    sy.wait_ge(ve.sem, ve_g2[ot - 2])
                LDP.tick(sy.dma_start(out=stg_in,
                                      in_=b_gate.ap()[ot * 128:(ot + 1) * 128, :]), 16)
            ld_in[ot] = LDP.n

            @blk.vector
            def _(v, ot=ot, stg_in=stg_in, i1=i1, LDP=LDP):
                v.wait_ge(LDP.sem, ld_in[ot])
                if ot >= 2:
                    v.wait_ge(ve.sem, po_g2[ot - 2])
                ve.tick(v.tensor_scalar(out=i1, in0=stg_in, scalar1=SCL[:, 4:5],
                                        scalar2=None, op0=ALU.mult), 1)
            vv = ve.n

            @blk.scalar
            def _(s, vv=vv, i1=i1, sgf=sgf):
                s.wait_ge(ve.sem, vv)
                se.tick(s.activation(out=sgf, in_=i1, func=AF.Sigmoid,
                                     scale=SCL[:, 2:3]), 1)
            ss = se.n

            @blk.vector
            def _(v, ot=ot, ss=ss, i2=i2, sgf=sgf):
                v.wait_ge(se.sem, ss)
                if ot >= 1:
                    v.wait_ge(ve.sem, po_g2[ot - 1])
                ve.tick(v.tensor_scalar(out=i2, in0=sgf, scalar1=65536.0,
                                        scalar2=65535.0, op0=ALU.mult, op1=ALU.min), 1)
            vv2 = ve.n

            @blk.vector
            def _(v, ot=ot, stg_out=stg_out, vv2=vv2, i1=i1, i2=i2, SWP=SWP):
                v.wait_ge(ve.sem, vv2)
                if ot >= 2:
                    v.wait_ge(SWP.sem, sw_g2[ot - 2])
                ve.tick(v.tensor_tensor(out=stg_out, in0=i1, in1=i2, op=ALU.mult), 1)
            po_g2[ot] = ve.n
            pp = ve.n

            @blk.vector
            def _(v, ot=ot, stg_out=stg_out, pp=pp):
                v.wait_ge(ve.sem, pp)
                if ot >= 1:
                    v.wait_ge(ve.sem, ve_g2[ot - 1])
                v.tensor_reduce(out=SM[:, 45:46], in_=stg_out, axis=AX.X,
                                op=ALU.max, apply_absolute_value=True)
                ve.tick(v.engine_nop(), 1)
                v.wait_ge(ve.sem, ve.n)
                if ot == 0:
                    ve.tick(v.tensor_copy(out=SM[:, 55:56], in_=SM[:, 45:46]), 1)
                else:
                    ve.tick(v.tensor_tensor(out=SM[:, 55:56], in0=SM[:, 55:56],
                                            in1=SM[:, 45:46], op=ALU.max), 1)
            ve_g2[ot] = ve.n

            @blk.scalar
            def _(s, ot=ot, stg_out=stg_out, pp=pp, SWP=SWP):
                s.wait_ge(ve.sem, pp)
                SWP.tick(s.dma_start(out=b_g2.ap()[ot * 128:(ot + 1) * 128, :],
                                     in_=stg_out), 16)
            sw_g2[ot] = SWP.n

        if STAGE >= 6:
            pe_used = {}
            ve_red = {}
            se_evac = {}
            sw_spill = {}
            mm_pass_env = dict(bsrc=b_wg.ap(), bdst=b_gate.ap())
            for ot in range(NOT):
                emit_mm_ot(ot, b_wg.ap(), b_gate.ap(), 56, pe_used, ve_red,
                           se_evac, sw_spill)

            gq_sg = allred_max(56, 58, 3)

            pe_used2 = {}
            ve_red2 = {}
            se_evac2 = {}
            sw_spill2 = {}
            ld_in = {}
            po_g2 = {}
            ve_g2 = {}
            sw_g2 = {}

            for ot in range(6):
                emit_mm_ot(ot, b_wu.ap(), b_up.ap(), 57, pe_used2, ve_red2,
                           se_evac2, sw_spill2)

            @blk.vector
            def _(v, gq_sg=gq_sg):
                v.wait_ge(gq.sem, gq_sg)
                v.tensor_scalar(out=SM[0:1, 60:61], in0=SM[0:1, 58:59],
                                scalar1=1.0 / QMAX, scalar2=1e-12, op0=ALU.mult, op1=ALU.max)
                ve.tick(v.engine_nop(), 1)
                v.wait_ge(ve.sem, ve.n)
                ve.tick(v.reciprocal(out=SM[0:1, 61:62], in_=SM[0:1, 60:61]), 1)
            v_sg = ve.n
            bc_cols(60, 1, 2, v_sg)
            ve_sg2 = bc_cols(61, 1, 4, ve.n)


            def up7_wait(t, vebc=ve.n):
                t.wait_ge(ve.sem, vebc)

            for ot in range(6, NOT):
                emit_mm_ot(ot, b_wu.ap(), b_up.ap(), 57, pe_used2, ve_red2,
                           se_evac2, sw_spill2,
                           first_extra_wait=(up7_wait if ot == 6 else None))
                emit_p3_ot(ot - 6, ld_in, po_g2, ve_g2, sw_g2)
            for ot2 in range(NOT - 6, NOT):
                emit_p3_ot(ot2, ld_in, po_g2, ve_g2, sw_g2)

            gq_su = allred_max(57, 59, 6)

            @blk.vector
            def _(v, gq_su=gq_su):
                v.wait_ge(gq.sem, gq_su)
                v.tensor_scalar(out=SM[0:1, 62:63], in0=SM[0:1, 59:60],
                                scalar1=1.0 / QMAX, scalar2=1e-12, op0=ALU.mult, op1=ALU.max)
                ve.tick(v.engine_nop(), 1)
                v.wait_ge(ve.sem, ve.n)
                ve.tick(v.reciprocal(out=SM[0:1, 63:64], in_=SM[0:1, 62:63]), 1)
            v_su = ve.n
            bc_cols(62, 1, 3, v_su)
            ve_scl6 = bc_cols(63, 1, 5, ve.n)

            gq_g2 = allred_max(55, 54, 4)

            @blk.vector
            def _(v, gq_g2=gq_g2):
                v.wait_ge(gq.sem, gq_g2)
                v.tensor_scalar(out=SM[0:1, 61:62], in0=SM[0:1, 60:61],
                                scalar1=1.0 / 65536.0, scalar2=None, op0=ALU.mult)
                ve.tick(v.engine_nop(), 1)
                v.wait_ge(ve.sem, ve.n)
                v.tensor_mul(out=SM[0:1, 54:55], in0=SM[0:1, 54:55], in1=SM[0:1, 61:62])
                ve.tick(v.engine_nop(), 1)
                v.wait_ge(ve.sem, ve.n)
                v.tensor_scalar(out=SM[0:1, 54:55], in0=SM[0:1, 54:55],
                                scalar1=1.0 / QMAX, scalar2=1e-12, op0=ALU.mult, op1=ALU.max)
                ve.tick(v.engine_nop(), 1)
                v.wait_ge(ve.sem, ve.n)
                ve.tick(v.reciprocal(out=SM[0:1, 53:54], in_=SM[0:1, 54:55]), 1)
                v.wait_ge(ve.sem, ve.n)
                ve.tick(v.tensor_mul(out=SM[0:1, 53:54], in0=SM[0:1, 53:54],
                                     in1=SM[0:1, 61:62]), 1)
            v18 = ve.n
            ve_scl8 = bc_cols(53, 2, 6, v18)

        if STAGE >= 8:
            ld_g2 = {}
            ld_up = {}
            po_op = {}
            ve_op = {}
            sw_op = {}
            for ot in range(NOT):
                par = ot % 2
                stg_g2 = STB1 if par == 0 else STB2
                stg_up = XBF[2] if par == 0 else XBF[3]
                stg_out = XBF[4] if par == 0 else XBF[5]

                LDP = ld if par == 0 else ldb
                SWP = sw if par == 0 else swb

                @blk.sync
                def _(sy, ot=ot, stg_g2=stg_g2, stg_up=stg_up, LDP=LDP):
                    if ot < 2:
                        sy.wait_ge(ve.sem, ve_scl8)
                        sy.wait_ge(sw.sem, sw.n)
                        sy.wait_ge(swb.sem, swb.n)
                        sy.wait_ge(sp3.sem, sp3.n)
                        sy.wait_ge(sp3b.sem, sp3b.n)
                    else:
                        sy.wait_ge(ve.sem, ve_op[ot - 2])
                        sy.wait_ge(ve.sem, po_op[ot - 2])
                    LDP.tick(sy.dma_start(out=stg_g2, in_=b_g2.ap()[ot * 128:(ot + 1) * 128, :]), 16)
                    LDP.tick(sy.dma_start(out=stg_up, in_=b_up.ap()[ot * 128:(ot + 1) * 128, :]), 16)
                need_ld = LDP.n

                i1 = INT1 if par == 0 else INT1o
                i2 = INT2 if par == 0 else INT2o

                @blk.vector
                def _(v, ot=ot, stg_g2=stg_g2, stg_up=stg_up, need_ld=need_ld,
                      i1=i1, i2=i2, LDP=LDP):
                    v.wait_ge(LDP.sem, need_ld)
                    if ot >= 2:
                        v.wait_ge(ve.sem, po_op[ot - 2])
                    v.tensor_scalar(out=i1, in0=stg_g2, scalar1=SCL[:, 6:7],
                                    scalar2=None, op0=ALU.mult)
                    ve.tick(v.tensor_scalar(out=i2, in0=stg_up, scalar1=SCL[:, 5:6],
                                            scalar2=None, op0=ALU.mult), 1)
                vv = ve.n

                @blk.vector
                def _(v, ot=ot, stg_out=stg_out, vv=vv, i1=i1, i2=i2, SWP=SWP):
                    v.wait_ge(ve.sem, vv)
                    if ot >= 2:
                        v.wait_ge(SWP.sem, sw_op[ot - 2])
                    ve.tick(v.tensor_tensor(out=stg_out, in0=i1, in1=i2,
                                            op=ALU.mult), 1)
                po_op[ot] = ve.n
                pp = ve.n

                @blk.vector
                def _(v, ot=ot, stg_out=stg_out, pp=pp):
                    v.wait_ge(ve.sem, pp)
                    if ot >= 1:
                        v.wait_ge(ve.sem, ve_op[ot - 1])
                    v.tensor_reduce(out=SM[:, 54:55], in_=stg_out, axis=AX.X,
                                    op=ALU.max, apply_absolute_value=True)
                    ve.tick(v.engine_nop(), 1)
                    v.wait_ge(ve.sem, ve.n)
                    if ot == 0:
                        ve.tick(v.tensor_copy(out=SM[:, 55:56], in_=SM[:, 54:55]), 1)
                    else:
                        ve.tick(v.tensor_tensor(out=SM[:, 55:56], in0=SM[:, 55:56],
                                                in1=SM[:, 54:55], op=ALU.max), 1)
                ve_op[ot] = ve.n

                @blk.scalar
                def _(s, ot=ot, stg_out=stg_out, pp=pp, SWP=SWP):
                    s.wait_ge(ve.sem, pp)
                    SWP.tick(s.dma_start(out=b_opre.ap()[ot * 128:(ot + 1) * 128, :],
                                         in_=stg_out), 16)
                sw_op[ot] = SWP.n

            @blk.tensor
            def _(t):
                t.wait_ge(ve.sem, ve.n)
                t.wait_ge(se.sem, se.n)
                pe.tick(t.transpose(PB[5][0:1, 0:128], SM[:, 55:56], IDE), 1)
            pe_ox = pe.n

            @blk.vector
            def _(v):
                v.wait_ge(pe.sem, pe_ox)
                ve.tick(v.tensor_reduce(out=SM[0:1, 54:55], in_=PB[5][0:1, 0:128],
                                        axis=AX.X, op=ALU.max), 1)
            ve_ox = ve.n

            @blk.gpsimd
            def _(g):
                g.wait_ge(ve.sem, ve_ox)
                gq.tick(g.dma_start(out=b_mx_in.ap()[0:1, 0:1], in_=SM[0:1, 54:55]), 16)
                g.wait_ge(gq.sem, gq.n)
                cc[5].tick(g.collective_compute(
                    "AllReduce", ALU.max, replica_groups=[list(range(NCORE))],
                    ins=[b_mx_in.ap()[0:1, 0:1]], outs=[b_mx_out.ap()[0:1, 0:1]]), 1)
                g.wait_ge(cc[5].sem, cc[5].n)
                gq.tick(g.dma_start(out=SM[0:1, 54:55], in_=b_mx_out.ap()[0:1, 0:1]), 16)
            gq_o = gq.n

            @blk.vector
            def _(v):
                v.wait_ge(gq.sem, gq_o)
                v.tensor_mul(out=SM[:, 60:61], in0=SCL[:, 3:4], in1=SCL[:, 7:8])
                ve.tick(v.engine_nop(), 1)
                v.wait_ge(ve.sem, ve.n)
                v.tensor_mul(out=SM[0:1, 54:55], in0=SM[0:1, 54:55], in1=SM[0:1, 60:61])
                ve.tick(v.engine_nop(), 1)
                v.wait_ge(ve.sem, ve.n)
                v.tensor_scalar(out=SM[0:1, 54:55], in0=SM[0:1, 54:55],
                                scalar1=1.0 / QMAX, scalar2=1e-12, op0=ALU.mult, op1=ALU.max)
                ve.tick(v.engine_nop(), 1)
                v.wait_ge(ve.sem, ve.n)
                ve.tick(v.reciprocal(out=SM[0:1, 53:54], in_=SM[0:1, 54:55]), 1)
                v.wait_ge(ve.sem, ve.n)
                ve.tick(v.tensor_mul(out=SM[0:1, 53:54], in0=SM[0:1, 53:54],
                                     in1=SM[0:1, 60:61]), 1)
            v19 = ve.n

            @blk.tensor
            def _(t):
                t.wait_ge(ve.sem, v19)
                t.wait_ge(se.sem, se.n)
                pe.tick(t.matmul(PB[5][:, 0:2], ONE, SM[0:1, 53:55], start=True, stop=True), 1)
            p25 = pe.n

            @blk.vector
            def _(v):
                v.wait_ge(pe.sem, p25)
                ve.tick(v.tensor_copy(out=SCL[:, 8:10], in_=PB[5][:, 0:2]), 1)
            ve_scl10 = ve.n

        if STAGE >= 9:
            OJH = [gb(1024 * j, 1024) for j in range(NOT)]
            for half in range(2):
                hof = half * 1024
                p5v = {}
                for j in range(NOT):
                    par = j % 2
                    stg = STB1[:, 0:1024] if par == 0 else STB1[:, 1024:2048]
                    ints = (GEN[:, 26624:28672].bitcast(I32) if par == 0
                            else GEN[:, 28672:30720].bitcast(I32))
                    LDP = lp5 if par == 0 else lp5b

                    @blk.sync
                    def _(sy, j=j, hof=hof, stg=stg, half=half, LDP=LDP):
                        if half == 0 and j < 2:
                            sy.wait_ge(ve.sem, ve_scl10)
                            sy.wait_ge(sw.sem, sw.n)
                            sy.wait_ge(swb.sem, swb.n)
                        elif j >= 2:
                            sy.wait_ge(ve.sem, p5v[j - 2])
                        else:
                            sy.wait_ge(ve.sem, ve.n)
                        LDP.tick(sy.dma_start(
                            out=stg, in_=b_opre.ap()[j * 128:(j + 1) * 128,
                                                     hof:hof + 1024]), 16)
                    need_ld = LDP.n

                    @blk.vector
                    def _(v, j=j, stg=stg, ints=ints, need_ld=need_ld, half=half,
                          pe_now=pe.n, LDP=LDP):
                        v.wait_ge(LDP.sem, need_ld)
                        if half > 0:
                            v.wait_ge(pe.sem, pe_now)
                        v.tensor_scalar(out=ints[:, 0:1024], in0=stg, scalar1=SCL[:, 8:9],
                                        scalar2=None, op0=ALU.mult)
                        ve.tick(v.engine_nop(), 1)
                        v.wait_ge(ve.sem, ve.n)
                        ve.tick(v.tensor_scalar(out=OJH[j], in0=ints[:, 0:1024],
                                                scalar1=SCL[:, 9:10], scalar2=None,
                                                op0=ALU.mult), 1)
                    p5v[j] = ve.n
                ve_p5 = ve.n

                dn_pe = []
                res_ve = {}
                for h in range(8):
                    bk0 = (h % 4) * 2
                    for jg in range(3):
                        gidx = h * 3 + jg
                        par = gidx % 2
                        wsl = GEN[:, 24576 + par * 1024:24576 + par * 1024 + 1024]
                        LDP = ldw2 if par == 0 else ldw2b

                        @blk.sync
                        def _(sy, h=h, jg=jg, wsl=wsl, gidx=gidx, dn_pe=tuple(dn_pe),
                              LDP=LDP):
                            if gidx >= 2:
                                sy.wait_ge(pe.sem, dn_pe[gidx - 2])
                            else:
                                sy.wait_ge(pe.sem, pe.n)
                            for jj in range(8):
                                j = jg * 8 + jj
                                LDP.tick(sy.dma_start(
                                    out=wsl[:, jj * 128:(jj + 1) * 128],
                                    in_=b_wd.ap()[h * FF + j * 128:h * FF + (j + 1) * 128, :]), 16)
                        need_ld = LDP.n

                        @blk.tensor
                        def _(t, h=h, jg=jg, bk0=bk0, wsl=wsl, need_ld=need_ld,
                              ve_p5=ve_p5, res_ve=dict(res_ve), LDP=LDP):
                            t.wait_ge(LDP.sem, need_ld)
                            t.wait_ge(ve.sem, ve_p5)
                            if jg == 0 and (h - 4) in res_ve:
                                t.wait_ge(ve.sem, res_ve[h - 4])
                            last = None
                            for jj in range(8):
                                j = jg * 8 + jj
                                for tc2 in range(2):
                                    last = t.matmul(PB[bk0 + tc2][:, 0:512],
                                                    wsl[:, jj * 128:(jj + 1) * 128],
                                                    OJH[j][:, tc2 * 512:(tc2 + 1) * 512],
                                                    start=(j == 0), stop=(j == NOT - 1))
                            pe.tick(last, 1)
                        dn_pe.append(pe.n)
                    need_pe = pe.n

                    @blk.vector
                    def _(v, h=h, bk0=bk0, hof=hof, need_pe=need_pe, st_now=st.n):
                        v.wait_ge(pe.sem, need_pe)
                        if DBG:
                            v.wait_ge(st.sem, st_now)
                        last = None
                        for tc2 in range(2):
                            sl = slice(h * T + hof + tc2 * 512, h * T + hof + (tc2 + 1) * 512)
                            last = v.tensor_add(out=XFFN[:, sl], in0=XFFN[:, sl],
                                                in1=PB[bk0 + tc2][:, 0:512])
                        ve.tick(last, 1)
                    res_ve[h] = ve.n

                for tc2 in range(2):
                    gsl = slice(hof + tc2 * 512, hof + (tc2 + 1) * 512)
                    ve_r2 = ve.n
                    for h in range(8):
                        @blk.scalar
                        def _(s, h=h, gsl=gsl, ve_r2=ve_r2, se_now=se.n):
                            s.wait_ge(ve.sem, ve_r2)
                            if h > 0:
                                s.wait_ge(pe.sem, pe.n)
                                s.wait_ge(se.sem, se_now)
                            sl = slice(h * T + gsl.start, h * T + gsl.stop)
                            se.tick(s.activation(out=STB2[:, 0:512], in_=XFFN[:, sl],
                                                 func=AF.Identity), 1)
                            s.wait_ge(se.sem, se.n)
                            se.tick(s.activation(out=STB2[:, 512:1024], in_=STB2[:, 0:512],
                                                 func=AF.Square), 1)
                        s13 = se.n

                        @blk.tensor
                        def _(t, h=h, s13=s13):
                            t.wait_ge(se.sem, s13)
                            if h == 0:
                                t.wait_ge(ve.sem, ve.n)
                            t.matmul(PB[0][0:1, 0:512], ONECB, STB2[:, 0:512],
                                     start=(h == 0), stop=(h == 7))
                            last = t.matmul(PB[1][0:1, 0:512], ONECB, STB2[:, 512:1024],
                                            start=(h == 0), stop=(h == 7))
                            pe.tick(last, 1)
                    p26 = pe.n

                    @blk.vector
                    def _(v, p26=p26):
                        v.wait_ge(pe.sem, p26)
                        v.tensor_scalar(out=MROW[:, 0:512], in0=PB[0][0:1, 0:512],
                                        scalar1=1.0 / D, scalar2=None, op0=ALU.mult)
                        ve.tick(v.tensor_scalar(out=SQR[:, 0:512], in0=PB[1][0:1, 0:512],
                                                scalar1=1.0 / D, scalar2=None,
                                                op0=ALU.mult), 1)
                        v.wait_ge(ve.sem, ve.n)
                        ve.tick(v.tensor_mul(out=VARR[:, 0:512], in0=MROW[:, 0:512],
                                             in1=MROW[:, 0:512]), 1)
                        v.wait_ge(ve.sem, ve.n)
                        v.tensor_sub(out=VARR[:, 0:512], in0=SQR[:, 0:512],
                                     in1=VARR[:, 0:512])
                        ve.tick(v.engine_nop(), 1)
                        v.wait_ge(ve.sem, ve.n)
                        ve.tick(v.tensor_scalar_add(out=VARR[:, 0:512],
                                                    in0=VARR[:, 0:512], scalar1=1e-5), 1)
                    v20 = ve.n

                    @blk.scalar
                    def _(s, v20=v20):
                        s.wait_ge(ve.sem, v20)
                        se.tick(s.activation(out=VARR[:, 0:512], in_=VARR[:, 0:512],
                                             func=AF.Sqrt), 1)
                    s14 = se.n

                    @blk.vector
                    def _(v, s14=s14):
                        v.wait_ge(se.sem, s14)
                        ve.tick(v.reciprocal(out=RSTD[:, 0:512], in_=VARR[:, 0:512]), 1)
                        v.wait_ge(ve.sem, ve.n)
                        ve.tick(v.tensor_mul(out=MRSTD[:, 0:512], in0=MROW[:, 0:512],
                                             in1=RSTD[:, 0:512]), 1)
                    v21 = ve.n

                    @blk.tensor
                    def _(t, v21=v21):
                        t.wait_ge(ve.sem, v21)
                        t.matmul(PB[2][:, 0:512], ONE, RSTD[:, 0:512],
                                 start=True, stop=True)
                        pe.tick(t.matmul(PB[3][:, 0:512], ONE, MRSTD[:, 0:512],
                                         start=True, stop=True), 1)
                    p27 = pe.n

                    for h in range(8):
                        @blk.vector
                        def _(v, h=h, gsl=gsl, p27=p27):
                            v.wait_ge(pe.sem, p27)
                            sl = slice(h * T + gsl.start, h * T + gsl.stop)
                            v.tensor_mul(out=XFFN[:, sl], in0=XFFN[:, sl], in1=PB[2][:, 0:512])
                            ve.tick(v.engine_nop(), 1)
                            v.wait_ge(ve.sem, ve.n)
                            ve.tick(v.tensor_sub(out=XFFN[:, sl], in0=XFFN[:, sl],
                                                 in1=PB[3][:, 0:512]), 1)
                        need_ve2 = ve.n

                        @blk.sync
                        def _(sy, h=h, gsl=gsl, need_ve2=need_ve2):
                            sy.wait_ge(ve.sem, need_ve2)
                            sl = slice(h * T + gsl.start, h * T + gsl.stop)
                            st.tick(sy.dma_start(out=o_out[h * 128:(h + 1) * 128, gsl],
                                                 in_=XFFN[:, sl]), 16)

            @blk.sync
            def _(sy):
                sy.wait_ge(st.sem, st.n)
        else:
            @blk.sync
            def _(sy):
                sy.wait_ge(ve.sem, ve.n)
                sy.wait_ge(se.sem, se.n)
                for h in range(8):
                    st.tick(sy.dma_start(out=o_out[h * 128:(h + 1) * 128, :],
                                         in_=XFFN[:, h * T:(h + 1) * T]), 16)
                sy.wait_ge(st.sem, st.n)

    return nc


_NC_CACHE = {}


def _idxdq():
    q = np.zeros((128, 64), np.int32)
    p32 = np.arange(128) // 32
    for it in range(8):
        q[:, it] = 4 * it + p32
        q[:, 8 + it] = 32 + 4 * it + p32
    for it in range(24):
        q[:, 16 + it] = 4 * it + p32
        q[:, 40 + it] = 96 + 4 * it + p32
    return q


def _consts():
    IDENT = np.eye(128, dtype=np.float32)
    s = np.arange(128)
    CUMT = (s[:, None] <= s[None, :]).astype(np.float32)
    MASKT = np.where(s[:, None] < s[None, :], 0.0, -1e30).astype(np.float32)
    ONES1 = np.ones((128, 128), np.float32)
    IDX32 = (np.arange(128) // 32).astype(np.int32).reshape(128, 1)
    return IDENT, CUMT, MASKT, ONES1, IDX32


def kernel(L0_attn_x, L0_attn_Wq, L0_attn_bq, L0_attn_Wk, L0_attn_bk,
           L0_attn_Wv, L0_attn_bv, L0_attn_Wa, L0_attn_ba, L0_attn_Wb,
           L0_attn_bb, L0_attn_Wg, L0_attn_bg, L0_attn_Wo, L0_attn_bo,
           L0_ffn_w_gate, L0_ffn_w_up, L0_ffn_w_down):
    x = np.asarray(L0_attn_x, np.float32)
    f = lambda a: np.ascontiguousarray(np.asarray(a, np.float32))
    Wq, Wk, Wv, Wg = f(L0_attn_Wq), f(L0_attn_Wk), f(L0_attn_Wv), f(L0_attn_Wg)
    Wa, Wb, Wo = f(L0_attn_Wa), f(L0_attn_Wb), f(L0_attn_Wo)
    bq, bk, bv, bg = f(L0_attn_bq), f(L0_attn_bk), f(L0_attn_bv), f(L0_attn_bg)
    ba, bb, bo = f(L0_attn_ba), f(L0_attn_bb), f(L0_attn_bo)
    wgate, wup, wdown = f(L0_ffn_w_gate), f(L0_ffn_w_up), f(L0_ffn_w_down)

    IDENT, CUMT, MASKT, ONES1, IDX32 = _consts()
    shared = {
        "wqT": np.ascontiguousarray(Wq.T), "wkT": np.ascontiguousarray(Wk.T),
        "wvT": np.ascontiguousarray(Wv.T), "wgT": np.ascontiguousarray(Wg.T),
        "wabT": np.ascontiguousarray(np.concatenate([Wa, Wb], 0).T),
        "woT": np.ascontiguousarray(Wo.T),
        "bqc": bq.reshape(C, 1), "bkc": bk.reshape(C, 1),
        "bvc": bv.reshape(C, 1), "bgc": bg.reshape(C, 1),
        "babb": np.ascontiguousarray(
            np.concatenate([np.full((C, 1), ba[0], np.float32),
                            np.full((C, 1), bb[0], np.float32)], 1)),
        "boc": np.ascontiguousarray(bo.reshape(8, 128).T),
        "IDENT": IDENT, "CUMT": CUMT, "MASKT": MASKT, "ONES1": ONES1,
        "IDX32": IDX32, "IDXDQ": _idxdq(),
    }

    if "nc" not in _NC_CACHE:
        _NC_CACHE["nc"] = build_nc()
    nc = _NC_CACHE["nc"]

    in_maps = []
    for r in range(NCORE):
        seq, t0 = r // 4, (r % 4) * T
        grp = r % 4
        base = (r // 4) * 4
        pdidx = np.zeros((128, 6), np.int32)
        for k in range(3):
            rowbase = (base + k) * 256
            pdidx[:, 2 * k] = rowbase + np.arange(128)
            pdidx[:, 2 * k + 1] = rowbase + 128 + np.arange(128)
        msel = np.zeros((128, 4), np.float32)
        msel[:, grp] = 1.0
        m = dict(shared)
        m["xT"] = np.ascontiguousarray(x[seq, t0:t0 + T, :].T)
        m["up_o"] = np.ascontiguousarray(wup[r * 384:(r + 1) * 384, :])
        m["up_t"] = np.ascontiguousarray(wup[r * 384:(r + 1) * 384, :].T)
        m["gt_o"] = np.ascontiguousarray(wgate[r * 384:(r + 1) * 384, :])
        m["gt_t"] = np.ascontiguousarray(wgate[r * 384:(r + 1) * 384, :].T)
        m["dn_o"] = np.ascontiguousarray(wdown[r * 128:(r + 1) * 128, :])
        m["dn_t"] = np.ascontiguousarray(wdown[r * 128:(r + 1) * 128, :].T)
        m["PDIDX"] = pdidx
        m["MSEL"] = msel
        in_maps.append(m)

    res = run_bass_kernel_spmd(nc, in_maps, core_ids=list(range(NCORE)))
    out = np.empty((2, 8192, 1024), np.float32)
    for r in range(NCORE):
        seq, t0 = r // 4, (r % 4) * T
        out[seq, t0:t0 + T, :] = np.asarray(res.results[r]["o_out"]).T
    if DBG:
        kernel.last_results = res.results
    return out
